# version 16
# baseline (speedup 1.0000x reference)
import math
import numpy as np
from contextlib import ExitStack
import concourse.bass as bass
import concourse.mybir as mybir
from concourse.bass_utils import run_bass_kernel_spmd

F32 = mybir.dt.float32
BF16 = mybir.dt.bfloat16
I32 = mybir.dt.int32
AF = mybir.ActivationFunctionType
ALU = mybir.AluOpType
AX = mybir.AxisListType

NCORE = 8
T = 2048
D = 1024
DI = 2048
NS = 16
DTR = 64
HF = 3072
NE = 8
FE = 2048
H = 16
HD = 64
LN_EPS = 1e-5
QSCALE = float(HD) ** -0.5
DEQ = float(np.float32(1.0) / np.float32(127.0))


class Sched:
    ENGS = ("tensor", "vector", "scalar", "gpsimd", "sync")

    def __init__(self, nc, stack):
        self.nc = nc
        self.stack = stack
        self.plan = {e: [] for e in self.ENGS}
        self.sems = {}
        self.counts = {}
        self.wm = {e: {} for e in self.ENGS}
        self.bufs = {}
        for e in self.ENGS:
            self._sem(e)

    def _sem(self, name):
        if name not in self.sems:
            self.sems[name] = self.stack.enter_context(self.nc.semaphore("s_" + name))
            self.counts[name] = 0
        return self.sems[name]

    def op(self, eng, fn, reads=(), writes=(), queue=None):
        semkey = queue or eng
        self._sem(semkey)
        inc = 16 if queue else 1
        waits = {}
        pe_auto_nop = getattr(self, "pe_auto_nop", True)

        def add(tok):
            if tok is None:
                return
            s, v = tok
            if self.wm[eng].get(s, 0) < v and waits.get(s, 0) < v:
                waits[s] = v

        for k in reads:
            st = self.bufs.get(k)
            if st:
                add(st["w"])
        for k in writes:
            st = self.bufs.get(k)
            if st:
                add(st["w"])
                for r in st["r"]:
                    add(r)

        for s in list(waits):
            if s not in self.ENGS:
                if self.counts[s] > waits[s]:
                    waits[s] = self.counts[s]
                if self.wm[eng].get(s, 0) >= waits[s]:
                    del waits[s]

        self.counts[semkey] += inc
        tok = (semkey, self.counts[semkey])
        wlist = sorted(waits.items())
        sems = self.sems

        pe_nop = (eng == "tensor" and queue is None and pe_auto_nop)

        def thunk(engine, wlist=wlist, fn=fn, semkey=semkey, inc=inc, pe_nop=pe_nop):
            for s, v in wlist:
                engine.wait_ge(sems[s], v)
            if pe_nop:
                engine.nop()
            fn(engine).then_inc(sems[semkey], inc)

        self.plan[eng].append(thunk)
        for s, v in waits.items():
            self.wm[eng][s] = max(self.wm[eng].get(s, 0), v)
        for k in writes:
            self.bufs[k] = {"w": tok, "r": []}
        for k in reads:
            self.bufs.setdefault(k, {"w": None, "r": []})["r"].append(tok)
        return tok

    def barrier(self):
        tgt = dict(self.counts)
        sems = self.sems
        for e in self.ENGS:
            wl = [(s, v) for s, v in sorted(tgt.items())
                  if v > 0 and self.wm[e].get(s, 0) < v]

            def thunk(engine, wl=wl):
                for s, v in wl:
                    engine.wait_ge(sems[s], v)
            self.plan[e].append(thunk)
            for s, v in tgt.items():
                self.wm[e][s] = max(self.wm[e].get(s, 0), v)
        self.bufs = {}

    def flush(self):
        def mk(plan):
            def closure(engine):
                for th in plan:
                    th(engine)
            return closure
        if any(self.plan[e] for e in self.ENGS):
            with self.nc.Block() as block:
                for e in self.ENGS:
                    if self.plan[e]:
                        getattr(block, e)(mk(self.plan[e]))
        self.plan = {e: [] for e in self.ENGS}

    def finalize(self):
        tgt = dict(self.counts)
        sems = self.sems

        def final_thunk(engine):
            for s, v in sorted(tgt.items()):
                if v > 0:
                    engine.wait_ge(sems[s], v)
        self.plan["sync"].append(final_thunk)
        self.flush()


def emit_consts(nc, S, stack):
    E = stack.enter_context
    ident = E(nc.sbuf_tensor("c_ident", [128, 128], BF16))
    ones128 = E(nc.sbuf_tensor("c_ones128", [1, 128], F32))

    S.op("gpsimd", lambda e: e.memset(ident[:], 1.0), writes=["ident"])
    S.op("gpsimd", lambda e: e.affine_select(
        out=ident[:], in_=ident[:], pattern=[[-1, 128]],
        compare_op=ALU.is_equal, fill=0.0, base=0, channel_multiplier=1),
        reads=["ident"], writes=["ident"])
    S.op("gpsimd", lambda e: e.memset(ones128[0:1, :], 1.0), writes=["ones128"])
    ceps = E(nc.sbuf_tensor("c_eps", [128, 1], F32))
    S.op("gpsimd", lambda e: e.memset(ceps[:], LN_EPS), writes=["ceps"])
    nc.const_aps.register(F32, LN_EPS, ceps[:, 0:1])
    S.barrier()
    return dict(ident=ident, ones128=ones128)


def emit_sel(nc, S, E):
    sel = E(nc.sbuf_tensor("c_sel", [NS, NS * 128], F32))
    S.op("gpsimd", lambda e: e.memset(sel[:], 1.0), writes=["sel"])
    S.op("gpsimd", lambda e: e.affine_select(
        out=sel[:], in_=sel[:], pattern=[[1, NS * 128]],
        compare_op=ALU.is_ge, fill=0.0, base=0, channel_multiplier=-128),
        reads=["sel"], writes=["sel"])
    S.op("gpsimd", lambda e: e.affine_select(
        out=sel[:], in_=sel[:], pattern=[[-1, NS * 128]],
        compare_op=ALU.is_ge, fill=0.0, base=127, channel_multiplier=128),
        reads=["sel"], writes=["sel"])
    return sel


def emit_ln(nc, S, key, tile, scr, tag, sq_key=None):
    sq, cols = scr["sq"], scr["cols"]
    ksq, kc = (sq_key or tag + "sq"), tag + "c"
    S.op("vector", lambda e: e.tensor_scalar(
        out=sq[:], in0=tile, scalar1=-1.0 / D, scalar2=None,
        op0=ALU.mult, op1=ALU.add, accum_out=cols[:, 1:2]),
        reads=[key], writes=[ksq, kc + "1"])
    S.op("scalar", lambda e: e.activation(
        out=sq[:], in_=tile, func=AF.Square, bias=cols[:, 1:2]),
        reads=[key, kc + "1"], writes=[ksq])
    S.op("vector", lambda e: e.tensor_scalar(
        out=sq[:], in0=sq[:], scalar1=1.0 / D, scalar2=None,
        op0=ALU.mult, op1=ALU.add, accum_out=cols[:, 2:3]),
        reads=[ksq], writes=[ksq, kc + "2"])
    S.op("scalar", lambda e: e.activation(
        out=cols[:, 2:3], in_=cols[:, 2:3], func=AF.Ln, bias=LN_EPS),
        reads=[kc + "2"], writes=[kc + "2"])
    S.op("scalar", lambda e: e.activation(
        out=cols[:, 2:3], in_=cols[:, 2:3], func=AF.Exp, scale=-0.5),
        reads=[kc + "2"], writes=[kc + "2"])
    S.op("vector", lambda e: e.tensor_tensor(
        out=cols[:, 3:4], in0=cols[:, 1:2], in1=cols[:, 2:3], op=ALU.mult),
        reads=[kc + "1", kc + "2"], writes=[kc + "3"])
    S.op("vector", lambda e: e.tensor_scalar(
        out=tile, in0=tile, scalar1=cols[:, 2:3], scalar2=cols[:, 3:4],
        op0=ALU.mult, op1=ALU.add),
        reads=[key, kc + "2", kc + "3"], writes=[key])


def decl_l0attn(nc, r1_kind):
    io = {}
    io["x"] = nc.dram_tensor("x", [T, D], F32, kind="ExternalInput").ap()
    io["w_in"] = nc.dram_tensor("w_in", [D, 2 * DI], F32, kind="ExternalInput").ap()
    io["w_xp"] = nc.dram_tensor("w_xp", [DI, 128], F32, kind="ExternalInput").ap()
    io["w_dt"] = nc.dram_tensor("w_dt", [DTR, DI], F32, kind="ExternalInput").ap()
    io["dtb_col"] = nc.dram_tensor("dtb_col", [128, 16], F32, kind="ExternalInput").ap()
    io["convw_col"] = nc.dram_tensor("convw_col", [128, 64], F32, kind="ExternalInput").ap()
    io["convb_col"] = nc.dram_tensor("convb_col", [128, 16], F32, kind="ExternalInput").ap()
    io["alog_col"] = nc.dram_tensor("alog_col", [128, 256], F32, kind="ExternalInput").ap()
    io["d_col"] = nc.dram_tensor("d_col", [128, 16], F32, kind="ExternalInput").ap()
    io["w_po"] = nc.dram_tensor("w_po", [DI, D], F32, kind="ExternalInput").ap()
    io["r1"] = nc.dram_tensor("r1", [T, D], F32, kind=r1_kind).ap()
    return io


def emit_l0attn(nc, S, C, io):
    x_in = io["x"]; w_in = io["w_in"]; w_xp = io["w_xp"]; w_dt = io["w_dt"]
    dtb_col = io["dtb_col"]; convw_col = io["convw_col"]; convb_col = io["convb_col"]
    alog_col = io["alog_col"]; d_col = io["d_col"]; w_po = io["w_po"]
    r1_out = io["r1"]

    s_xi = nc.dram_tensor("s_xi", [DI, T], F32, kind="Internal").ap()
    s_gres = nc.dram_tensor("s_gres", [DI, T], F32, kind="Internal").ap()
    s_dt = nc.dram_tensor("s_dt", [DI, T], F32, kind="Internal").ap()
    s_prod = nc.dram_tensor("s_prod", [DI, T], BF16, kind="Internal").ap()
    s_bt = nc.dram_tensor("s_bt", [NS, T], F32, kind="Internal").ap()
    s_ct = nc.dram_tensor("s_ct", [NS, T], F32, kind="Internal").ap()

    TC = 512
    NCH = T // TC

    with ExitStack() as stack:
        E = stack.enter_context
        C = dict(C, sel=emit_sel(nc, S, E))

        a_sb = E(nc.sbuf_tensor("a_sb", [128, 256], F32))
        dcol_sb = E(nc.sbuf_tensor("dcol_sb", [128, 16], F32))
        dtb_sb = E(nc.sbuf_tensor("dtb_sb", [128, 16], F32))
        cwcol_sb = E(nc.sbuf_tensor("cwcol_sb", [128, 64], F32))
        cbcol_sb = E(nc.sbuf_tensor("cbcol_sb", [128, 16], F32))
        hist = E(nc.sbuf_tensor("hist", [128, 3 * 16], F32))

        S.op("sync", lambda e: e.dma_start(out=a_sb[:], in_=alog_col),
             writes=["A"], queue="qA")
        S.op("scalar", lambda e: e.activation(out=a_sb[:], in_=a_sb[:], func=AF.Exp),
             reads=["A"], writes=["A"])
        S.op("vector", lambda e: e.tensor_scalar(
            out=a_sb[:], in0=a_sb[:], scalar1=-1.0, scalar2=None, op0=ALU.mult),
            reads=["A"], writes=["A"])
        S.op("sync", lambda e: e.dma_start(out=dcol_sb[:], in_=d_col),
             writes=["Dcol"], queue="qD")
        S.op("sync", lambda e: e.dma_start(out=dtb_sb[:], in_=dtb_col),
             writes=["dtb"], queue="qdtb")
        S.op("sync", lambda e: e.dma_start(out=cwcol_sb[:], in_=convw_col),
             writes=["cw"], queue="qcw")
        S.op("sync", lambda e: e.dma_start(out=cbcol_sb[:], in_=convb_col),
             writes=["cb"], queue="qcb")
        S.op("vector", lambda e: e.memset(hist[:], 0.0), writes=["hist"])

        with ExitStack() as ph:
            ph.enter_context(nc.named_scope("l0a_inproj"))
            Ep = ph.enter_context
            w_in_sb = [Ep(nc.sbuf_tensor(f"w_in_sb{d}", [128, 2 * DI], BF16))
                       for d in range(8)]
            w_xp_sb = Ep(nc.sbuf_tensor("w_xp_sb", [128, 16 * 128], BF16))
            w_dt_sb = Ep(nc.sbuf_tensor("w_dt_sb", [128, DI], BF16))
            x_tok = [Ep(nc.sbuf_tensor(f"x_tok{j}", [128, D], F32)) for j in range(4)]
            xbf = [Ep(nc.sbuf_tensor(f"xbf{j}", [128, D], BF16)) for j in range(4)]
            xT = [Ep(nc.sbuf_tensor(f"xT{d}", [128, TC], BF16)) for d in range(8)]
            NB = 4
            convbuf = [Ep(nc.sbuf_tensor(f"convbuf{i}", [128, TC + 3], F32)) for i in range(NB)]
            convacc = [Ep(nc.sbuf_tensor(f"convacc{i}", [128, TC], F32)) for i in range(NB)]
            xi_t = [Ep(nc.sbuf_tensor(f"xi_t{i}", [128, TC], F32)) for i in range(NB)]
            xibk_t = [Ep(nc.sbuf_tensor(f"xibk_t{k}", [128, TC], BF16)) for k in range(16)]
            gres_t = [Ep(nc.sbuf_tensor(f"gres_t{i}", [128, TC], F32)) for i in range(NB)]
            xdbl = Ep(nc.sbuf_tensor("xdbl", [128, TC], F32))
            deltab = Ep(nc.sbuf_tensor("deltab", [128, TC], BF16))
            dt_t = [Ep(nc.sbuf_tensor(f"dt_t{i}", [128, TC], F32)) for i in range(NB)]
            ps_tr = Ep(nc.psum_tensor("ps_tr", [128, TC], BF16))
            ps_m = [Ep(nc.psum_tensor(f"ps_m{i}", [128, 512], F32)) for i in range(4)]
            ps_xp = Ep(nc.psum_tensor("ps_xp", [128, 512], F32))
            ps_dt = [Ep(nc.psum_tensor(f"ps_dt{i}", [128, 512], F32)) for i in range(2)]

            for d in range(8):
                S.op("gpsimd", lambda e, d=d: e.dma_start(
                    out=w_in_sb[d][:], in_=w_in[d * 128:(d + 1) * 128, :]),
                    writes=[f"w_in{d}"], queue="qwin")
            S.op("gpsimd", lambda e: e.dma_start(
                out=w_xp_sb.ap().rearrange("p (b m) -> p b m", m=128),
                in_=w_xp.rearrange("(b p) m -> p b m", p=128)),
                writes=["w_xp"], queue="qwxp")
            S.op("gpsimd", lambda e: e.dma_start(out=w_dt_sb[64:128, :], in_=w_dt),
                 writes=["w_dt"], queue="qwdt")

            for c in range(NCH):
                t0 = c * TC
                for j in range(4):
                    S.op("sync", lambda e, j=j, t0=t0: e.dma_start(
                        out=x_tok[j][:], in_=x_in[t0 + j * 128: t0 + (j + 1) * 128, :]),
                        writes=[f"x_tok{j}"], queue=f"qx{j}")
                    S.op("scalar", lambda e, j=j: e.copy(out=xbf[j][:], in_=x_tok[j][:]),
                         reads=[f"x_tok{j}"], writes=[f"xbf{j}"])
                for d in range(8):
                    def tr(e, d=d):
                        for j in range(4):
                            last = e.transpose(
                                ps_tr[:, j * 128:(j + 1) * 128],
                                xbf[j][:, d * 128:(d + 1) * 128], C["ident"][:])
                        return last
                    S.op("tensor", tr,
                         reads=[f"xbf{j}" for j in range(4)] + ["ident"],
                         writes=["ps_tr"])
                    S.op("scalar", lambda e, d=d: e.copy(out=xT[d][:], in_=ps_tr[:]),
                         reads=["ps_tr"], writes=[f"xT{d}"])

                for fo in range(32):
                    pm = ps_m[fo % 4]
                    pk = f"ps_m{fo % 4}"

                    def mm(e, fo=fo, pm=pm):
                        for d in range(8):
                            last = e.matmul(
                                pm[:], w_in_sb[d][:, fo * 128:(fo + 1) * 128],
                                xT[d][:], start=(d == 0), stop=(d == 7))
                        return last
                    S.op("tensor", mm,
                         reads=[f"w_in{d}" for d in range(8)] + [f"xT{d}" for d in range(8)],
                         writes=[pk])
                    if fo < 16:
                        dblk = fo
                        i = dblk % 4
                        S.op("scalar", lambda e, pm=pm, i=i: e.copy(
                            out=convbuf[i][:, 3:TC + 3], in_=pm[:]),
                            reads=[pk], writes=[f"convbuf{i}"])
                        S.op("vector", lambda e, i=i, dblk=dblk: e.tensor_copy(
                            convbuf[i][:, 0:3], hist[:, dblk * 3:(dblk + 1) * 3]),
                            reads=["hist"], writes=[f"convbuf{i}"])
                        S.op("vector", lambda e, i=i, dblk=dblk: e.tensor_scalar(
                            out=convacc[i][:], in0=convbuf[i][:, 0:TC],
                            scalar1=cwcol_sb[:, dblk * 4:dblk * 4 + 1], scalar2=None,
                            op0=ALU.mult),
                            reads=[f"convbuf{i}", "cw"], writes=[f"convacc{i}"])
                        for k in (1, 2, 3):
                            S.op("vector", lambda e, i=i, dblk=dblk, k=k: e.scalar_tensor_tensor(
                                out=convacc[i][:], in0=convbuf[i][:, k:TC + k],
                                scalar=cwcol_sb[:, dblk * 4 + k:dblk * 4 + k + 1],
                                in1=convacc[i][:], op0=ALU.mult, op1=ALU.add),
                                reads=[f"convbuf{i}", "cw", f"convacc{i}"],
                                writes=[f"convacc{i}"])
                        S.op("vector", lambda e, i=i, dblk=dblk: e.tensor_copy(
                            hist[:, dblk * 3:(dblk + 1) * 3], convbuf[i][:, TC:TC + 3]),
                            reads=[f"convbuf{i}"], writes=["hist"])
                        S.op("scalar", lambda e, i=i, dblk=dblk: e.activation(
                            out=xi_t[i][:], in_=convacc[i][:], func=AF.Silu,
                            bias=cbcol_sb[:, dblk:dblk + 1]),
                            reads=[f"convacc{i}", "cb"], writes=[f"xi_t{i}"])
                        S.op("gpsimd", lambda e, i=i, dblk=dblk, t0=t0: e.dma_start(
                            out=s_xi[dblk * 128:(dblk + 1) * 128, t0:t0 + TC],
                            in_=xi_t[i][:]),
                            reads=[f"xi_t{i}"], queue="dst")
                        S.op("scalar", lambda e, i=i, dblk=dblk: e.copy(
                            out=xibk_t[dblk][:], in_=xi_t[i][:]),
                            reads=[f"xi_t{i}"], writes=[f"xib{dblk}"])
                    else:
                        gblk = fo - 16
                        i = gblk % 4
                        S.op("scalar", lambda e, pm=pm, i=i: e.activation(
                            out=gres_t[i][:], in_=pm[:], func=AF.Silu),
                            reads=[pk], writes=[f"gres_t{i}"])
                        S.op("gpsimd", lambda e, i=i, gblk=gblk, t0=t0: e.dma_start(
                            out=s_gres[gblk * 128:(gblk + 1) * 128, t0:t0 + TC],
                            in_=gres_t[i][:]),
                            reads=[f"gres_t{i}"], queue="dst")

                def mmxp(e):
                    for dblk in range(16):
                        last = e.matmul(
                            ps_xp[:], w_xp_sb[:, dblk * 128:(dblk + 1) * 128],
                            xibk_t[dblk][:], start=(dblk == 0), stop=(dblk == 15))
                    return last
                S.op("tensor", mmxp,
                     reads=[f"xib{k}" for k in range(16)] + ["w_xp"],
                     writes=["ps_xp"])

                S.op("scalar", lambda e: e.copy(out=xdbl[:], in_=ps_xp[:]),
                     reads=["ps_xp"], writes=["xdbl"])
                S.op("sync", lambda e, t0=t0: e.dma_start(
                    out=s_bt[:, t0:t0 + TC], in_=xdbl[0:16, :]),
                    reads=["xdbl"], writes=["BT"], queue="qbt")
                S.op("sync", lambda e, t0=t0: e.dma_start(
                    out=s_ct[:, t0:t0 + TC], in_=xdbl[32:48, :]),
                    reads=["xdbl"], writes=["CT"], queue="qct")
                S.op("scalar", lambda e: e.copy(
                    out=deltab[64:128, :], in_=xdbl[64:128, :]),
                    reads=["xdbl"], writes=["deltab"])
                for dblk in range(16):
                    i = dblk % 4
                    pd = dblk % 2
                    S.op("tensor", lambda e, dblk=dblk, pd=pd: e.matmul(
                        ps_dt[pd][:], w_dt_sb[64:128, dblk * 128:(dblk + 1) * 128],
                        deltab[64:128, :], start=True, stop=True),
                        reads=["deltab", "w_dt"], writes=[f"ps_dt{pd}"])
                    S.op("scalar", lambda e, dblk=dblk, i=i, pd=pd: e.activation(
                        out=dt_t[i][:], in_=ps_dt[pd][:], func=AF.Exp,
                        bias=dtb_sb[:, dblk:dblk + 1]),
                        reads=[f"ps_dt{pd}", "dtb"], writes=[f"dt_t{i}"])
                    S.op("scalar", lambda e, i=i: e.activation(
                        out=dt_t[i][:], in_=dt_t[i][:], func=AF.Ln, bias=1.0),
                        reads=[f"dt_t{i}"], writes=[f"dt_t{i}"])
                    S.op("gpsimd", lambda e, i=i, dblk=dblk, t0=t0: e.dma_start(
                        out=s_dt[dblk * 128:(dblk + 1) * 128, t0:t0 + TC],
                        in_=dt_t[i][:]),
                        reads=[f"dt_t{i}"], queue="dst")

            S.barrier()
            S.flush()

        with ExitStack() as ph:
            ph.enter_context(nc.named_scope("l0a_scan"))
            Ep = ph.enter_context
            dtd = [Ep(nc.sbuf_tensor(f"dtd{i}", [128, T], F32)) for i in range(2)]
            xid = [Ep(nc.sbuf_tensor(f"xid{i}", [128, T], F32)) for i in range(2)]
            grd = [Ep(nc.sbuf_tensor(f"grd{i}", [128, T], F32)) for i in range(2)]
            dud = [Ep(nc.sbuf_tensor(f"dud{i}", [128, T], F32)) for i in range(2)]
            bb = [Ep(nc.sbuf_tensor(f"bb{i}", [128, T], F32)) for i in range(2)]
            cbx = [Ep(nc.sbuf_tensor(f"cbx{i}", [128, T], F32)) for i in range(2)]
            at = [Ep(nc.sbuf_tensor(f"at{i}", [128, T], F32)) for i in range(2)]
            ut = Ep(nc.sbuf_tensor("ut", [128, T], F32))
            st_ = Ep(nc.sbuf_tensor("st_", [128, T], F32))
            zt = [Ep(nc.sbuf_tensor(f"zt{i}", [128, T], F32)) for i in range(2)]
            yt = [Ep(nc.sbuf_tensor(f"yt{i}", [128, T], F32)) for i in range(2)]
            prodb = [Ep(nc.sbuf_tensor(f"prodb{i}", [128, T], BF16)) for i in range(2)]

            def bcast(n, src_dram, src_key, dst, dst_key, qn):
                S.op("gpsimd", lambda e, n=n: e.dma_start(
                    out=dst[:], in_=bass.AP(src_dram.tensor, n * T + src_dram.offset,
                                            [[0, 128], [1, T]])),
                    reads=[src_key], writes=[dst_key], queue=qn)

            for dpair in range(8):
                for p in range(2):
                    dblk = 2 * dpair + p
                    S.op("sync", lambda e, dblk=dblk, p=p: e.dma_start(
                        out=dtd[p][:], in_=s_dt[dblk * 128:(dblk + 1) * 128, :]),
                        writes=[f"dtd{p}"], queue=f"qdtd{p}")
                    S.op("sync", lambda e, dblk=dblk, p=p: e.dma_start(
                        out=xid[p][:], in_=s_xi[dblk * 128:(dblk + 1) * 128, :]),
                        writes=[f"xid{p}"], queue=f"qxid{p}")
                    S.op("sync", lambda e, dblk=dblk, p=p: e.dma_start(
                        out=grd[p][:], in_=s_gres[dblk * 128:(dblk + 1) * 128, :]),
                        writes=[f"grd{p}"], queue=f"qgrd{p}")
                    S.op("vector", lambda e, p=p: e.tensor_tensor(
                        out=dud[p][:], in0=dtd[p][:], in1=xid[p][:], op=ALU.mult),
                        reads=[f"dtd{p}", f"xid{p}"], writes=[f"dud{p}"])
                for n in range(NS):
                    i = n % 2
                    bcast(n, s_bt, "BT", bb[i], f"bb{i}", f"qbb{i}")
                    bcast(n, s_ct, "CT", cbx[i], f"cbx{i}", f"qcb{i}")
                    for p in range(2):
                        dblk = 2 * dpair + p
                        S.op("scalar", lambda e, p=p, dblk=dblk, n=n: e.activation(
                            out=at[p][:], in_=dtd[p][:], func=AF.Exp,
                            scale=a_sb[:, dblk * 16 + n:dblk * 16 + n + 1]),
                            reads=[f"dtd{p}", "A"], writes=[f"at{p}"])
                        S.op("vector", lambda e, i=i, p=p: e.tensor_tensor(
                            out=ut[:], in0=dud[p][:], in1=bb[i][:], op=ALU.mult),
                            reads=[f"dud{p}", f"bb{i}"], writes=["ut"])
                        S.op("vector", lambda e, p=p: e.tensor_tensor_scan(
                            out=st_[:], data0=at[p][:], data1=ut[:], initial=0.0,
                            op0=ALU.mult, op1=ALU.add),
                            reads=[f"at{p}", "ut"], writes=["st"])
                        S.op("vector", lambda e, i=i, p=p: e.tensor_tensor(
                            out=zt[p][:], in0=st_[:], in1=cbx[i][:], op=ALU.mult),
                            reads=["st", f"cbx{i}"], writes=[f"zt{p}"])
                        if n == 0:
                            S.op("gpsimd", lambda e, p=p: e.dma_start(
                                out=yt[p][:], in_=zt[p][:]),
                                reads=[f"zt{p}"], writes=[f"yt{p}"], queue=f"qy{p}")
                        else:
                            S.op("gpsimd", lambda e, p=p: e.dma_start(
                                out=yt[p][:], in_=zt[p][:], accum_op=ALU.add),
                                reads=[f"zt{p}"], writes=[f"yt{p}"], queue=f"qy{p}")
                for p in range(2):
                    dblk = 2 * dpair + p
                    S.op("vector", lambda e, p=p, dblk=dblk: e.scalar_tensor_tensor(
                        out=yt[p][:], in0=xid[p][:], scalar=dcol_sb[:, dblk:dblk + 1],
                        in1=yt[p][:], op0=ALU.mult, op1=ALU.add),
                        reads=[f"xid{p}", "Dcol", f"yt{p}"], writes=[f"yt{p}"])
                    S.op("vector", lambda e, p=p: e.tensor_tensor(
                        out=prodb[p][:], in0=yt[p][:], in1=grd[p][:], op=ALU.mult),
                        reads=[f"yt{p}", f"grd{p}"], writes=[f"prodb{p}"])
                    S.op("gpsimd", lambda e, dblk=dblk, p=p: e.dma_start(
                        out=s_prod[dblk * 128:(dblk + 1) * 128, :], in_=prodb[p][:]),
                        reads=[f"prodb{p}"], queue="dst")

            S.barrier()
            S.flush()

        with ExitStack() as ph:
            ph.enter_context(nc.named_scope("l0a_outproj"))
            Ep = ph.enter_context
            w_po_sb = [Ep(nc.sbuf_tensor(f"w_po_sb{b}", [128, D], BF16)) for b in range(16)]
            prod_sb = [Ep(nc.sbuf_tensor(f"prod_sb{i}", [128, 16 * 128], BF16)) for i in range(2)]
            x_res = [Ep(nc.sbuf_tensor(f"x_res{i}", [128, D], F32)) for i in range(2)]
            r1_t = [Ep(nc.sbuf_tensor(f"r1_t{i}", [128, D], F32)) for i in range(2)]
            ln_sq = Ep(nc.sbuf_tensor("ln_sq", [128, D], F32))
            ln_cols = Ep(nc.sbuf_tensor("ln_cols", [128, 4], F32))
            ps_o = [Ep(nc.psum_tensor(f"ps_o{i}", [128, 512], F32)) for i in range(2)]

            for b in range(16):
                S.op("gpsimd", lambda e, b=b: e.dma_start(
                    out=w_po_sb[b][:], in_=w_po[b * 128:(b + 1) * 128, :]),
                    writes=[f"w_po{b}"], queue="qwpo")

            for tt in range(16):
                i = tt % 2
                S.op("sync", lambda e, tt=tt, i=i: e.dma_start(
                    out=prod_sb[i][:],
                    in_=s_prod.rearrange("(b p) t -> p b t", p=128)[:, :, tt * 128:(tt + 1) * 128]),
                    writes=[f"prod_sb{i}"], queue=f"qprodl{i}")
                S.op("sync", lambda e, tt=tt, i=i: e.dma_start(
                    out=x_res[i][:], in_=x_in[tt * 128:(tt + 1) * 128, :]),
                    writes=[f"x_res{i}"], queue=f"qxr{i}")
                for ch in range(2):
                    def mm(e, i=i, ch=ch):
                        for b in range(16):
                            last = e.matmul(
                                ps_o[ch][:], prod_sb[i][:, b * 128:(b + 1) * 128],
                                w_po_sb[b][:, ch * 512:(ch + 1) * 512],
                                start=(b == 0), stop=(b == 15))
                        return last
                    S.op("tensor", mm,
                         reads=[f"prod_sb{i}"] + [f"w_po{b}" for b in range(16)],
                         writes=[f"ps_o{ch}"])
                    S.op("vector", lambda e, i=i, ch=ch: e.tensor_add(
                        r1_t[i][:, ch * 512:(ch + 1) * 512], ps_o[ch][:],
                        x_res[i][:, ch * 512:(ch + 1) * 512]),
                        reads=[f"ps_o{ch}", f"x_res{i}"], writes=[f"r1_t{i}"])
                emit_ln(nc, S, f"r1_t{i}", r1_t[i][:],
                        dict(sq=ln_sq, cols=ln_cols), f"ln{tt}")
                S.op("sync", lambda e, tt=tt, i=i: e.dma_start(
                    out=r1_out[tt * 128:(tt + 1) * 128, :], in_=r1_t[i][:]),
                    reads=[f"r1_t{i}"], queue="dso")

            S.barrier()
            S.flush()


def build_l0attn():
    nc = bass.Bass("TRN2", target_bir_lowering=False, debug=False)
    io = decl_l0attn(nc, "ExternalOutput")
    with ExitStack() as stack:
        S = Sched(nc, stack)
        C = emit_consts(nc, S, stack)
        emit_l0attn(nc, S, C, io)
        S.finalize()
    return nc


def prep_l0attn(ins, core):
    conv_w = ins["L0_attn_conv_w"].reshape(16, 128, 4).transpose(1, 0, 2).reshape(128, 64)
    conv_b = ins["L0_attn_conv_b"].reshape(16, 128).T
    dtb = ins["L0_attn_dt_proj_b"].reshape(16, 128).T
    alog = ins["L0_attn_A_log"].reshape(16, 128, 16).transpose(1, 0, 2).reshape(128, 256)
    dcol = ins["L0_attn_Dparam"].reshape(16, 128).T
    xp = ins["L0_attn_x_proj_w"]
    xp_pad = np.zeros((DI, 128), np.float32)
    xp_pad[:, 0:16] = xp[:, 64:80]
    xp_pad[:, 32:48] = xp[:, 80:96]
    xp_pad[:, 64:128] = xp[:, 0:64]
    return {
        "x": np.ascontiguousarray(ins["L0_attn_x"][core]),
        "w_in": ins["L0_attn_in_proj_w"],
        "w_xp": np.ascontiguousarray(xp_pad),
        "w_dt": ins["L0_attn_dt_proj_w"],
        "dtb_col": np.ascontiguousarray(dtb),
        "convw_col": np.ascontiguousarray(conv_w),
        "convb_col": np.ascontiguousarray(conv_b),
        "alog_col": np.ascontiguousarray(alog),
        "d_col": np.ascontiguousarray(dcol),
        "w_po": ins["L0_attn_out_proj_w"],
    }


def decl_l0ffn(nc, io, r1_kind, r2_kind):
    if "r1" not in io:
        io["r1"] = nc.dram_tensor("r1", [T, D], F32, kind=r1_kind).ap()
    io["wgT"] = nc.dram_tensor("wgT", [D, HF], I32, kind="ExternalInput").ap()
    io["wuT"] = nc.dram_tensor("wuT", [D, HF], I32, kind="ExternalInput").ap()
    io["wdT"] = nc.dram_tensor("wdT", [HF, D], I32, kind="ExternalInput").ap()
    io["r2"] = nc.dram_tensor("r2", [T, D], F32, kind=r2_kind).ap()
    return io


def emit_l0ffn(nc, S, C, io):
    r1_in = io["r1"]; wgT = io["wgT"]; wuT = io["wuT"]; wdT = io["wdT"]
    r2_out = io["r2"]

    TG = 256
    NG = T // TG

    S.pe_auto_nop = False
    with ExitStack() as stack:
        E = stack.enter_context
        wg_sb = [E(nc.sbuf_tensor(f"wg_sb{d}", [128, HF], BF16)) for d in range(8)]
        wu_sb = [E(nc.sbuf_tensor(f"wu_sb{d}", [128, HF], BF16)) for d in range(8)]
        wd_sb = [E(nc.sbuf_tensor(f"wd_sb{h}", [128, D], BF16)) for h in range(24)]
        r1_tok = [E(nc.sbuf_tensor(f"r1_tok{j}", [128, D], F32)) for j in range(2)]
        r1bf = [E(nc.sbuf_tensor(f"r1bf{j}", [128, D], BF16)) for j in range(2)]
        r1T = [E(nc.sbuf_tensor(f"r1T{d}", [128, TG], BF16)) for d in range(8)]
        g_s = E(nc.sbuf_tensor("g_s", [128, TG], F32))
        u_s = E(nc.sbuf_tensor("u_s", [128, TG], F32))
        h_sb = [E(nc.sbuf_tensor(f"h_sb{f}", [128, TG], BF16)) for f in range(24)]
        r2_t = E(nc.sbuf_tensor("r2_t", [128, D], F32))
        ln_sq = E(nc.sbuf_tensor("ln_sq", [128, D], F32))
        ln_cols = E(nc.sbuf_tensor("ln_cols", [128, 4], F32))
        ps_tr = E(nc.psum_tensor("ps_tr", [128, TG], BF16))
        ps_g = E(nc.psum_tensor("ps_g", [128, TG], F32))
        ps_u = E(nc.psum_tensor("ps_u", [128, TG], F32))
        ps_d = [E(nc.psum_tensor(f"ps_d{i}", [128, 512], F32)) for i in range(2)]

        for d in range(8):
            S.op("gpsimd", lambda e, d=d: e.dma_start(
                out=wg_sb[d][:], in_=wgT[d * 128:(d + 1) * 128, :]),
                writes=[f"wg{d}"], queue="qwg")
            S.op("gpsimd", lambda e, d=d: e.dma_start(
                out=wu_sb[d][:], in_=wuT[d * 128:(d + 1) * 128, :]),
                writes=[f"wu{d}"], queue="qwu")
        for hh in range(24):
            S.op("gpsimd", lambda e, hh=hh: e.dma_start(
                out=wd_sb[hh][:], in_=wdT[hh * 128:(hh + 1) * 128, :]),
                writes=[f"wd{hh}"], queue="qwd")

        for g in range(NG):
            t0 = g * TG
            for j in range(2):
                S.op("sync", lambda e, j=j, t0=t0: e.dma_start(
                    out=r1_tok[j][:], in_=r1_in[t0 + j * 128: t0 + (j + 1) * 128, :]),
                    writes=[f"r1_tok{j}"], queue=f"qr1{j}")
                S.op("scalar", lambda e, j=j: e.copy(out=r1bf[j][:], in_=r1_tok[j][:]),
                     reads=[f"r1_tok{j}"], writes=[f"r1bf{j}"])
            for d in range(8):
                def tr(e, d=d):
                    for j in range(2):
                        last = e.transpose(
                            ps_tr[:, j * 128:(j + 1) * 128],
                            r1bf[j][:, d * 128:(d + 1) * 128], C["ident"][:])
                    return last
                S.op("tensor", tr,
                     reads=[f"r1bf{j}" for j in range(2)] + ["ident"],
                     writes=["ps_tr"])
                S.op("scalar", lambda e, d=d: e.copy(out=r1T[d][:], in_=ps_tr[:]),
                     reads=["ps_tr"], writes=[f"r1T{d}"])

            for f in range(24):
                def mmg(e, f=f):
                    for d in range(8):
                        last = e.matmul(ps_g[:], wg_sb[d][:, f * 128:(f + 1) * 128],
                                        r1T[d][:], start=(d == 0), stop=(d == 7))
                    return last
                S.op("tensor", mmg,
                     reads=[f"wg{d}" for d in range(8)] + [f"r1T{d}" for d in range(8)],
                     writes=["ps_g"])

                def mmu(e, f=f):
                    for d in range(8):
                        last = e.matmul(ps_u[:], wu_sb[d][:, f * 128:(f + 1) * 128],
                                        r1T[d][:], start=(d == 0), stop=(d == 7))
                    return last
                S.op("tensor", mmu,
                     reads=[f"wu{d}" for d in range(8)] + [f"r1T{d}" for d in range(8)],
                     writes=["ps_u"])
                S.op("scalar", lambda e: e.activation(
                    out=g_s[:], in_=ps_g[:], func=AF.Silu, scale=DEQ),
                    reads=["ps_g"], writes=["g_s"])
                S.op("scalar", lambda e: e.activation(
                    out=u_s[:], in_=ps_u[:], func=AF.Copy, scale=DEQ),
                    reads=["ps_u"], writes=["u_s"])
                S.op("vector", lambda e, f=f: e.tensor_tensor(
                    out=h_sb[f][:], in0=g_s[:], in1=u_s[:], op=ALU.mult),
                    reads=["g_s", "u_s"], writes=[f"h{f}"])

            for j in range(2):
                for ch in range(2):
                    def mmd(e, j=j, ch=ch):
                        for hh in range(24):
                            last = e.matmul(
                                ps_d[ch][:], h_sb[hh][:, j * 128:(j + 1) * 128],
                                wd_sb[hh][:, ch * 512:(ch + 1) * 512],
                                start=(hh == 0), stop=(hh == 23))
                        return last
                    S.op("tensor", mmd,
                         reads=[f"h{hh}" for hh in range(24)] + [f"wd{hh}" for hh in range(24)],
                         writes=[f"ps_d{ch}"])
                    S.op("vector", lambda e, j=j, ch=ch: e.scalar_tensor_tensor(
                        out=r2_t[:, ch * 512:(ch + 1) * 512], in0=ps_d[ch][:],
                        scalar=DEQ, in1=r1_tok[j][:, ch * 512:(ch + 1) * 512],
                        op0=ALU.mult, op1=ALU.add),
                        reads=[f"ps_d{ch}", f"r1_tok{j}"], writes=["r2_t"])
                emit_ln(nc, S, "r2_t", r2_t[:],
                        dict(sq=ln_sq, cols=ln_cols), f"ln{g}_{j}")
                S.op("sync", lambda e, j=j, t0=t0: e.dma_start(
                    out=r2_out[t0 + j * 128: t0 + (j + 1) * 128, :], in_=r2_t[:]),
                    reads=["r2_t"], queue="dso")

        S.barrier()
        S.flush()
        S.pe_auto_nop = True


def build_l0ffn():
    nc = bass.Bass("TRN2", target_bir_lowering=False, debug=False)
    io = decl_l0ffn(nc, {}, "ExternalInput", "ExternalOutput")
    with ExitStack() as stack:
        S = Sched(nc, stack)
        C = emit_consts(nc, S, stack)
        emit_l0ffn(nc, S, C, io)
        S.finalize()
    return nc


def prep_l0ffn(ins, core, r1):
    return {
        "r1": r1,
        "wgT": np.ascontiguousarray(ins["L0_ffn_w_gate"].T),
        "wuT": np.ascontiguousarray(ins["L0_ffn_w_up"].T),
        "wdT": np.ascontiguousarray(ins["L0_ffn_w_down"].T),
    }


def perm_runs():
    g = int(math.ceil(math.sqrt(T)))
    grid = np.arange(g * g).reshape(g, g)
    grid[1::2] = grid[1::2, ::-1]
    flat = grid.reshape(-1)
    flat = flat[flat < T][:T]
    runs = []
    i = 0
    while i < len(flat):
        j = i + 1
        step = 0
        while j < len(flat) and flat[j] - flat[j - 1] in (1, -1) and \
                (step == 0 or flat[j] - flat[j - 1] == step):
            step = flat[j] - flat[j - 1]
            j += 1
        runs.append((int(flat[i]), int(step) if step else 1, j - i))
        i = j
    return flat, runs


def decl_l1attn(nc, io, r2_kind, r3_kind):
    if "r2" not in io:
        io["r2"] = nc.dram_tensor("r2", [T, D], F32, kind=r2_kind).ap()
    io["wqkvT"] = nc.dram_tensor("wqkvT", [D, 3 * D], F32, kind="ExternalInput").ap()
    io["woT"] = nc.dram_tensor("woT", [D, D], F32, kind="ExternalInput").ap()
    io["r3"] = nc.dram_tensor("r3", [T, D], F32, kind=r3_kind).ap()
    return io


def emit_l1attn(nc, S, C, io, stop=99):
    r2_in = io["r2"]; wqkvT = io["wqkvT"]; woT = io["woT"]; r3_out = io["r3"]
    s_xp = nc.dram_tensor("s_xp", [T, D], F32, kind="Internal").ap()

    TG = 512
    NG = T // TG

    with ExitStack() as stack:
        E = stack.enter_context
        wq_sb = [E(nc.sbuf_tensor(f"wq_sb{d}", [128, 3 * D], BF16)) for d in range(8)]
        wo_sb = [E(nc.sbuf_tensor(f"wo_sb{d}", [128, D], BF16)) for d in range(8)]
        xp_tok = [E(nc.sbuf_tensor(f"xp_tok{j}", [128, D], F32)) for j in range(4)]
        xpbf = [E(nc.sbuf_tensor(f"xpbf{j}", [128, D], BF16)) for j in range(4)]
        xpT = [E(nc.sbuf_tensor(f"xpT{d}", [128, TG], BF16)) for d in range(8)]
        qkvT = [E(nc.sbuf_tensor(f"qkvT{f}", [128, TG], BF16)) for f in range(24)]
        sc_sb = E(nc.sbuf_tensor("sc_sb", [128, H * HD], F32))
        nmax = E(nc.sbuf_tensor("nmax", [128, H], F32))
        esum = E(nc.sbuf_tensor("esum", [128, H], F32))
        attn_bf = E(nc.sbuf_tensor("attn_bf", [128, H * HD], BF16))
        attnT = E(nc.sbuf_tensor("attnT", [64, 16 * 128], BF16))
        vd_sb = [E(nc.sbuf_tensor(f"vd_sb{s}", [64, 16 * 64], BF16)) for s in range(4)]
        oT = E(nc.sbuf_tensor("oT", [128, 8 * 128], BF16))
        x_res = E(nc.sbuf_tensor("x_res", [128, D], F32))
        r3_t = E(nc.sbuf_tensor("r3_t", [128, D], F32))
        ln_sq = E(nc.sbuf_tensor("ln_sq", [128, D], F32))
        ln_cols = E(nc.sbuf_tensor("ln_cols", [128, 4], F32))
        ps_tr = E(nc.psum_tensor("ps_tr", [128, TG], BF16))
        ps_q = E(nc.psum_tensor("ps_q", [128, TG], F32))
        ps_sc = E(nc.psum_tensor("ps_sc", [128, H * HD], F32))
        ps_ot = E(nc.psum_tensor("ps_ot", [128, 8 * 128], F32))
        ps_o = [E(nc.psum_tensor(f"ps_o{i}", [128, 512], F32)) for i in range(2)]

        for d in range(8):
            S.op("gpsimd", lambda e, d=d: e.dma_start(
                out=wq_sb[d][:], in_=wqkvT[d * 128:(d + 1) * 128, :]),
                writes=[f"wq{d}"], queue="qwq")
            S.op("gpsimd", lambda e, d=d: e.dma_start(
                out=wo_sb[d][:], in_=woT[d * 128:(d + 1) * 128, :]),
                writes=[f"wo{d}"], queue="qwo")

        G46 = 46
        nfull = 22
        S.op("sync", lambda e: e.dma_start(
            out=bass.AP(s_xp.tensor, 0, [[2 * G46 * D, nfull], [1, G46 * D]]),
            in_=bass.AP(r2_in.tensor, 0, [[2 * G46 * D, nfull], [1, G46 * D]])),
            writes=["s_xp"], queue="qgather")
        S.op("sync", lambda e: e.dma_start(
            out=bass.AP(s_xp.tensor, G46 * D, [[2 * G46 * D, nfull], [D, G46], [1, D]]),
            in_=bass.AP(r2_in.tensor, (2 * G46 - 1) * D,
                        [[2 * G46 * D, nfull], [-D, G46], [1, D]])),
            writes=["s_xp"], queue="qgather")
        S.op("sync", lambda e: e.dma_start(
            out=bass.AP(s_xp.tensor, 44 * G46 * D, [[1, (T - 44 * G46) * D]]),
            in_=bass.AP(r2_in.tensor, 44 * G46 * D, [[1, (T - 44 * G46) * D]])),
            writes=["s_xp"], queue="qgather")

        for g in range(NG):
            t0 = g * TG
            for j in range(4):
                S.op("sync", lambda e, j=j, t0=t0: e.dma_start(
                    out=xp_tok[j][:], in_=s_xp[t0 + j * 128: t0 + (j + 1) * 128, :]),
                    reads=["s_xp"], writes=[f"xp_tok{j}"], queue=f"qxp{j}")
                S.op("scalar", lambda e, j=j: e.copy(out=xpbf[j][:], in_=xp_tok[j][:]),
                     reads=[f"xp_tok{j}"], writes=[f"xpbf{j}"])
            for d in range(8 if stop >= 2 else 0):
                def tr(e, d=d):
                    for j in range(4):
                        last = e.transpose(
                            ps_tr[:, j * 128:(j + 1) * 128],
                            xpbf[j][:, d * 128:(d + 1) * 128], C["ident"][:])
                    return last
                S.op("tensor", tr,
                     reads=[f"xpbf{j}" for j in range(4)] + ["ident"],
                     writes=["ps_tr"])
                S.op("scalar", lambda e, d=d: e.copy(out=xpT[d][:], in_=ps_tr[:]),
                     reads=["ps_tr"], writes=[f"xpT{d}"])

            for f in range(24 if stop >= 3 else 0):
                def mmq(e, f=f):
                    for d in range(8):
                        last = e.matmul(ps_q[:], wq_sb[d][:, f * 128:(f + 1) * 128],
                                        xpT[d][:], start=(d == 0), stop=(d == 7))
                    return last
                S.op("tensor", mmq,
                     reads=[f"wq{d}" for d in range(8)] + [f"xpT{d}" for d in range(8)],
                     writes=["ps_q"])
                S.op("scalar", lambda e, f=f: e.copy(out=qkvT[f][:], in_=ps_q[:]),
                     reads=["ps_q"], writes=[f"qkvT{f}"])

            qk = [f"qkvT{f}" for f in range(24)]
            for sg in range(4 if stop >= 4 else 0):
                s0 = sg * 128
                for par in (0, 1):
                    b = 64 * par

                    def trv(e, par=par, b=b, s0=s0):
                        for i in range(8):
                            h = 2 * i + par
                            vt = qkvT[16 + h // 2][b:b + 64, s0:s0 + 128:2]
                            last = e.transpose(
                                ps_tr[0:64, i * 64:i * 64 + 64],
                                vt, C["ident"][b:b + 64, b:b + 64])
                        return last
                    S.op("tensor", trv, reads=qk + ["ident"], writes=["ps_tr"])
                    S.op("scalar", lambda e, par=par, sg=sg: e.copy(
                        out=vd_sb[sg][0:64, :].rearrange(
                            "p (h k) -> p h k", k=64)[:, par::2, :],
                        in_=ps_tr[0:64, 0:512].rearrange(
                            "p (h k) -> p h k", k=64)),
                        reads=["ps_tr"], writes=[f"vd{sg}"])
            for sg in range(4):
                if stop < 4:
                    break
                s0 = sg * 128

                def mmsc(e, s0=s0):
                    for par in (0, 1):
                        if par:
                            e.nop()
                        for h in range(par, H, 2):
                            b = 64 * par
                            qt = qkvT[h // 2][b:b + 64, s0:s0 + 128]
                            kt = qkvT[8 + h // 2][b:b + 64, s0:s0 + 128:2]
                            last = e.matmul(ps_sc[:, h * 64:(h + 1) * 64], qt, kt,
                                            start=True, stop=True)
                    return last
                S.op("tensor", mmsc, reads=qk, writes=["ps_sc"])
                if stop < 5:
                    continue
                S.op("vector", lambda e: e.tensor_reduce(
                    out=nmax[:], in_=ps_sc.ap().rearrange("p (h k) -> p h k", k=HD),
                    axis=AX.X, op=ALU.max, negate=True),
                    reads=["ps_sc"], writes=["nmax"])
                S.op("vector", lambda e: e.tensor_tensor(
                    out=sc_sb.ap().rearrange("p (h k) -> p h k", k=HD),
                    in0=ps_sc.ap().rearrange("p (h k) -> p h k", k=HD),
                    in1=nmax[:, :, None].broadcast_to([128, H, HD]),
                    op=ALU.add),
                    reads=["ps_sc", "nmax"], writes=["sc"])
                S.op("scalar", lambda e: e.activation(
                    out=sc_sb[:], in_=sc_sb[:], func=AF.Exp, scale=QSCALE),
                    reads=["sc"], writes=["sc"])
                S.op("vector", lambda e: e.tensor_reduce(
                    out=esum[:], in_=sc_sb.ap().rearrange("p (h k) -> p h k", k=HD),
                    axis=AX.X, op=ALU.add),
                    reads=["sc"], writes=["esum"])
                S.op("vector", lambda e: e.reciprocal(out=esum[:], in_=esum[:]),
                     reads=["esum"], writes=["esum"])
                S.op("vector", lambda e: e.tensor_tensor(
                    out=attn_bf.ap().rearrange("p (h k) -> p h k", k=HD),
                    in0=sc_sb.ap().rearrange("p (h k) -> p h k", k=HD),
                    in1=esum[:, :, None].broadcast_to([128, H, HD]),
                    op=ALU.mult),
                    reads=["sc", "esum"], writes=["attn"])
                if stop < 6:
                    continue
                for hb in range(4):
                    def trb(e, hb=hb):
                        for hh in range(4):
                            h = hb * 4 + hh
                            last = e.transpose(
                                ps_tr[0:64, hh * 128:hh * 128 + 128],
                                attn_bf[:, h * 64:(h + 1) * 64], C["ident"][:])
                        return last
                    S.op("tensor", trb, reads=["attn", "ident"], writes=["ps_tr"])
                    S.op("scalar", lambda e, hb=hb: e.copy(
                        out=attnT[0:64, hb * 512:(hb + 1) * 512], in_=ps_tr[0:64, 0:512]),
                        reads=["ps_tr"], writes=["attnT"])
                if stop < 7:
                    continue

                def mmo(e, sg=sg):
                    for h in range(H):
                        last = e.matmul(
                            ps_ot[64 * (h % 2):64 * (h % 2) + 64,
                                  (h // 2) * 128:(h // 2) * 128 + 128],
                            vd_sb[sg][0:64, h * 64:(h + 1) * 64],
                            attnT[0:64, h * 128:(h + 1) * 128],
                            start=True, stop=True)
                    return last
                S.op("tensor", mmo, reads=[f"vd{sg}", "attnT"], writes=["ps_ot"])
                S.op("scalar", lambda e: e.copy(out=oT[:], in_=ps_ot[:]),
                     reads=["ps_ot"], writes=["oT"])

                if stop < 8:
                    continue
                tseg = t0 + s0
                S.op("sync", lambda e, tseg=tseg: e.dma_start(
                    out=x_res[:], in_=r2_in[tseg:tseg + 128, :]),
                    writes=["x_res"], queue="qxr")
                for ch in range(2):
                    def mmop(e, ch=ch):
                        for d in range(8):
                            last = e.matmul(
                                ps_o[ch][:], oT[:, d * 128:(d + 1) * 128],
                                wo_sb[d][:, ch * 512:(ch + 1) * 512],
                                start=(d == 0), stop=(d == 7))
                        return last
                    S.op("tensor", mmop,
                         reads=["oT"] + [f"wo{d}" for d in range(8)],
                         writes=[f"ps_o{ch}"])
                    S.op("vector", lambda e, ch=ch: e.tensor_add(
                        r3_t[:, ch * 512:(ch + 1) * 512], ps_o[ch][:],
                        x_res[:, ch * 512:(ch + 1) * 512]),
                        reads=[f"ps_o{ch}", "x_res"], writes=["r3_t"])
                emit_ln(nc, S, "r3_t", r3_t[:],
                        dict(sq=ln_sq, cols=ln_cols), f"ln{g}_{sg}")
                S.op("sync", lambda e, tseg=tseg: e.dma_start(
                    out=r3_out[tseg:tseg + 128, :], in_=r3_t[:]),
                    reads=["r3_t"], queue="dso")

        S.barrier()
        S.flush()


def build_l1attn(stop=99):
    nc = bass.Bass("TRN2", target_bir_lowering=False, debug=False)
    io = decl_l1attn(nc, {}, "ExternalInput", "ExternalOutput")
    with ExitStack() as stack:
        S = Sched(nc, stack)
        C = emit_consts(nc, S, stack)
        emit_l1attn(nc, S, C, io, stop=stop)
        S.finalize()
    return nc


def prep_l1attn(ins, core, r2):
    return {
        "r2": r2,
        "wqkvT": np.ascontiguousarray(ins["L1_attn_w_qkv"].T),
        "woT": np.ascontiguousarray(ins["L1_attn_w_out"].T),
    }


def decl_l1moe(nc, io, r3_kind):
    if "r3" not in io:
        io["r3"] = nc.dram_tensor("r3", [T, D], F32, kind=r3_kind).ap()
    io["wgate"] = nc.dram_tensor("wgate", [D, NE], F32, kind="ExternalInput").ap()
    io["bgate"] = nc.dram_tensor("bgate", [1, NE], F32, kind="ExternalInput").ap()
    io["w1"] = nc.dram_tensor("w1", [NE, D, FE], F32, kind="ExternalInput").ap()
    io["b1"] = nc.dram_tensor("b1", [NE, 128, 16], F32, kind="ExternalInput").ap()
    io["w2"] = nc.dram_tensor("w2", [NE, FE, D], F32, kind="ExternalInput").ap()
    io["b2"] = nc.dram_tensor("b2", [NE, D], F32, kind="ExternalInput").ap()
    io["gam"] = nc.dram_tensor("gam", [1, D], F32, kind="ExternalInput").ap()
    io["bet"] = nc.dram_tensor("bet", [1, D], F32, kind="ExternalInput").ap()
    io["out"] = nc.dram_tensor("out", [T, D], F32, kind="ExternalOutput").ap()
    return io


def emit_l1moe(nc, S, C, io):
    r3_in = io["r3"]; wg_in = io["wgate"]; bg_in = io["bgate"]
    w1_in = io["w1"]; b1_in = io["b1"]; w2_in = io["w2"]; b2_in = io["b2"]
    gam_in = io["gam"]; bet_in = io["bet"]; out_t = io["out"]

    s_ym = nc.dram_tensor("s_ym", [T, D], F32, kind="Internal").ap()

    TG = 512
    NG = T // TG
    BIG = 1e30

    S.pe_auto_nop = False
    with ExitStack() as stack:
        E = stack.enter_context
        r3T = [E(nc.sbuf_tensor(f"r3T{d}", [128, T], BF16)) for d in range(8)]
        wgate_sb = E(nc.sbuf_tensor("wgate_sb", [128, 8 * NE], BF16))
        bgate_sb = E(nc.sbuf_tensor("bgate_sb", [1, NE], F32))
        wfull = [E(nc.sbuf_tensor(f"wfull{tt}", [128, NE], F32)) for tt in range(16)]
        wfT = E(nc.sbuf_tensor("wfT", [NE, 128], F32))
        w1_sb = [[E(nc.sbuf_tensor(f"w1_sb{sl}_{d}", [128, FE], BF16)) for d in range(8)]
                 for sl in range(2)]
        w2_one = [E(nc.sbuf_tensor(f"w2_sb{h}", [128, D], BF16)) for h in range(16)]
        w2_sb = [w2_one, w2_one]
        b1_sb = [E(nc.sbuf_tensor(f"b1_sb{sl}", [128, 16], F32)) for sl in range(2)]
        b2_sb = E(nc.sbuf_tensor("b2_sb", [NE, D], F32))
        h_sb = [E(nc.sbuf_tensor(f"h_sb{f}", [128, TG], BF16)) for f in range(16)]
        gam_b = E(nc.sbuf_tensor("gam_b", [128, D], F32))
        bet_b = E(nc.sbuf_tensor("bet_b", [128, D], F32))
        identf = E(nc.sbuf_tensor("identf", [128, 128], F32))
        lg = E(nc.sbuf_tensor("lg", [128, NE], F32))
        lcols = E(nc.sbuf_tensor("lcols", [128, 4], F32))
        mask = E(nc.sbuf_tensor("mask", [128, NE], F32))
        r3_tok = [E(nc.sbuf_tensor(f"r3_tok{j}", [128, D], F32)) for j in range(2)]
        r3bf = E(nc.sbuf_tensor("r3bf", [128, D], BF16))
        ytmp0 = E(nc.sbuf_tensor("ytmp0", [128, D], F32))
        y_t = E(nc.sbuf_tensor("y_t", [128, D], F32))
        z_t = E(nc.sbuf_tensor("z_t", [128, D], F32))
        ytbuf = [ytmp0, z_t]
        kbuf = ["ytmp0", "z_t"]
        ln_cols = E(nc.sbuf_tensor("ln_cols", [128, 4], F32))
        ps_tr = E(nc.psum_tensor("ps_tr", [128, 512], BF16))
        ps_trf = E(nc.psum_tensor("ps_trf", [128, 128], F32))
        ps_lg = E(nc.psum_tensor("ps_lg", [128, NE], F32))
        ps_h = [E(nc.psum_tensor(f"ps_h{i}", [128, TG], F32)) for i in range(3)]
        ps_eo = [E(nc.psum_tensor(f"ps_eo{i}", [128, 512], F32)) for i in range(2)]

        S.op("gpsimd", lambda e: e.memset(identf[:], 1.0), writes=["identf"])
        S.op("gpsimd", lambda e: e.affine_select(
            out=identf[:], in_=identf[:], pattern=[[-1, 128]],
            compare_op=ALU.is_equal, fill=0.0, base=0, channel_multiplier=1),
            reads=["identf"], writes=["identf"])

        S.op("gpsimd", lambda e: e.dma_start(
            out=wgate_sb.ap().rearrange("p (b m) -> p b m", m=NE),
            in_=wg_in.rearrange("(b p) m -> p b m", p=128)),
            writes=["wgate"], queue="qwgt")
        S.op("sync", lambda e: e.dma_start(out=bgate_sb[0:1, :], in_=bg_in),
             writes=["bgate"], queue="qbg")
        for src_in, dst, dkey in ((gam_in, gam_b, "gam_b"), (bet_in, bet_b, "bet_b")):
            S.op("sync", lambda e, src_in=src_in: e.dma_start(
                out=b2_sb[0:1, :], in_=src_in),
                reads=["ps_eo0", "ps_eo1"], writes=["brow"], queue="qbrow")
            for half in range(2):
                S.op("tensor", lambda e, half=half: e.matmul(
                    ps_eo[half][:], C["ones128"][0:1, :],
                    b2_sb[0:1, half * 512:(half + 1) * 512], start=True, stop=True),
                    reads=["brow", "ones128"], writes=[f"ps_eo{half}"])
                S.op("scalar", lambda e, half=half, dst=dst: e.copy(
                    out=dst[:, half * 512:(half + 1) * 512], in_=ps_eo[half][:]),
                    reads=[f"ps_eo{half}"], writes=[dkey])
        S.op("sync", lambda e: e.dma_start(out=b2_sb[:], in_=b2_in),
             reads=["brow", "ps_eo0", "ps_eo1"], writes=["b2", "brow"], queue="qb2")

        for tt in range(16):
            j = tt % 2
            S.op("sync", lambda e, tt=tt, j=j: e.dma_start(
                out=r3_tok[j][:], in_=r3_in[tt * 128:(tt + 1) * 128, :]),
                writes=[f"r3_tok{j}"], queue=f"qr3{j}")
            S.op("scalar", lambda e, j=j: e.copy(out=r3bf[:], in_=r3_tok[j][:]),
                 reads=[f"r3_tok{j}"], writes=["r3bf"])
            for d in range(8):
                S.op("tensor", lambda e, d=d: e.transpose(
                    ps_tr[:, 0:128], r3bf[:, d * 128:(d + 1) * 128], C["ident"][:]),
                    reads=["r3bf", "ident"], writes=["ps_tr"])
                S.op("scalar", lambda e, d=d, tt=tt: e.copy(
                    out=r3T[d][:, tt * 128:(tt + 1) * 128], in_=ps_tr[:, 0:128]),
                    reads=["ps_tr"], writes=[f"r3T{d}"])

            def mmlg(e, tt=tt):
                for d in range(8):
                    e.matmul(ps_lg[:], r3T[d][:, tt * 128:(tt + 1) * 128],
                             wgate_sb[:, d * NE:(d + 1) * NE],
                             start=(d == 0), stop=False)
                return e.matmul(ps_lg[:], C["ones128"][0:1, 0:128],
                                bgate_sb[0:1, :], start=False, stop=True)
            S.op("tensor", mmlg,
                 reads=[f"r3T{d}" for d in range(8)] + ["wgate", "bgate", "ones128"],
                 writes=["ps_lg"])
            S.op("scalar", lambda e: e.copy(out=lg[:], in_=ps_lg[:]),
                 reads=["ps_lg"], writes=["lg"])
            S.op("vector", lambda e: e.reduce_max(
                out=lcols[:, 0:1], in_=lg[:], axis=AX.X),
                reads=["lg"], writes=["m1"])
            S.op("vector", lambda e: e.tensor_scalar(
                out=mask[:], in0=lg[:], scalar1=lcols[:, 0:1], scalar2=None,
                op0=ALU.is_ge),
                reads=["lg", "m1"], writes=["mask"])
            S.op("vector", lambda e: e.scalar_tensor_tensor(
                out=mask[:], in0=mask[:], scalar=-BIG, in1=lg[:],
                op0=ALU.mult, op1=ALU.add),
                reads=["mask", "lg"], writes=["mask"])
            S.op("vector", lambda e: e.reduce_max(
                out=lcols[:, 1:2], in_=mask[:], axis=AX.X),
                reads=["mask"], writes=["m2"])
            S.op("vector", lambda e: e.tensor_scalar(
                out=lcols[:, 2:3], in0=lcols[:, 0:1], scalar1=-1.0, scalar2=None,
                op0=ALU.mult),
                reads=["m1"], writes=["nm1"])
            S.op("scalar", lambda e: e.activation(
                out=lg[:], in_=lg[:], func=AF.Exp, bias=lcols[:, 2:3]),
                reads=["lg", "nm1"], writes=["lg"])
            S.op("scalar", lambda e: e.activation(
                out=lcols[:, 3:4], in_=lcols[:, 1:2], func=AF.Exp, bias=lcols[:, 2:3]),
                reads=["m2", "nm1"], writes=["em2"])
            S.op("vector", lambda e: e.tensor_scalar(
                out=mask[:], in0=lg[:], scalar1=lcols[:, 3:4], scalar2=None,
                op0=ALU.is_ge),
                reads=["lg", "em2"], writes=["mask"])
            S.op("vector", lambda e: e.tensor_tensor(
                out=lg[:], in0=lg[:], in1=mask[:], op=ALU.mult),
                reads=["lg", "mask"], writes=["lg"])
            S.op("vector", lambda e: e.tensor_scalar(
                out=lcols[:, 3:4], in0=lcols[:, 3:4], scalar1=1.0, scalar2=None,
                op0=ALU.add),
                reads=["em2"], writes=["em2"])
            S.op("vector", lambda e: e.reciprocal(
                out=lcols[:, 3:4], in_=lcols[:, 3:4]),
                reads=["em2"], writes=["em2"])
            S.op("vector", lambda e, tt=tt: e.tensor_scalar(
                out=wfull[tt][:], in0=lg[:], scalar1=lcols[:, 3:4], scalar2=None,
                op0=ALU.mult),
                reads=["lg", "em2"], writes=[f"wfull{tt}"])
            S.op("tensor", lambda e, tt=tt: e.transpose(
                ps_trf[0:NE, 0:128], wfull[tt][:], identf[:]),
                reads=[f"wfull{tt}", "identf"], writes=["ps_trf"])
            S.op("scalar", lambda e: e.copy(out=wfT[:], in_=ps_trf[0:NE, 0:128]),
                 reads=["ps_trf"], writes=["wfT"])
            i = tt % 2
            for ch in range(2):
                S.op("tensor", lambda e, ch=ch: e.matmul(
                    ps_eo[ch][:], wfT[:], b2_sb[:, ch * 512:(ch + 1) * 512],
                    start=True, stop=True),
                    reads=["wfT", "b2"], writes=[f"ps_eo{ch}"])
                S.op("scalar", lambda e, ch=ch, i=i: e.copy(
                    out=ytbuf[i][:, ch * 512:(ch + 1) * 512], in_=ps_eo[ch][:]),
                    reads=[f"ps_eo{ch}"], writes=[kbuf[i]])
            S.op("gpsimd", lambda e, tt=tt, i=i: e.dma_start(
                out=s_ym[tt * 128:(tt + 1) * 128, :], in_=ytbuf[i][:]),
                reads=[kbuf[i]], writes=[f"sym{tt}"], queue=f"qym{tt % 4}")

        for ex in range(NE):
            sl = ex % 2
            for d in range(8):
                S.op("gpsimd", lambda e, ex=ex, d=d, sl=sl: e.dma_start(
                    out=w1_sb[sl][d][:], in_=w1_in[ex, d * 128:(d + 1) * 128, :]),
                    writes=[f"w1_{sl}_{d}"], queue=f"qw1{sl}")
            for hh in range(16):
                S.op("gpsimd", lambda e, ex=ex, hh=hh, sl=sl: e.dma_start(
                    out=w2_sb[sl][hh][:], in_=w2_in[ex, hh * 128:(hh + 1) * 128, :]),
                    writes=[f"w2_{hh}"], queue="qw2")
            S.op("sync", lambda e, ex=ex, sl=sl: e.dma_start(
                out=b1_sb[sl][:], in_=b1_in[ex]),
                writes=[f"b1{sl}"], queue=f"qb1{sl}")
            for g in range(NG):
                t0 = g * TG
                for f in range(16):
                    pi = f % 3

                    def mmh(e, f=f, t0=t0, pi=pi, sl=sl):
                        for d in range(8):
                            last = e.matmul(ps_h[pi][:],
                                            w1_sb[sl][d][:, f * 128:(f + 1) * 128],
                                            r3T[d][:, t0:t0 + TG], start=(d == 0),
                                            stop=(d == 7))
                        return last
                    S.op("tensor", mmh,
                         reads=[f"w1_{sl}_{d}" for d in range(8)] +
                               [f"r3T{d}" for d in range(8)],
                         writes=[f"ps_h{pi}"])
                    S.op("scalar", lambda e, f=f, pi=pi, sl=sl: e.activation(
                        out=h_sb[f][:], in_=ps_h[pi][:], func=AF.Relu,
                        bias=b1_sb[sl][:, f:f + 1]),
                        reads=[f"ps_h{pi}", f"b1{sl}"], writes=[f"h{f}"])
                for j in range(4):
                    tt = g * 4 + j
                    i = j % 2
                    for ch in range(2):
                        pi = ch

                        def mme(e, j=j, ch=ch, pi=pi, sl=sl):
                            for hh in range(16):
                                last = e.matmul(ps_eo[pi][:],
                                                h_sb[hh][:, j * 128:(j + 1) * 128],
                                                w2_sb[sl][hh][:, ch * 512:(ch + 1) * 512],
                                                start=(hh == 0), stop=(hh == 15))
                            return last
                        S.op("tensor", mme,
                             reads=[f"h{hh}" for hh in range(16)] +
                                   [f"w2_{hh}" for hh in range(16)],
                             writes=[f"ps_eo{pi}"])
                        S.op("vector", lambda e, tt=tt, ch=ch, pi=pi, i=i, ex=ex: e.tensor_scalar(
                            out=ytbuf[i][:, ch * 512:(ch + 1) * 512],
                            in0=ps_eo[pi][:], scalar1=wfull[tt][:, ex:ex + 1],
                            scalar2=None, op0=ALU.mult),
                            reads=[f"ps_eo{pi}", f"wfull{tt}"],
                            writes=[kbuf[i]])
                    S.op("gpsimd", lambda e, tt=tt, i=i: e.dma_start(
                        out=s_ym[tt * 128:(tt + 1) * 128, :], in_=ytbuf[i][:],
                        accum_op=ALU.add),
                        reads=[kbuf[i]], writes=[f"sym{tt}"], queue=f"qym{tt % 4}")

        for tt in range(16):
            j = tt % 2
            S.op("sync", lambda e, tt=tt, j=j: e.dma_start(
                out=r3_tok[j][:], in_=r3_in[tt * 128:(tt + 1) * 128, :]),
                writes=[f"r3_tok{j}"], queue=f"qr3{j}")
            S.op("sync", lambda e, tt=tt, j=j: e.dma_start(
                out=y_t[:], in_=s_ym[tt * 128:(tt + 1) * 128, :]),
                reads=[f"sym{tt}"], writes=["y_t"], queue="qyl")
            S.op("vector", lambda e, tt=tt, j=j: e.tensor_add(
                y_t[:], y_t[:], r3_tok[j][:]),
                reads=["y_t", f"r3_tok{j}"], writes=["y_t"])
            emit_ln(nc, S, "y_t", y_t[:], dict(sq=z_t, cols=ln_cols), f"iln{tt}",
                    sq_key="z_t")
            S.op("vector", lambda e: e.tensor_tensor(
                out=y_t[:], in0=y_t[:], in1=gam_b[:], op=ALU.mult),
                reads=["y_t", "gam_b"], writes=["y_t"])
            S.op("vector", lambda e: e.tensor_add(y_t[:], y_t[:], bet_b[:]),
                 reads=["y_t", "bet_b"], writes=["y_t"])
            S.op("vector", lambda e, j=j: e.tensor_add(z_t[:], y_t[:], r3_tok[j][:]),
                 reads=["y_t", f"r3_tok{j}"], writes=["z_t"])
            emit_ln(nc, S, "z_t", z_t[:], dict(sq=y_t, cols=ln_cols), f"oln{tt}",
                    sq_key="y_t")
            S.op("sync", lambda e, tt=tt: e.dma_start(
                out=out_t[tt * 128:(tt + 1) * 128, :], in_=z_t[:]),
                reads=["z_t"], queue="dso")

        S.barrier()
        S.flush()
        S.pe_auto_nop = True


def build_l1moe():
    nc = bass.Bass("TRN2", target_bir_lowering=False, debug=False)
    io = decl_l1moe(nc, {}, "ExternalInput")
    with ExitStack() as stack:
        S = Sched(nc, stack)
        C = emit_consts(nc, S, stack)
        emit_l1moe(nc, S, C, io)
        S.finalize()
    return nc


def prep_l1moe(ins, core, r3):
    return {
        "r3": r3,
        "wgate": ins["L1_ffn_Wg"],
        "bgate": ins["L1_ffn_bg"].reshape(1, NE),
        "w1": ins["L1_ffn_W1"],
        "b1": np.ascontiguousarray(
            ins["L1_ffn_b1"].reshape(NE, 16, 128).transpose(0, 2, 1)),
        "w2": ins["L1_ffn_W2"],
        "b2": ins["L1_ffn_b2"],
        "gam": ins["L1_ffn_gamma"].reshape(1, D),
        "bet": ins["L1_ffn_beta"].reshape(1, D),
    }


class _Prefix:

    def __init__(self, nc, p):
        self._nc = nc
        self._p = p

    def sbuf_tensor(self, name, *a, **k):
        return self._nc.sbuf_tensor(self._p + name, *a, **k)

    def psum_tensor(self, name, *a, **k):
        return self._nc.psum_tensor(self._p + name, *a, **k)

    def dram_tensor(self, name, *a, **k):
        return self._nc.dram_tensor(self._p + name, *a, **k)

    def __getattr__(self, k):
        return getattr(self._nc, k)


def build_fused():
    nc = bass.Bass("TRN2", target_bir_lowering=False, debug=False)
    io = decl_l0attn(nc, "Internal")
    decl_l0ffn(nc, io, None, "Internal")
    decl_l1attn(nc, io, None, "Internal")
    decl_l1moe(nc, io, None)
    with ExitStack() as stack:
        S = Sched(nc, stack)
        C = emit_consts(nc, S, stack)
        emit_l0attn(_Prefix(nc, "a_"), S, C, io)
        with nc.named_scope("l0ffn"):
            emit_l0ffn(_Prefix(nc, "b_"), S, C, io)
        with nc.named_scope("l1attn"):
            emit_l1attn(_Prefix(nc, "c_"), S, C, io)
        with nc.named_scope("l1moe"):
            emit_l1moe(_Prefix(nc, "d_"), S, C, io)
        S.finalize()
    return nc


def prep_fused(ins, core):
    m = prep_l0attn(ins, core)
    m.update({k: v for k, v in prep_l0ffn(ins, core, None).items() if k != "r1"})
    m.update({k: v for k, v in prep_l1attn(ins, core, None).items() if k != "r2"})
    m.update({k: v for k, v in prep_l1moe(ins, core, None).items() if k != "r3"})
    return m


def kernel(**ins):
    cores = list(range(NCORE))
    nc = build_fused()
    res = run_bass_kernel_spmd(nc, [prep_fused(ins, c) for c in cores], core_ids=cores)
    out = [np.asarray(res.results[c]["out"]) for c in cores]
    return np.stack(out, axis=0)


def kernel_unfused(**ins):
    cores = list(range(NCORE))

    nc1 = build_l0attn()
    res1 = run_bass_kernel_spmd(nc1, [prep_l0attn(ins, c) for c in cores], core_ids=cores)
    r1 = [np.asarray(res1.results[c]["r1"]) for c in cores]

    nc2 = build_l0ffn()
    res2 = run_bass_kernel_spmd(nc2, [prep_l0ffn(ins, c, r1[c]) for c in cores], core_ids=cores)
    r2 = [np.asarray(res2.results[c]["r2"]) for c in cores]

    nc3 = build_l1attn()
    res3 = run_bass_kernel_spmd(nc3, [prep_l1attn(ins, c, r2[c]) for c in cores], core_ids=cores)
    r3 = [np.asarray(res3.results[c]["r3"]) for c in cores]

    nc4 = build_l1moe()
    res4 = run_bass_kernel_spmd(nc4, [prep_l1moe(ins, c, r3[c]) for c in cores], core_ids=cores)
    out = [np.asarray(res4.results[c]["out"]) for c in cores]

    return np.stack(out, axis=0)


# version 17
# speedup vs baseline: 1.1697x; 1.1697x over previous
import math
import numpy as np
from contextlib import ExitStack
import concourse.bass as bass
import concourse.mybir as mybir
from concourse.bass_utils import run_bass_kernel_spmd

F32 = mybir.dt.float32
BF16 = mybir.dt.bfloat16
I32 = mybir.dt.int32
AF = mybir.ActivationFunctionType
ALU = mybir.AluOpType
AX = mybir.AxisListType

NCORE = 8
T = 2048
D = 1024
DI = 2048
NS = 16
DTR = 64
HF = 3072
NE = 8
FE = 2048
H = 16
HD = 64
LN_EPS = 1e-5
QSCALE = float(HD) ** -0.5
DEQ = float(np.float32(1.0) / np.float32(127.0))


class Sched:
    ENGS = ("tensor", "vector", "scalar", "gpsimd", "sync")

    def __init__(self, nc, stack):
        self.nc = nc
        self.stack = stack
        self.plan = {e: [] for e in self.ENGS}
        self.sems = {}
        self.counts = {}
        self.wm = {e: {} for e in self.ENGS}
        self.bufs = {}
        for e in self.ENGS:
            self._sem(e)

    def _sem(self, name):
        if name not in self.sems:
            self.sems[name] = self.stack.enter_context(self.nc.semaphore("s_" + name))
            self.counts[name] = 0
        return self.sems[name]

    def op(self, eng, fn, reads=(), writes=(), queue=None):
        semkey = queue or eng
        self._sem(semkey)
        inc = 16 if queue else 1
        waits = {}
        pe_auto_nop = getattr(self, "pe_auto_nop", True)

        def add(tok):
            if tok is None:
                return
            s, v = tok
            if self.wm[eng].get(s, 0) < v and waits.get(s, 0) < v:
                waits[s] = v

        for k in reads:
            st = self.bufs.get(k)
            if st:
                add(st["w"])
        for k in writes:
            st = self.bufs.get(k)
            if st:
                add(st["w"])
                for r in st["r"]:
                    add(r)

        for s in list(waits):
            if s not in self.ENGS:
                if self.counts[s] > waits[s]:
                    waits[s] = self.counts[s]
                if self.wm[eng].get(s, 0) >= waits[s]:
                    del waits[s]

        self.counts[semkey] += inc
        tok = (semkey, self.counts[semkey])
        wlist = sorted(waits.items())
        sems = self.sems

        pe_nop = (eng == "tensor" and queue is None and pe_auto_nop)

        def thunk(engine, wlist=wlist, fn=fn, semkey=semkey, inc=inc, pe_nop=pe_nop):
            for s, v in wlist:
                engine.wait_ge(sems[s], v)
            if pe_nop:
                engine.nop()
            fn(engine).then_inc(sems[semkey], inc)

        self.plan[eng].append(thunk)
        for s, v in waits.items():
            self.wm[eng][s] = max(self.wm[eng].get(s, 0), v)
        for k in writes:
            self.bufs[k] = {"w": tok, "r": []}
        for k in reads:
            self.bufs.setdefault(k, {"w": None, "r": []})["r"].append(tok)
        return tok

    def barrier(self):
        tgt = dict(self.counts)
        sems = self.sems
        for e in self.ENGS:
            wl = [(s, v) for s, v in sorted(tgt.items())
                  if v > 0 and self.wm[e].get(s, 0) < v]

            def thunk(engine, wl=wl):
                for s, v in wl:
                    engine.wait_ge(sems[s], v)
            self.plan[e].append(thunk)
            for s, v in tgt.items():
                self.wm[e][s] = max(self.wm[e].get(s, 0), v)
        self.bufs = {}

    def flush(self):
        def mk(plan):
            def closure(engine):
                for th in plan:
                    th(engine)
            return closure
        if any(self.plan[e] for e in self.ENGS):
            with self.nc.Block() as block:
                for e in self.ENGS:
                    if self.plan[e]:
                        getattr(block, e)(mk(self.plan[e]))
        self.plan = {e: [] for e in self.ENGS}

    def finalize(self):
        tgt = dict(self.counts)
        sems = self.sems

        def final_thunk(engine):
            for s, v in sorted(tgt.items()):
                if v > 0:
                    engine.wait_ge(sems[s], v)
        self.plan["sync"].append(final_thunk)
        self.flush()


def emit_consts(nc, S, stack):
    E = stack.enter_context
    ident = E(nc.sbuf_tensor("c_ident", [128, 128], BF16))
    ones128 = E(nc.sbuf_tensor("c_ones128", [1, 128], F32))

    S.op("gpsimd", lambda e: e.memset(ident[:], 1.0), writes=["ident"])
    S.op("gpsimd", lambda e: e.affine_select(
        out=ident[:], in_=ident[:], pattern=[[-1, 128]],
        compare_op=ALU.is_equal, fill=0.0, base=0, channel_multiplier=1),
        reads=["ident"], writes=["ident"])
    S.op("gpsimd", lambda e: e.memset(ones128[0:1, :], 1.0), writes=["ones128"])
    ceps = E(nc.sbuf_tensor("c_eps", [128, 1], F32))
    S.op("gpsimd", lambda e: e.memset(ceps[:], LN_EPS), writes=["ceps"])
    nc.const_aps.register(F32, LN_EPS, ceps[:, 0:1])
    S.barrier()
    return dict(ident=ident, ones128=ones128)


def emit_sel(nc, S, E):
    sel = E(nc.sbuf_tensor("c_sel", [NS, NS * 128], F32))
    S.op("gpsimd", lambda e: e.memset(sel[:], 1.0), writes=["sel"])
    S.op("gpsimd", lambda e: e.affine_select(
        out=sel[:], in_=sel[:], pattern=[[1, NS * 128]],
        compare_op=ALU.is_ge, fill=0.0, base=0, channel_multiplier=-128),
        reads=["sel"], writes=["sel"])
    S.op("gpsimd", lambda e: e.affine_select(
        out=sel[:], in_=sel[:], pattern=[[-1, NS * 128]],
        compare_op=ALU.is_ge, fill=0.0, base=127, channel_multiplier=128),
        reads=["sel"], writes=["sel"])
    return sel


def emit_ln(nc, S, key, tile, scr, tag, sq_key=None):
    sq, cols = scr["sq"], scr["cols"]
    ksq, kc = (sq_key or tag + "sq"), tag + "c"
    S.op("vector", lambda e: e.tensor_scalar(
        out=sq[:], in0=tile, scalar1=-1.0 / D, scalar2=None,
        op0=ALU.mult, op1=ALU.add, accum_out=cols[:, 1:2]),
        reads=[key], writes=[ksq, kc + "1"])
    S.op("scalar", lambda e: e.activation(
        out=sq[:], in_=tile, func=AF.Square, bias=cols[:, 1:2]),
        reads=[key, kc + "1"], writes=[ksq])
    S.op("vector", lambda e: e.tensor_scalar(
        out=sq[:], in0=sq[:], scalar1=1.0 / D, scalar2=None,
        op0=ALU.mult, op1=ALU.add, accum_out=cols[:, 2:3]),
        reads=[ksq], writes=[ksq, kc + "2"])
    S.op("scalar", lambda e: e.activation(
        out=cols[:, 2:3], in_=cols[:, 2:3], func=AF.Ln, bias=LN_EPS),
        reads=[kc + "2"], writes=[kc + "2"])
    S.op("scalar", lambda e: e.activation(
        out=cols[:, 2:3], in_=cols[:, 2:3], func=AF.Exp, scale=-0.5),
        reads=[kc + "2"], writes=[kc + "2"])
    S.op("vector", lambda e: e.tensor_tensor(
        out=cols[:, 3:4], in0=cols[:, 1:2], in1=cols[:, 2:3], op=ALU.mult),
        reads=[kc + "1", kc + "2"], writes=[kc + "3"])
    S.op("vector", lambda e: e.tensor_scalar(
        out=tile, in0=tile, scalar1=cols[:, 2:3], scalar2=cols[:, 3:4],
        op0=ALU.mult, op1=ALU.add),
        reads=[key, kc + "2", kc + "3"], writes=[key])


def decl_l0attn(nc, r1_kind):
    io = {}
    io["x"] = nc.dram_tensor("x", [T, D], F32, kind="ExternalInput").ap()
    io["w_in"] = nc.dram_tensor("w_in", [D, 2 * DI], F32, kind="ExternalInput").ap()
    io["w_xp"] = nc.dram_tensor("w_xp", [DI, 128], F32, kind="ExternalInput").ap()
    io["w_dt"] = nc.dram_tensor("w_dt", [DTR, DI], F32, kind="ExternalInput").ap()
    io["dtb_col"] = nc.dram_tensor("dtb_col", [128, 16], F32, kind="ExternalInput").ap()
    io["convw_col"] = nc.dram_tensor("convw_col", [128, 64], F32, kind="ExternalInput").ap()
    io["convb_col"] = nc.dram_tensor("convb_col", [128, 16], F32, kind="ExternalInput").ap()
    io["alog_col"] = nc.dram_tensor("alog_col", [128, 256], F32, kind="ExternalInput").ap()
    io["d_col"] = nc.dram_tensor("d_col", [128, 16], F32, kind="ExternalInput").ap()
    io["w_po"] = nc.dram_tensor("w_po", [DI, D], F32, kind="ExternalInput").ap()
    io["r1"] = nc.dram_tensor("r1", [T, D], F32, kind=r1_kind).ap()
    return io


def emit_l0attn(nc, S, C, io):
    x_in = io["x"]; w_in = io["w_in"]; w_xp = io["w_xp"]; w_dt = io["w_dt"]
    dtb_col = io["dtb_col"]; convw_col = io["convw_col"]; convb_col = io["convb_col"]
    alog_col = io["alog_col"]; d_col = io["d_col"]; w_po = io["w_po"]
    r1_out = io["r1"]

    s_xi = nc.dram_tensor("s_xi", [DI, T], F32, kind="Internal").ap()
    s_gres = nc.dram_tensor("s_gres", [DI, T], F32, kind="Internal").ap()
    s_dt = nc.dram_tensor("s_dt", [DI, T], F32, kind="Internal").ap()
    s_prod = nc.dram_tensor("s_prod", [DI, T], BF16, kind="Internal").ap()
    s_bt = nc.dram_tensor("s_bt", [NS, T], F32, kind="Internal").ap()
    s_ct = nc.dram_tensor("s_ct", [NS, T], F32, kind="Internal").ap()

    TC = 512
    NCH = T // TC

    with ExitStack() as stack:
        E = stack.enter_context
        C = dict(C, sel=emit_sel(nc, S, E))

        a_sb = E(nc.sbuf_tensor("a_sb", [128, 256], F32))
        dcol_sb = E(nc.sbuf_tensor("dcol_sb", [128, 16], F32))
        dtb_sb = E(nc.sbuf_tensor("dtb_sb", [128, 16], F32))
        cwcol_sb = E(nc.sbuf_tensor("cwcol_sb", [128, 64], F32))
        cbcol_sb = E(nc.sbuf_tensor("cbcol_sb", [128, 16], F32))
        hist = E(nc.sbuf_tensor("hist", [128, 3 * 16], F32))

        S.op("sync", lambda e: e.dma_start(out=a_sb[:], in_=alog_col),
             writes=["A"], queue="qA")
        S.op("scalar", lambda e: e.activation(out=a_sb[:], in_=a_sb[:], func=AF.Exp),
             reads=["A"], writes=["A"])
        S.op("vector", lambda e: e.tensor_scalar(
            out=a_sb[:], in0=a_sb[:], scalar1=-1.0, scalar2=None, op0=ALU.mult),
            reads=["A"], writes=["A"])
        S.op("sync", lambda e: e.dma_start(out=dcol_sb[:], in_=d_col),
             writes=["Dcol"], queue="qD")
        S.op("sync", lambda e: e.dma_start(out=dtb_sb[:], in_=dtb_col),
             writes=["dtb"], queue="qdtb")
        S.op("sync", lambda e: e.dma_start(out=cwcol_sb[:], in_=convw_col),
             writes=["cw"], queue="qcw")
        S.op("sync", lambda e: e.dma_start(out=cbcol_sb[:], in_=convb_col),
             writes=["cb"], queue="qcb")
        S.op("vector", lambda e: e.memset(hist[:], 0.0), writes=["hist"])

        with ExitStack() as ph:
            ph.enter_context(nc.named_scope("l0a_inproj"))
            Ep = ph.enter_context
            w_in_sb = [Ep(nc.sbuf_tensor(f"w_in_sb{d}", [128, 2 * DI], BF16))
                       for d in range(8)]
            w_xp_sb = Ep(nc.sbuf_tensor("w_xp_sb", [128, 16 * 128], BF16))
            w_dt_sb = Ep(nc.sbuf_tensor("w_dt_sb", [128, DI], BF16))
            x_tok = [Ep(nc.sbuf_tensor(f"x_tok{j}", [128, D], F32)) for j in range(4)]
            xbf = [Ep(nc.sbuf_tensor(f"xbf{j}", [128, D], BF16)) for j in range(4)]
            xT = [Ep(nc.sbuf_tensor(f"xT{d}", [128, TC], BF16)) for d in range(8)]
            NB = 4
            convbuf = [Ep(nc.sbuf_tensor(f"convbuf{i}", [128, TC + 3], F32)) for i in range(NB)]
            convacc = [Ep(nc.sbuf_tensor(f"convacc{i}", [128, TC], F32)) for i in range(NB)]
            xi_t = [Ep(nc.sbuf_tensor(f"xi_t{i}", [128, TC], F32)) for i in range(NB)]
            xibk_t = [Ep(nc.sbuf_tensor(f"xibk_t{k}", [128, TC], BF16)) for k in range(16)]
            gres_t = [Ep(nc.sbuf_tensor(f"gres_t{i}", [128, TC], F32)) for i in range(NB)]
            xdbl = Ep(nc.sbuf_tensor("xdbl", [128, TC], F32))
            deltab = Ep(nc.sbuf_tensor("deltab", [128, TC], BF16))
            dt_t = [Ep(nc.sbuf_tensor(f"dt_t{i}", [128, TC], F32)) for i in range(NB)]
            ps_tr = Ep(nc.psum_tensor("ps_tr", [128, TC], BF16))
            ps_m = [Ep(nc.psum_tensor(f"ps_m{i}", [128, 512], F32)) for i in range(4)]
            ps_xp = Ep(nc.psum_tensor("ps_xp", [128, 512], F32))
            ps_dt = [Ep(nc.psum_tensor(f"ps_dt{i}", [128, 512], F32)) for i in range(2)]

            for d in range(8):
                S.op("gpsimd", lambda e, d=d: e.dma_start(
                    out=w_in_sb[d][:], in_=w_in[d * 128:(d + 1) * 128, :]),
                    writes=[f"w_in{d}"], queue="qwin")
            S.op("gpsimd", lambda e: e.dma_start(
                out=w_xp_sb.ap().rearrange("p (b m) -> p b m", m=128),
                in_=w_xp.rearrange("(b p) m -> p b m", p=128)),
                writes=["w_xp"], queue="qwxp")
            S.op("gpsimd", lambda e: e.dma_start(out=w_dt_sb[64:128, :], in_=w_dt),
                 writes=["w_dt"], queue="qwdt")

            for c in range(NCH):
                t0 = c * TC
                for j in range(4):
                    S.op("sync", lambda e, j=j, t0=t0: e.dma_start(
                        out=x_tok[j][:], in_=x_in[t0 + j * 128: t0 + (j + 1) * 128, :]),
                        writes=[f"x_tok{j}"], queue=f"qx{j}")
                    S.op("scalar", lambda e, j=j: e.copy(out=xbf[j][:], in_=x_tok[j][:]),
                         reads=[f"x_tok{j}"], writes=[f"xbf{j}"])
                for d in range(8):
                    def tr(e, d=d):
                        for j in range(4):
                            last = e.transpose(
                                ps_tr[:, j * 128:(j + 1) * 128],
                                xbf[j][:, d * 128:(d + 1) * 128], C["ident"][:])
                        return last
                    S.op("tensor", tr,
                         reads=[f"xbf{j}" for j in range(4)] + ["ident"],
                         writes=["ps_tr"])
                    S.op("scalar", lambda e, d=d: e.copy(out=xT[d][:], in_=ps_tr[:]),
                         reads=["ps_tr"], writes=[f"xT{d}"])

                for fo in range(32):
                    pm = ps_m[fo % 4]
                    pk = f"ps_m{fo % 4}"

                    def mm(e, fo=fo, pm=pm):
                        for d in range(8):
                            last = e.matmul(
                                pm[:], w_in_sb[d][:, fo * 128:(fo + 1) * 128],
                                xT[d][:], start=(d == 0), stop=(d == 7))
                        return last
                    S.op("tensor", mm,
                         reads=[f"w_in{d}" for d in range(8)] + [f"xT{d}" for d in range(8)],
                         writes=[pk])
                    if fo < 16:
                        dblk = fo
                        i = dblk % 4
                        S.op("scalar", lambda e, pm=pm, i=i: e.copy(
                            out=convbuf[i][:, 3:TC + 3], in_=pm[:]),
                            reads=[pk], writes=[f"convbuf{i}"])
                        S.op("vector", lambda e, i=i, dblk=dblk: e.tensor_copy(
                            convbuf[i][:, 0:3], hist[:, dblk * 3:(dblk + 1) * 3]),
                            reads=["hist"], writes=[f"convbuf{i}"])
                        S.op("vector", lambda e, i=i, dblk=dblk: e.tensor_scalar(
                            out=convacc[i][:], in0=convbuf[i][:, 0:TC],
                            scalar1=cwcol_sb[:, dblk * 4:dblk * 4 + 1], scalar2=None,
                            op0=ALU.mult),
                            reads=[f"convbuf{i}", "cw"], writes=[f"convacc{i}"])
                        for k in (1, 2, 3):
                            S.op("vector", lambda e, i=i, dblk=dblk, k=k: e.scalar_tensor_tensor(
                                out=convacc[i][:], in0=convbuf[i][:, k:TC + k],
                                scalar=cwcol_sb[:, dblk * 4 + k:dblk * 4 + k + 1],
                                in1=convacc[i][:], op0=ALU.mult, op1=ALU.add),
                                reads=[f"convbuf{i}", "cw", f"convacc{i}"],
                                writes=[f"convacc{i}"])
                        S.op("vector", lambda e, i=i, dblk=dblk: e.tensor_copy(
                            hist[:, dblk * 3:(dblk + 1) * 3], convbuf[i][:, TC:TC + 3]),
                            reads=[f"convbuf{i}"], writes=["hist"])
                        S.op("scalar", lambda e, i=i, dblk=dblk: e.activation(
                            out=xi_t[i][:], in_=convacc[i][:], func=AF.Silu,
                            bias=cbcol_sb[:, dblk:dblk + 1]),
                            reads=[f"convacc{i}", "cb"], writes=[f"xi_t{i}"])
                        S.op("gpsimd", lambda e, i=i, dblk=dblk, t0=t0: e.dma_start(
                            out=s_xi[dblk * 128:(dblk + 1) * 128, t0:t0 + TC],
                            in_=xi_t[i][:]),
                            reads=[f"xi_t{i}"], queue="dst")
                        S.op("scalar", lambda e, i=i, dblk=dblk: e.copy(
                            out=xibk_t[dblk][:], in_=xi_t[i][:]),
                            reads=[f"xi_t{i}"], writes=[f"xib{dblk}"])
                    else:
                        gblk = fo - 16
                        i = gblk % 4
                        S.op("scalar", lambda e, pm=pm, i=i: e.activation(
                            out=gres_t[i][:], in_=pm[:], func=AF.Silu),
                            reads=[pk], writes=[f"gres_t{i}"])
                        S.op("gpsimd", lambda e, i=i, gblk=gblk, t0=t0: e.dma_start(
                            out=s_gres[gblk * 128:(gblk + 1) * 128, t0:t0 + TC],
                            in_=gres_t[i][:]),
                            reads=[f"gres_t{i}"], queue="dst")

                def mmxp(e):
                    for dblk in range(16):
                        last = e.matmul(
                            ps_xp[:], w_xp_sb[:, dblk * 128:(dblk + 1) * 128],
                            xibk_t[dblk][:], start=(dblk == 0), stop=(dblk == 15))
                    return last
                S.op("tensor", mmxp,
                     reads=[f"xib{k}" for k in range(16)] + ["w_xp"],
                     writes=["ps_xp"])

                S.op("scalar", lambda e: e.copy(out=xdbl[:], in_=ps_xp[:]),
                     reads=["ps_xp"], writes=["xdbl"])
                S.op("sync", lambda e, t0=t0: e.dma_start(
                    out=s_bt[:, t0:t0 + TC], in_=xdbl[0:16, :]),
                    reads=["xdbl"], writes=["BT"], queue="qbt")
                S.op("sync", lambda e, t0=t0: e.dma_start(
                    out=s_ct[:, t0:t0 + TC], in_=xdbl[32:48, :]),
                    reads=["xdbl"], writes=["CT"], queue="qct")
                S.op("scalar", lambda e: e.copy(
                    out=deltab[64:128, :], in_=xdbl[64:128, :]),
                    reads=["xdbl"], writes=["deltab"])
                for dblk in range(16):
                    i = dblk % 4
                    pd = dblk % 2
                    S.op("tensor", lambda e, dblk=dblk, pd=pd: e.matmul(
                        ps_dt[pd][:], w_dt_sb[64:128, dblk * 128:(dblk + 1) * 128],
                        deltab[64:128, :], start=True, stop=True),
                        reads=["deltab", "w_dt"], writes=[f"ps_dt{pd}"])
                    S.op("scalar", lambda e, dblk=dblk, i=i, pd=pd: e.activation(
                        out=dt_t[i][:], in_=ps_dt[pd][:], func=AF.Exp,
                        bias=dtb_sb[:, dblk:dblk + 1]),
                        reads=[f"ps_dt{pd}", "dtb"], writes=[f"dt_t{i}"])
                    S.op("scalar", lambda e, i=i: e.activation(
                        out=dt_t[i][:], in_=dt_t[i][:], func=AF.Ln, bias=1.0),
                        reads=[f"dt_t{i}"], writes=[f"dt_t{i}"])
                    S.op("gpsimd", lambda e, i=i, dblk=dblk, t0=t0: e.dma_start(
                        out=s_dt[dblk * 128:(dblk + 1) * 128, t0:t0 + TC],
                        in_=dt_t[i][:]),
                        reads=[f"dt_t{i}"], queue="dst")

            S.barrier()
            S.flush()

        with ExitStack() as ph:
            ph.enter_context(nc.named_scope("l0a_scan"))
            Ep = ph.enter_context
            dtd = [Ep(nc.sbuf_tensor(f"dtd{i}", [128, T], F32)) for i in range(2)]
            xid = [Ep(nc.sbuf_tensor(f"xid{i}", [128, T], F32)) for i in range(2)]
            grd = [Ep(nc.sbuf_tensor(f"grd{i}", [128, T], F32)) for i in range(2)]
            dud = [Ep(nc.sbuf_tensor(f"dud{i}", [128, T], F32)) for i in range(2)]
            bb = [Ep(nc.sbuf_tensor(f"bb{i}", [128, T], F32)) for i in range(2)]
            cbx = [Ep(nc.sbuf_tensor(f"cbx{i}", [128, T], F32)) for i in range(2)]
            at = [Ep(nc.sbuf_tensor(f"at{i}", [128, T], F32)) for i in range(2)]
            ut = Ep(nc.sbuf_tensor("ut", [128, T], F32))
            st_ = Ep(nc.sbuf_tensor("st_", [128, T], F32))
            zt = [Ep(nc.sbuf_tensor(f"zt{i}", [128, T], BF16)) for i in range(2)]
            yt = [Ep(nc.sbuf_tensor(f"yt{i}", [128, T], F32)) for i in range(2)]
            prodb = [Ep(nc.sbuf_tensor(f"prodb{i}", [128, T], BF16)) for i in range(2)]
            ps_y = [Ep(nc.psum_tensor(f"ps_y{i}", [128, T], F32)) for i in range(2)]

            def bcast(n, src_dram, src_key, dst, dst_key, qn):
                S.op("gpsimd", lambda e, n=n: e.dma_start(
                    out=dst[:], in_=bass.AP(src_dram.tensor, n * T + src_dram.offset,
                                            [[0, 128], [1, T]])),
                    reads=[src_key], writes=[dst_key], queue=qn)

            for dpair in range(8):
                for p in range(2):
                    dblk = 2 * dpair + p
                    S.op("sync", lambda e, dblk=dblk, p=p: e.dma_start(
                        out=dtd[p][:], in_=s_dt[dblk * 128:(dblk + 1) * 128, :]),
                        writes=[f"dtd{p}"], queue=f"qdtd{p}")
                    S.op("sync", lambda e, dblk=dblk, p=p: e.dma_start(
                        out=xid[p][:], in_=s_xi[dblk * 128:(dblk + 1) * 128, :]),
                        writes=[f"xid{p}"], queue=f"qxid{p}")
                    S.op("sync", lambda e, dblk=dblk, p=p: e.dma_start(
                        out=grd[p][:], in_=s_gres[dblk * 128:(dblk + 1) * 128, :]),
                        writes=[f"grd{p}"], queue=f"qgrd{p}")
                    S.op("vector", lambda e, p=p: e.tensor_tensor(
                        out=dud[p][:], in0=dtd[p][:], in1=xid[p][:], op=ALU.mult),
                        reads=[f"dtd{p}", f"xid{p}"], writes=[f"dud{p}"])
                for n in range(NS):
                    i = n % 2
                    bcast(n, s_bt, "BT", bb[i], f"bb{i}", f"qbb{i}")
                    bcast(n, s_ct, "CT", cbx[i], f"cbx{i}", f"qcb{i}")
                    for p in range(2):
                        dblk = 2 * dpair + p
                        S.op("scalar", lambda e, p=p, dblk=dblk, n=n: e.activation(
                            out=at[p][:], in_=dtd[p][:], func=AF.Exp,
                            scale=a_sb[:, dblk * 16 + n:dblk * 16 + n + 1]),
                            reads=[f"dtd{p}", "A"], writes=[f"at{p}"])
                        S.op("vector", lambda e, i=i, p=p: e.tensor_tensor(
                            out=ut[:], in0=dud[p][:], in1=bb[i][:], op=ALU.mult),
                            reads=[f"dud{p}", f"bb{i}"], writes=["ut"])
                        S.op("vector", lambda e, p=p: e.tensor_tensor_scan(
                            out=st_[:], data0=at[p][:], data1=ut[:], initial=0.0,
                            op0=ALU.mult, op1=ALU.add),
                            reads=[f"at{p}", "ut"], writes=["st"])
                        S.op("vector", lambda e, i=i, p=p: e.tensor_tensor(
                            out=zt[p][:], in0=st_[:], in1=cbx[i][:], op=ALU.mult),
                            reads=["st", f"cbx{i}"], writes=[f"zt{p}"])

                        def mmy(e, p=p, n=n):
                            for q in range(4):
                                last = e.matmul(
                                    ps_y[p][:, q * 512:(q + 1) * 512],
                                    C["ident"][:], zt[p][:, q * 512:(q + 1) * 512],
                                    start=(n == 0), stop=(n == NS - 1))
                            return last
                        S.op("tensor", mmy, reads=[f"zt{p}", "ident"],
                             writes=[f"ps_y{p}"])
                for p in range(2):
                    dblk = 2 * dpair + p
                    S.op("vector", lambda e, p=p, dblk=dblk: e.scalar_tensor_tensor(
                        out=yt[p][:], in0=xid[p][:], scalar=dcol_sb[:, dblk:dblk + 1],
                        in1=ps_y[p][:], op0=ALU.mult, op1=ALU.add),
                        reads=[f"xid{p}", "Dcol", f"ps_y{p}"], writes=[f"yt{p}"])
                    S.op("vector", lambda e, p=p: e.tensor_tensor(
                        out=prodb[p][:], in0=yt[p][:], in1=grd[p][:], op=ALU.mult),
                        reads=[f"yt{p}", f"grd{p}"], writes=[f"prodb{p}"])
                    S.op("gpsimd", lambda e, dblk=dblk, p=p: e.dma_start(
                        out=s_prod[dblk * 128:(dblk + 1) * 128, :], in_=prodb[p][:]),
                        reads=[f"prodb{p}"], queue="dst")

            S.barrier()
            S.flush()

        with ExitStack() as ph:
            ph.enter_context(nc.named_scope("l0a_outproj"))
            Ep = ph.enter_context
            w_po_sb = [Ep(nc.sbuf_tensor(f"w_po_sb{b}", [128, D], BF16)) for b in range(16)]
            prod_sb = [Ep(nc.sbuf_tensor(f"prod_sb{i}", [128, 16 * 128], BF16)) for i in range(2)]
            x_res = [Ep(nc.sbuf_tensor(f"x_res{i}", [128, D], F32)) for i in range(2)]
            r1_t = [Ep(nc.sbuf_tensor(f"r1_t{i}", [128, D], F32)) for i in range(2)]
            ln_sq = Ep(nc.sbuf_tensor("ln_sq", [128, D], F32))
            ln_cols = Ep(nc.sbuf_tensor("ln_cols", [128, 4], F32))
            ps_o = [Ep(nc.psum_tensor(f"ps_o{i}", [128, 512], F32)) for i in range(2)]

            for b in range(16):
                S.op("gpsimd", lambda e, b=b: e.dma_start(
                    out=w_po_sb[b][:], in_=w_po[b * 128:(b + 1) * 128, :]),
                    writes=[f"w_po{b}"], queue="qwpo")

            for tt in range(16):
                i = tt % 2
                S.op("sync", lambda e, tt=tt, i=i: e.dma_start(
                    out=prod_sb[i][:],
                    in_=s_prod.rearrange("(b p) t -> p b t", p=128)[:, :, tt * 128:(tt + 1) * 128]),
                    writes=[f"prod_sb{i}"], queue=f"qprodl{i}")
                S.op("sync", lambda e, tt=tt, i=i: e.dma_start(
                    out=x_res[i][:], in_=x_in[tt * 128:(tt + 1) * 128, :]),
                    writes=[f"x_res{i}"], queue=f"qxr{i}")
                for ch in range(2):
                    def mm(e, i=i, ch=ch):
                        for b in range(16):
                            last = e.matmul(
                                ps_o[ch][:], prod_sb[i][:, b * 128:(b + 1) * 128],
                                w_po_sb[b][:, ch * 512:(ch + 1) * 512],
                                start=(b == 0), stop=(b == 15))
                        return last
                    S.op("tensor", mm,
                         reads=[f"prod_sb{i}"] + [f"w_po{b}" for b in range(16)],
                         writes=[f"ps_o{ch}"])
                    S.op("vector", lambda e, i=i, ch=ch: e.tensor_add(
                        r1_t[i][:, ch * 512:(ch + 1) * 512], ps_o[ch][:],
                        x_res[i][:, ch * 512:(ch + 1) * 512]),
                        reads=[f"ps_o{ch}", f"x_res{i}"], writes=[f"r1_t{i}"])
                emit_ln(nc, S, f"r1_t{i}", r1_t[i][:],
                        dict(sq=ln_sq, cols=ln_cols), f"ln{tt}")
                S.op("sync", lambda e, tt=tt, i=i: e.dma_start(
                    out=r1_out[tt * 128:(tt + 1) * 128, :], in_=r1_t[i][:]),
                    reads=[f"r1_t{i}"], queue="dso")

            S.barrier()
            S.flush()


def build_l0attn():
    nc = bass.Bass("TRN2", target_bir_lowering=False, debug=False)
    io = decl_l0attn(nc, "ExternalOutput")
    with ExitStack() as stack:
        S = Sched(nc, stack)
        C = emit_consts(nc, S, stack)
        emit_l0attn(nc, S, C, io)
        S.finalize()
    return nc


def prep_l0attn(ins, core):
    conv_w = ins["L0_attn_conv_w"].reshape(16, 128, 4).transpose(1, 0, 2).reshape(128, 64)
    conv_b = ins["L0_attn_conv_b"].reshape(16, 128).T
    dtb = ins["L0_attn_dt_proj_b"].reshape(16, 128).T
    alog = ins["L0_attn_A_log"].reshape(16, 128, 16).transpose(1, 0, 2).reshape(128, 256)
    dcol = ins["L0_attn_Dparam"].reshape(16, 128).T
    xp = ins["L0_attn_x_proj_w"]
    xp_pad = np.zeros((DI, 128), np.float32)
    xp_pad[:, 0:16] = xp[:, 64:80]
    xp_pad[:, 32:48] = xp[:, 80:96]
    xp_pad[:, 64:128] = xp[:, 0:64]
    return {
        "x": np.ascontiguousarray(ins["L0_attn_x"][core]),
        "w_in": ins["L0_attn_in_proj_w"],
        "w_xp": np.ascontiguousarray(xp_pad),
        "w_dt": ins["L0_attn_dt_proj_w"],
        "dtb_col": np.ascontiguousarray(dtb),
        "convw_col": np.ascontiguousarray(conv_w),
        "convb_col": np.ascontiguousarray(conv_b),
        "alog_col": np.ascontiguousarray(alog),
        "d_col": np.ascontiguousarray(dcol),
        "w_po": ins["L0_attn_out_proj_w"],
    }


def decl_l0ffn(nc, io, r1_kind, r2_kind):
    if "r1" not in io:
        io["r1"] = nc.dram_tensor("r1", [T, D], F32, kind=r1_kind).ap()
    io["wgT"] = nc.dram_tensor("wgT", [D, HF], I32, kind="ExternalInput").ap()
    io["wuT"] = nc.dram_tensor("wuT", [D, HF], I32, kind="ExternalInput").ap()
    io["wdT"] = nc.dram_tensor("wdT", [HF, D], I32, kind="ExternalInput").ap()
    io["r2"] = nc.dram_tensor("r2", [T, D], F32, kind=r2_kind).ap()
    return io


def emit_l0ffn(nc, S, C, io):
    r1_in = io["r1"]; wgT = io["wgT"]; wuT = io["wuT"]; wdT = io["wdT"]
    r2_out = io["r2"]

    TG = 256
    NG = T // TG

    S.pe_auto_nop = False
    with ExitStack() as stack:
        E = stack.enter_context
        wg_sb = [E(nc.sbuf_tensor(f"wg_sb{d}", [128, HF], BF16)) for d in range(8)]
        wu_sb = [E(nc.sbuf_tensor(f"wu_sb{d}", [128, HF], BF16)) for d in range(8)]
        wd_sb = [E(nc.sbuf_tensor(f"wd_sb{h}", [128, D], BF16)) for h in range(24)]
        r1_tok = [E(nc.sbuf_tensor(f"r1_tok{j}", [128, D], F32)) for j in range(2)]
        r1bf = [E(nc.sbuf_tensor(f"r1bf{j}", [128, D], BF16)) for j in range(2)]
        r1T = [E(nc.sbuf_tensor(f"r1T{d}", [128, TG], BF16)) for d in range(8)]
        g_s = E(nc.sbuf_tensor("g_s", [128, TG], F32))
        u_s = E(nc.sbuf_tensor("u_s", [128, TG], F32))
        h_sb = [E(nc.sbuf_tensor(f"h_sb{f}", [128, TG], BF16)) for f in range(24)]
        r2_t = E(nc.sbuf_tensor("r2_t", [128, D], F32))
        ln_sq = E(nc.sbuf_tensor("ln_sq", [128, D], F32))
        ln_cols = E(nc.sbuf_tensor("ln_cols", [128, 4], F32))
        ps_tr = E(nc.psum_tensor("ps_tr", [128, TG], BF16))
        ps_g = E(nc.psum_tensor("ps_g", [128, TG], F32))
        ps_u = E(nc.psum_tensor("ps_u", [128, TG], F32))
        ps_d = [E(nc.psum_tensor(f"ps_d{i}", [128, 512], F32)) for i in range(2)]

        for d in range(8):
            S.op("gpsimd", lambda e, d=d: e.dma_start(
                out=wg_sb[d][:], in_=wgT[d * 128:(d + 1) * 128, :]),
                writes=[f"wg{d}"], queue="qwg")
            S.op("gpsimd", lambda e, d=d: e.dma_start(
                out=wu_sb[d][:], in_=wuT[d * 128:(d + 1) * 128, :]),
                writes=[f"wu{d}"], queue="qwu")
        for hh in range(24):
            S.op("gpsimd", lambda e, hh=hh: e.dma_start(
                out=wd_sb[hh][:], in_=wdT[hh * 128:(hh + 1) * 128, :]),
                writes=[f"wd{hh}"], queue="qwd")

        for g in range(NG):
            t0 = g * TG
            for j in range(2):
                S.op("sync", lambda e, j=j, t0=t0: e.dma_start(
                    out=r1_tok[j][:], in_=r1_in[t0 + j * 128: t0 + (j + 1) * 128, :]),
                    writes=[f"r1_tok{j}"], queue=f"qr1{j}")
                S.op("scalar", lambda e, j=j: e.copy(out=r1bf[j][:], in_=r1_tok[j][:]),
                     reads=[f"r1_tok{j}"], writes=[f"r1bf{j}"])
            for d in range(8):
                def tr(e, d=d):
                    for j in range(2):
                        last = e.transpose(
                            ps_tr[:, j * 128:(j + 1) * 128],
                            r1bf[j][:, d * 128:(d + 1) * 128], C["ident"][:])
                    return last
                S.op("tensor", tr,
                     reads=[f"r1bf{j}" for j in range(2)] + ["ident"],
                     writes=["ps_tr"])
                S.op("scalar", lambda e, d=d: e.copy(out=r1T[d][:], in_=ps_tr[:]),
                     reads=["ps_tr"], writes=[f"r1T{d}"])

            for f in range(24):
                def mmg(e, f=f):
                    for d in range(8):
                        last = e.matmul(ps_g[:], wg_sb[d][:, f * 128:(f + 1) * 128],
                                        r1T[d][:], start=(d == 0), stop=(d == 7))
                    return last
                S.op("tensor", mmg,
                     reads=[f"wg{d}" for d in range(8)] + [f"r1T{d}" for d in range(8)],
                     writes=["ps_g"])

                def mmu(e, f=f):
                    for d in range(8):
                        last = e.matmul(ps_u[:], wu_sb[d][:, f * 128:(f + 1) * 128],
                                        r1T[d][:], start=(d == 0), stop=(d == 7))
                    return last
                S.op("tensor", mmu,
                     reads=[f"wu{d}" for d in range(8)] + [f"r1T{d}" for d in range(8)],
                     writes=["ps_u"])
                S.op("scalar", lambda e: e.activation(
                    out=g_s[:], in_=ps_g[:], func=AF.Silu, scale=DEQ),
                    reads=["ps_g"], writes=["g_s"])
                S.op("scalar", lambda e: e.activation(
                    out=u_s[:], in_=ps_u[:], func=AF.Copy, scale=DEQ),
                    reads=["ps_u"], writes=["u_s"])
                S.op("vector", lambda e, f=f: e.tensor_tensor(
                    out=h_sb[f][:], in0=g_s[:], in1=u_s[:], op=ALU.mult),
                    reads=["g_s", "u_s"], writes=[f"h{f}"])

            for j in range(2):
                for ch in range(2):
                    def mmd(e, j=j, ch=ch):
                        for hh in range(24):
                            last = e.matmul(
                                ps_d[ch][:], h_sb[hh][:, j * 128:(j + 1) * 128],
                                wd_sb[hh][:, ch * 512:(ch + 1) * 512],
                                start=(hh == 0), stop=(hh == 23))
                        return last
                    S.op("tensor", mmd,
                         reads=[f"h{hh}" for hh in range(24)] + [f"wd{hh}" for hh in range(24)],
                         writes=[f"ps_d{ch}"])
                    S.op("vector", lambda e, j=j, ch=ch: e.scalar_tensor_tensor(
                        out=r2_t[:, ch * 512:(ch + 1) * 512], in0=ps_d[ch][:],
                        scalar=DEQ, in1=r1_tok[j][:, ch * 512:(ch + 1) * 512],
                        op0=ALU.mult, op1=ALU.add),
                        reads=[f"ps_d{ch}", f"r1_tok{j}"], writes=["r2_t"])
                emit_ln(nc, S, "r2_t", r2_t[:],
                        dict(sq=ln_sq, cols=ln_cols), f"ln{g}_{j}")
                S.op("sync", lambda e, j=j, t0=t0: e.dma_start(
                    out=r2_out[t0 + j * 128: t0 + (j + 1) * 128, :], in_=r2_t[:]),
                    reads=["r2_t"], queue="dso")

        S.barrier()
        S.flush()
        S.pe_auto_nop = True


def build_l0ffn():
    nc = bass.Bass("TRN2", target_bir_lowering=False, debug=False)
    io = decl_l0ffn(nc, {}, "ExternalInput", "ExternalOutput")
    with ExitStack() as stack:
        S = Sched(nc, stack)
        C = emit_consts(nc, S, stack)
        emit_l0ffn(nc, S, C, io)
        S.finalize()
    return nc


def prep_l0ffn(ins, core, r1):
    return {
        "r1": r1,
        "wgT": np.ascontiguousarray(ins["L0_ffn_w_gate"].T),
        "wuT": np.ascontiguousarray(ins["L0_ffn_w_up"].T),
        "wdT": np.ascontiguousarray(ins["L0_ffn_w_down"].T),
    }


def perm_runs():
    g = int(math.ceil(math.sqrt(T)))
    grid = np.arange(g * g).reshape(g, g)
    grid[1::2] = grid[1::2, ::-1]
    flat = grid.reshape(-1)
    flat = flat[flat < T][:T]
    runs = []
    i = 0
    while i < len(flat):
        j = i + 1
        step = 0
        while j < len(flat) and flat[j] - flat[j - 1] in (1, -1) and \
                (step == 0 or flat[j] - flat[j - 1] == step):
            step = flat[j] - flat[j - 1]
            j += 1
        runs.append((int(flat[i]), int(step) if step else 1, j - i))
        i = j
    return flat, runs


def decl_l1attn(nc, io, r2_kind, r3_kind):
    if "r2" not in io:
        io["r2"] = nc.dram_tensor("r2", [T, D], F32, kind=r2_kind).ap()
    io["wqkvT"] = nc.dram_tensor("wqkvT", [D, 3 * D], F32, kind="ExternalInput").ap()
    io["woT"] = nc.dram_tensor("woT", [D, D], F32, kind="ExternalInput").ap()
    io["r3"] = nc.dram_tensor("r3", [T, D], F32, kind=r3_kind).ap()
    return io


def emit_l1attn(nc, S, C, io, stop=99):
    r2_in = io["r2"]; wqkvT = io["wqkvT"]; woT = io["woT"]; r3_out = io["r3"]
    s_xp = nc.dram_tensor("s_xp", [T, D], F32, kind="Internal").ap()

    TG = 512
    NG = T // TG

    with ExitStack() as stack:
        E = stack.enter_context
        wq_sb = [E(nc.sbuf_tensor(f"wq_sb{d}", [128, 3 * D], BF16)) for d in range(8)]
        wo_sb = [E(nc.sbuf_tensor(f"wo_sb{d}", [128, D], BF16)) for d in range(8)]
        xp_tok = [E(nc.sbuf_tensor(f"xp_tok{j}", [128, D], F32)) for j in range(4)]
        xpbf = [E(nc.sbuf_tensor(f"xpbf{j}", [128, D], BF16)) for j in range(4)]
        xpT = [E(nc.sbuf_tensor(f"xpT{d}", [128, TG], BF16)) for d in range(8)]
        qkvT = [E(nc.sbuf_tensor(f"qkvT{f}", [128, TG], BF16)) for f in range(24)]
        sc_sb = E(nc.sbuf_tensor("sc_sb", [128, H * HD], F32))
        nmax = E(nc.sbuf_tensor("nmax", [128, H], F32))
        esum = E(nc.sbuf_tensor("esum", [128, H], F32))
        attn_bf = E(nc.sbuf_tensor("attn_bf", [128, H * HD], BF16))
        attnT = E(nc.sbuf_tensor("attnT", [64, 16 * 128], BF16))
        vd_sb = [E(nc.sbuf_tensor(f"vd_sb{s}", [64, 16 * 64], BF16)) for s in range(4)]
        oT = E(nc.sbuf_tensor("oT", [128, 8 * 128], BF16))
        x_res = E(nc.sbuf_tensor("x_res", [128, D], F32))
        r3_t = E(nc.sbuf_tensor("r3_t", [128, D], F32))
        ln_sq = E(nc.sbuf_tensor("ln_sq", [128, D], F32))
        ln_cols = E(nc.sbuf_tensor("ln_cols", [128, 4], F32))
        ps_tr = E(nc.psum_tensor("ps_tr", [128, TG], BF16))
        ps_q = E(nc.psum_tensor("ps_q", [128, TG], F32))
        ps_sc = E(nc.psum_tensor("ps_sc", [128, H * HD], F32))
        ps_ot = E(nc.psum_tensor("ps_ot", [128, 8 * 128], F32))
        ps_o = [E(nc.psum_tensor(f"ps_o{i}", [128, 512], F32)) for i in range(2)]

        for d in range(8):
            S.op("gpsimd", lambda e, d=d: e.dma_start(
                out=wq_sb[d][:], in_=wqkvT[d * 128:(d + 1) * 128, :]),
                writes=[f"wq{d}"], queue="qwq")
            S.op("gpsimd", lambda e, d=d: e.dma_start(
                out=wo_sb[d][:], in_=woT[d * 128:(d + 1) * 128, :]),
                writes=[f"wo{d}"], queue="qwo")

        G46 = 46
        nfull = 22
        S.op("sync", lambda e: e.dma_start(
            out=bass.AP(s_xp.tensor, 0, [[2 * G46 * D, nfull], [1, G46 * D]]),
            in_=bass.AP(r2_in.tensor, 0, [[2 * G46 * D, nfull], [1, G46 * D]])),
            writes=["s_xp"], queue="qgather")
        S.op("sync", lambda e: e.dma_start(
            out=bass.AP(s_xp.tensor, G46 * D, [[2 * G46 * D, nfull], [D, G46], [1, D]]),
            in_=bass.AP(r2_in.tensor, (2 * G46 - 1) * D,
                        [[2 * G46 * D, nfull], [-D, G46], [1, D]])),
            writes=["s_xp"], queue="qgather")
        S.op("sync", lambda e: e.dma_start(
            out=bass.AP(s_xp.tensor, 44 * G46 * D, [[1, (T - 44 * G46) * D]]),
            in_=bass.AP(r2_in.tensor, 44 * G46 * D, [[1, (T - 44 * G46) * D]])),
            writes=["s_xp"], queue="qgather")

        for g in range(NG):
            t0 = g * TG
            for j in range(4):
                S.op("sync", lambda e, j=j, t0=t0: e.dma_start(
                    out=xp_tok[j][:], in_=s_xp[t0 + j * 128: t0 + (j + 1) * 128, :]),
                    reads=["s_xp"], writes=[f"xp_tok{j}"], queue=f"qxp{j}")
                S.op("scalar", lambda e, j=j: e.copy(out=xpbf[j][:], in_=xp_tok[j][:]),
                     reads=[f"xp_tok{j}"], writes=[f"xpbf{j}"])
            for d in range(8 if stop >= 2 else 0):
                def tr(e, d=d):
                    for j in range(4):
                        last = e.transpose(
                            ps_tr[:, j * 128:(j + 1) * 128],
                            xpbf[j][:, d * 128:(d + 1) * 128], C["ident"][:])
                    return last
                S.op("tensor", tr,
                     reads=[f"xpbf{j}" for j in range(4)] + ["ident"],
                     writes=["ps_tr"])
                S.op("scalar", lambda e, d=d: e.copy(out=xpT[d][:], in_=ps_tr[:]),
                     reads=["ps_tr"], writes=[f"xpT{d}"])

            for f in range(24 if stop >= 3 else 0):
                def mmq(e, f=f):
                    for d in range(8):
                        last = e.matmul(ps_q[:], wq_sb[d][:, f * 128:(f + 1) * 128],
                                        xpT[d][:], start=(d == 0), stop=(d == 7))
                    return last
                S.op("tensor", mmq,
                     reads=[f"wq{d}" for d in range(8)] + [f"xpT{d}" for d in range(8)],
                     writes=["ps_q"])
                S.op("scalar", lambda e, f=f: e.copy(out=qkvT[f][:], in_=ps_q[:]),
                     reads=["ps_q"], writes=[f"qkvT{f}"])

            qk = [f"qkvT{f}" for f in range(24)]
            for sg in range(4 if stop >= 4 else 0):
                s0 = sg * 128
                for par in (0, 1):
                    b = 64 * par

                    def trv(e, par=par, b=b, s0=s0):
                        for i in range(8):
                            h = 2 * i + par
                            vt = qkvT[16 + h // 2][b:b + 64, s0:s0 + 128:2]
                            last = e.transpose(
                                ps_tr[0:64, i * 64:i * 64 + 64],
                                vt, C["ident"][b:b + 64, b:b + 64])
                        return last
                    S.op("tensor", trv, reads=qk + ["ident"], writes=["ps_tr"])
                    S.op("scalar", lambda e, par=par, sg=sg: e.copy(
                        out=vd_sb[sg][0:64, :].rearrange(
                            "p (h k) -> p h k", k=64)[:, par::2, :],
                        in_=ps_tr[0:64, 0:512].rearrange(
                            "p (h k) -> p h k", k=64)),
                        reads=["ps_tr"], writes=[f"vd{sg}"])
            for sg in range(4):
                if stop < 4:
                    break
                s0 = sg * 128

                def mmsc(e, s0=s0):
                    for par in (0, 1):
                        if par:
                            e.nop()
                        for h in range(par, H, 2):
                            b = 64 * par
                            qt = qkvT[h // 2][b:b + 64, s0:s0 + 128]
                            kt = qkvT[8 + h // 2][b:b + 64, s0:s0 + 128:2]
                            last = e.matmul(ps_sc[:, h * 64:(h + 1) * 64], qt, kt,
                                            start=True, stop=True)
                    return last
                S.op("tensor", mmsc, reads=qk, writes=["ps_sc"])
                if stop < 5:
                    continue
                S.op("vector", lambda e: e.tensor_reduce(
                    out=nmax[:], in_=ps_sc.ap().rearrange("p (h k) -> p h k", k=HD),
                    axis=AX.X, op=ALU.max, negate=True),
                    reads=["ps_sc"], writes=["nmax"])
                S.op("vector", lambda e: e.tensor_tensor(
                    out=sc_sb.ap().rearrange("p (h k) -> p h k", k=HD),
                    in0=ps_sc.ap().rearrange("p (h k) -> p h k", k=HD),
                    in1=nmax[:, :, None].broadcast_to([128, H, HD]),
                    op=ALU.add),
                    reads=["ps_sc", "nmax"], writes=["sc"])
                S.op("scalar", lambda e: e.activation(
                    out=sc_sb[:], in_=sc_sb[:], func=AF.Exp, scale=QSCALE),
                    reads=["sc"], writes=["sc"])
                S.op("vector", lambda e: e.tensor_reduce(
                    out=esum[:], in_=sc_sb.ap().rearrange("p (h k) -> p h k", k=HD),
                    axis=AX.X, op=ALU.add),
                    reads=["sc"], writes=["esum"])
                S.op("vector", lambda e: e.reciprocal(out=esum[:], in_=esum[:]),
                     reads=["esum"], writes=["esum"])
                S.op("vector", lambda e: e.tensor_tensor(
                    out=attn_bf.ap().rearrange("p (h k) -> p h k", k=HD),
                    in0=sc_sb.ap().rearrange("p (h k) -> p h k", k=HD),
                    in1=esum[:, :, None].broadcast_to([128, H, HD]),
                    op=ALU.mult),
                    reads=["sc", "esum"], writes=["attn"])
                if stop < 6:
                    continue
                for hb in range(4):
                    def trb(e, hb=hb):
                        for hh in range(4):
                            h = hb * 4 + hh
                            last = e.transpose(
                                ps_tr[0:64, hh * 128:hh * 128 + 128],
                                attn_bf[:, h * 64:(h + 1) * 64], C["ident"][:])
                        return last
                    S.op("tensor", trb, reads=["attn", "ident"], writes=["ps_tr"])
                    S.op("scalar", lambda e, hb=hb: e.copy(
                        out=attnT[0:64, hb * 512:(hb + 1) * 512], in_=ps_tr[0:64, 0:512]),
                        reads=["ps_tr"], writes=["attnT"])
                if stop < 7:
                    continue

                def mmo(e, sg=sg):
                    for h in range(H):
                        last = e.matmul(
                            ps_ot[64 * (h % 2):64 * (h % 2) + 64,
                                  (h // 2) * 128:(h // 2) * 128 + 128],
                            vd_sb[sg][0:64, h * 64:(h + 1) * 64],
                            attnT[0:64, h * 128:(h + 1) * 128],
                            start=True, stop=True)
                    return last
                S.op("tensor", mmo, reads=[f"vd{sg}", "attnT"], writes=["ps_ot"])
                S.op("scalar", lambda e: e.copy(out=oT[:], in_=ps_ot[:]),
                     reads=["ps_ot"], writes=["oT"])

                if stop < 8:
                    continue
                tseg = t0 + s0
                S.op("sync", lambda e, tseg=tseg: e.dma_start(
                    out=x_res[:], in_=r2_in[tseg:tseg + 128, :]),
                    writes=["x_res"], queue="qxr")
                for ch in range(2):
                    def mmop(e, ch=ch):
                        for d in range(8):
                            last = e.matmul(
                                ps_o[ch][:], oT[:, d * 128:(d + 1) * 128],
                                wo_sb[d][:, ch * 512:(ch + 1) * 512],
                                start=(d == 0), stop=(d == 7))
                        return last
                    S.op("tensor", mmop,
                         reads=["oT"] + [f"wo{d}" for d in range(8)],
                         writes=[f"ps_o{ch}"])
                    S.op("vector", lambda e, ch=ch: e.tensor_add(
                        r3_t[:, ch * 512:(ch + 1) * 512], ps_o[ch][:],
                        x_res[:, ch * 512:(ch + 1) * 512]),
                        reads=[f"ps_o{ch}", "x_res"], writes=["r3_t"])
                emit_ln(nc, S, "r3_t", r3_t[:],
                        dict(sq=ln_sq, cols=ln_cols), f"ln{g}_{sg}")
                S.op("sync", lambda e, tseg=tseg: e.dma_start(
                    out=r3_out[tseg:tseg + 128, :], in_=r3_t[:]),
                    reads=["r3_t"], queue="dso")

        S.barrier()
        S.flush()


def build_l1attn(stop=99):
    nc = bass.Bass("TRN2", target_bir_lowering=False, debug=False)
    io = decl_l1attn(nc, {}, "ExternalInput", "ExternalOutput")
    with ExitStack() as stack:
        S = Sched(nc, stack)
        C = emit_consts(nc, S, stack)
        emit_l1attn(nc, S, C, io, stop=stop)
        S.finalize()
    return nc


def prep_l1attn(ins, core, r2):
    return {
        "r2": r2,
        "wqkvT": np.ascontiguousarray(ins["L1_attn_w_qkv"].T),
        "woT": np.ascontiguousarray(ins["L1_attn_w_out"].T),
    }


def decl_l1moe(nc, io, r3_kind):
    if "r3" not in io:
        io["r3"] = nc.dram_tensor("r3", [T, D], F32, kind=r3_kind).ap()
    io["wgate"] = nc.dram_tensor("wgate", [D, NE], F32, kind="ExternalInput").ap()
    io["bgate"] = nc.dram_tensor("bgate", [1, NE], F32, kind="ExternalInput").ap()
    io["w1"] = nc.dram_tensor("w1", [NE, D, FE], F32, kind="ExternalInput").ap()
    io["b1"] = nc.dram_tensor("b1", [NE, 128, 16], F32, kind="ExternalInput").ap()
    io["w2"] = nc.dram_tensor("w2", [NE, FE, D], F32, kind="ExternalInput").ap()
    io["b2"] = nc.dram_tensor("b2", [NE, D], F32, kind="ExternalInput").ap()
    io["gam"] = nc.dram_tensor("gam", [1, D], F32, kind="ExternalInput").ap()
    io["bet"] = nc.dram_tensor("bet", [1, D], F32, kind="ExternalInput").ap()
    io["out"] = nc.dram_tensor("out", [T, D], F32, kind="ExternalOutput").ap()
    return io


def emit_l1moe(nc, S, C, io):
    r3_in = io["r3"]; wg_in = io["wgate"]; bg_in = io["bgate"]
    w1_in = io["w1"]; b1_in = io["b1"]; w2_in = io["w2"]; b2_in = io["b2"]
    gam_in = io["gam"]; bet_in = io["bet"]; out_t = io["out"]

    s_ym = nc.dram_tensor("s_ym", [T, D], F32, kind="Internal").ap()

    TG = 512
    NG = T // TG
    BIG = 1e30

    S.pe_auto_nop = False
    with ExitStack() as stack:
        E = stack.enter_context
        r3T = [E(nc.sbuf_tensor(f"r3T{d}", [128, T], BF16)) for d in range(8)]
        wgate_sb = E(nc.sbuf_tensor("wgate_sb", [128, 8 * NE], BF16))
        bgate_sb = E(nc.sbuf_tensor("bgate_sb", [1, NE], F32))
        wfull = [E(nc.sbuf_tensor(f"wfull{tt}", [128, NE], F32)) for tt in range(16)]
        wfT = E(nc.sbuf_tensor("wfT", [NE, 128], F32))
        w1_sb = [[E(nc.sbuf_tensor(f"w1_sb{sl}_{d}", [128, FE], BF16)) for d in range(8)]
                 for sl in range(2)]
        w2_one = [E(nc.sbuf_tensor(f"w2_sb{h}", [128, D], BF16)) for h in range(16)]
        w2_sb = [w2_one, w2_one]
        b1_sb = [E(nc.sbuf_tensor(f"b1_sb{sl}", [128, 16], F32)) for sl in range(2)]
        b2_sb = E(nc.sbuf_tensor("b2_sb", [NE, D], F32))
        h_sb = [E(nc.sbuf_tensor(f"h_sb{f}", [128, TG], BF16)) for f in range(16)]
        gam_b = E(nc.sbuf_tensor("gam_b", [128, D], F32))
        bet_b = E(nc.sbuf_tensor("bet_b", [128, D], F32))
        identf = E(nc.sbuf_tensor("identf", [128, 128], F32))
        lg = E(nc.sbuf_tensor("lg", [128, NE], F32))
        lcols = E(nc.sbuf_tensor("lcols", [128, 4], F32))
        mask = E(nc.sbuf_tensor("mask", [128, NE], F32))
        r3_tok = [E(nc.sbuf_tensor(f"r3_tok{j}", [128, D], F32)) for j in range(2)]
        r3bf = E(nc.sbuf_tensor("r3bf", [128, D], BF16))
        ytmp0 = E(nc.sbuf_tensor("ytmp0", [128, D], F32))
        y_t = E(nc.sbuf_tensor("y_t", [128, D], F32))
        z_t = E(nc.sbuf_tensor("z_t", [128, D], F32))
        ytbuf = [ytmp0, z_t]
        kbuf = ["ytmp0", "z_t"]
        ln_cols = E(nc.sbuf_tensor("ln_cols", [128, 4], F32))
        ps_tr = E(nc.psum_tensor("ps_tr", [128, 512], BF16))
        ps_trf = E(nc.psum_tensor("ps_trf", [128, 128], F32))
        ps_lg = E(nc.psum_tensor("ps_lg", [128, NE], F32))
        ps_h = [E(nc.psum_tensor(f"ps_h{i}", [128, TG], F32)) for i in range(3)]
        ps_eo = [E(nc.psum_tensor(f"ps_eo{i}", [128, 512], F32)) for i in range(2)]

        S.op("gpsimd", lambda e: e.memset(identf[:], 1.0), writes=["identf"])
        S.op("gpsimd", lambda e: e.affine_select(
            out=identf[:], in_=identf[:], pattern=[[-1, 128]],
            compare_op=ALU.is_equal, fill=0.0, base=0, channel_multiplier=1),
            reads=["identf"], writes=["identf"])

        S.op("gpsimd", lambda e: e.dma_start(
            out=wgate_sb.ap().rearrange("p (b m) -> p b m", m=NE),
            in_=wg_in.rearrange("(b p) m -> p b m", p=128)),
            writes=["wgate"], queue="qwgt")
        S.op("sync", lambda e: e.dma_start(out=bgate_sb[0:1, :], in_=bg_in),
             writes=["bgate"], queue="qbg")
        for src_in, dst, dkey in ((gam_in, gam_b, "gam_b"), (bet_in, bet_b, "bet_b")):
            S.op("sync", lambda e, src_in=src_in: e.dma_start(
                out=b2_sb[0:1, :], in_=src_in),
                reads=["ps_eo0", "ps_eo1"], writes=["brow"], queue="qbrow")
            for half in range(2):
                S.op("tensor", lambda e, half=half: e.matmul(
                    ps_eo[half][:], C["ones128"][0:1, :],
                    b2_sb[0:1, half * 512:(half + 1) * 512], start=True, stop=True),
                    reads=["brow", "ones128"], writes=[f"ps_eo{half}"])
                S.op("scalar", lambda e, half=half, dst=dst: e.copy(
                    out=dst[:, half * 512:(half + 1) * 512], in_=ps_eo[half][:]),
                    reads=[f"ps_eo{half}"], writes=[dkey])
        S.op("sync", lambda e: e.dma_start(out=b2_sb[:], in_=b2_in),
             reads=["brow", "ps_eo0", "ps_eo1"], writes=["b2", "brow"], queue="qb2")

        for tt in range(16):
            j = tt % 2
            S.op("sync", lambda e, tt=tt, j=j: e.dma_start(
                out=r3_tok[j][:], in_=r3_in[tt * 128:(tt + 1) * 128, :]),
                writes=[f"r3_tok{j}"], queue=f"qr3{j}")
            S.op("scalar", lambda e, j=j: e.copy(out=r3bf[:], in_=r3_tok[j][:]),
                 reads=[f"r3_tok{j}"], writes=["r3bf"])
            for d in range(8):
                S.op("tensor", lambda e, d=d: e.transpose(
                    ps_tr[:, 0:128], r3bf[:, d * 128:(d + 1) * 128], C["ident"][:]),
                    reads=["r3bf", "ident"], writes=["ps_tr"])
                S.op("scalar", lambda e, d=d, tt=tt: e.copy(
                    out=r3T[d][:, tt * 128:(tt + 1) * 128], in_=ps_tr[:, 0:128]),
                    reads=["ps_tr"], writes=[f"r3T{d}"])

            def mmlg(e, tt=tt):
                for d in range(8):
                    e.matmul(ps_lg[:], r3T[d][:, tt * 128:(tt + 1) * 128],
                             wgate_sb[:, d * NE:(d + 1) * NE],
                             start=(d == 0), stop=False)
                return e.matmul(ps_lg[:], C["ones128"][0:1, 0:128],
                                bgate_sb[0:1, :], start=False, stop=True)
            S.op("tensor", mmlg,
                 reads=[f"r3T{d}" for d in range(8)] + ["wgate", "bgate", "ones128"],
                 writes=["ps_lg"])
            S.op("scalar", lambda e: e.copy(out=lg[:], in_=ps_lg[:]),
                 reads=["ps_lg"], writes=["lg"])
            S.op("vector", lambda e: e.reduce_max(
                out=lcols[:, 0:1], in_=lg[:], axis=AX.X),
                reads=["lg"], writes=["m1"])
            S.op("vector", lambda e: e.tensor_scalar(
                out=mask[:], in0=lg[:], scalar1=lcols[:, 0:1], scalar2=None,
                op0=ALU.is_ge),
                reads=["lg", "m1"], writes=["mask"])
            S.op("vector", lambda e: e.scalar_tensor_tensor(
                out=mask[:], in0=mask[:], scalar=-BIG, in1=lg[:],
                op0=ALU.mult, op1=ALU.add),
                reads=["mask", "lg"], writes=["mask"])
            S.op("vector", lambda e: e.reduce_max(
                out=lcols[:, 1:2], in_=mask[:], axis=AX.X),
                reads=["mask"], writes=["m2"])
            S.op("vector", lambda e: e.tensor_scalar(
                out=lcols[:, 2:3], in0=lcols[:, 0:1], scalar1=-1.0, scalar2=None,
                op0=ALU.mult),
                reads=["m1"], writes=["nm1"])
            S.op("scalar", lambda e: e.activation(
                out=lg[:], in_=lg[:], func=AF.Exp, bias=lcols[:, 2:3]),
                reads=["lg", "nm1"], writes=["lg"])
            S.op("scalar", lambda e: e.activation(
                out=lcols[:, 3:4], in_=lcols[:, 1:2], func=AF.Exp, bias=lcols[:, 2:3]),
                reads=["m2", "nm1"], writes=["em2"])
            S.op("vector", lambda e: e.tensor_scalar(
                out=mask[:], in0=lg[:], scalar1=lcols[:, 3:4], scalar2=None,
                op0=ALU.is_ge),
                reads=["lg", "em2"], writes=["mask"])
            S.op("vector", lambda e: e.tensor_tensor(
                out=lg[:], in0=lg[:], in1=mask[:], op=ALU.mult),
                reads=["lg", "mask"], writes=["lg"])
            S.op("vector", lambda e: e.tensor_scalar(
                out=lcols[:, 3:4], in0=lcols[:, 3:4], scalar1=1.0, scalar2=None,
                op0=ALU.add),
                reads=["em2"], writes=["em2"])
            S.op("vector", lambda e: e.reciprocal(
                out=lcols[:, 3:4], in_=lcols[:, 3:4]),
                reads=["em2"], writes=["em2"])
            S.op("vector", lambda e, tt=tt: e.tensor_scalar(
                out=wfull[tt][:], in0=lg[:], scalar1=lcols[:, 3:4], scalar2=None,
                op0=ALU.mult),
                reads=["lg", "em2"], writes=[f"wfull{tt}"])
            S.op("tensor", lambda e, tt=tt: e.transpose(
                ps_trf[0:NE, 0:128], wfull[tt][:], identf[:]),
                reads=[f"wfull{tt}", "identf"], writes=["ps_trf"])
            S.op("scalar", lambda e: e.copy(out=wfT[:], in_=ps_trf[0:NE, 0:128]),
                 reads=["ps_trf"], writes=["wfT"])
            i = tt % 2
            for ch in range(2):
                S.op("tensor", lambda e, ch=ch: e.matmul(
                    ps_eo[ch][:], wfT[:], b2_sb[:, ch * 512:(ch + 1) * 512],
                    start=True, stop=True),
                    reads=["wfT", "b2"], writes=[f"ps_eo{ch}"])
                S.op("scalar", lambda e, ch=ch, i=i: e.copy(
                    out=ytbuf[i][:, ch * 512:(ch + 1) * 512], in_=ps_eo[ch][:]),
                    reads=[f"ps_eo{ch}"], writes=[kbuf[i]])
            S.op("gpsimd", lambda e, tt=tt, i=i: e.dma_start(
                out=s_ym[tt * 128:(tt + 1) * 128, :], in_=ytbuf[i][:]),
                reads=[kbuf[i]], writes=[f"sym{tt}"], queue=f"qym{tt % 4}")

        for ex in range(NE):
            sl = ex % 2
            for d in range(8):
                S.op("gpsimd", lambda e, ex=ex, d=d, sl=sl: e.dma_start(
                    out=w1_sb[sl][d][:], in_=w1_in[ex, d * 128:(d + 1) * 128, :]),
                    writes=[f"w1_{sl}_{d}"], queue=f"qw1{sl}")
            for hh in range(16):
                S.op("gpsimd", lambda e, ex=ex, hh=hh, sl=sl: e.dma_start(
                    out=w2_sb[sl][hh][:], in_=w2_in[ex, hh * 128:(hh + 1) * 128, :]),
                    writes=[f"w2_{hh}"], queue="qw2")
            S.op("sync", lambda e, ex=ex, sl=sl: e.dma_start(
                out=b1_sb[sl][:], in_=b1_in[ex]),
                writes=[f"b1{sl}"], queue=f"qb1{sl}")
            for g in range(NG):
                t0 = g * TG
                for f in range(16):
                    pi = f % 3

                    def mmh(e, f=f, t0=t0, pi=pi, sl=sl):
                        for d in range(8):
                            last = e.matmul(ps_h[pi][:],
                                            w1_sb[sl][d][:, f * 128:(f + 1) * 128],
                                            r3T[d][:, t0:t0 + TG], start=(d == 0),
                                            stop=(d == 7))
                        return last
                    S.op("tensor", mmh,
                         reads=[f"w1_{sl}_{d}" for d in range(8)] +
                               [f"r3T{d}" for d in range(8)],
                         writes=[f"ps_h{pi}"])
                    S.op("scalar", lambda e, f=f, pi=pi, sl=sl: e.activation(
                        out=h_sb[f][:], in_=ps_h[pi][:], func=AF.Relu,
                        bias=b1_sb[sl][:, f:f + 1]),
                        reads=[f"ps_h{pi}", f"b1{sl}"], writes=[f"h{f}"])
                for j in range(4):
                    tt = g * 4 + j
                    i = j % 2
                    for ch in range(2):
                        pi = ch

                        def mme(e, j=j, ch=ch, pi=pi, sl=sl):
                            for hh in range(16):
                                last = e.matmul(ps_eo[pi][:],
                                                h_sb[hh][:, j * 128:(j + 1) * 128],
                                                w2_sb[sl][hh][:, ch * 512:(ch + 1) * 512],
                                                start=(hh == 0), stop=(hh == 15))
                            return last
                        S.op("tensor", mme,
                             reads=[f"h{hh}" for hh in range(16)] +
                                   [f"w2_{hh}" for hh in range(16)],
                             writes=[f"ps_eo{pi}"])
                        S.op("vector", lambda e, tt=tt, ch=ch, pi=pi, i=i, ex=ex: e.tensor_scalar(
                            out=ytbuf[i][:, ch * 512:(ch + 1) * 512],
                            in0=ps_eo[pi][:], scalar1=wfull[tt][:, ex:ex + 1],
                            scalar2=None, op0=ALU.mult),
                            reads=[f"ps_eo{pi}", f"wfull{tt}"],
                            writes=[kbuf[i]])
                    S.op("gpsimd", lambda e, tt=tt, i=i: e.dma_start(
                        out=s_ym[tt * 128:(tt + 1) * 128, :], in_=ytbuf[i][:],
                        accum_op=ALU.add),
                        reads=[kbuf[i]], writes=[f"sym{tt}"], queue=f"qym{tt % 4}")

        for tt in range(16):
            j = tt % 2
            S.op("sync", lambda e, tt=tt, j=j: e.dma_start(
                out=r3_tok[j][:], in_=r3_in[tt * 128:(tt + 1) * 128, :]),
                writes=[f"r3_tok{j}"], queue=f"qr3{j}")
            S.op("sync", lambda e, tt=tt, j=j: e.dma_start(
                out=y_t[:], in_=s_ym[tt * 128:(tt + 1) * 128, :]),
                reads=[f"sym{tt}"], writes=["y_t"], queue="qyl")
            S.op("vector", lambda e, tt=tt, j=j: e.tensor_add(
                y_t[:], y_t[:], r3_tok[j][:]),
                reads=["y_t", f"r3_tok{j}"], writes=["y_t"])
            emit_ln(nc, S, "y_t", y_t[:], dict(sq=z_t, cols=ln_cols), f"iln{tt}",
                    sq_key="z_t")
            S.op("vector", lambda e: e.tensor_tensor(
                out=y_t[:], in0=y_t[:], in1=gam_b[:], op=ALU.mult),
                reads=["y_t", "gam_b"], writes=["y_t"])
            S.op("vector", lambda e: e.tensor_add(y_t[:], y_t[:], bet_b[:]),
                 reads=["y_t", "bet_b"], writes=["y_t"])
            S.op("vector", lambda e, j=j: e.tensor_add(z_t[:], y_t[:], r3_tok[j][:]),
                 reads=["y_t", f"r3_tok{j}"], writes=["z_t"])
            emit_ln(nc, S, "z_t", z_t[:], dict(sq=y_t, cols=ln_cols), f"oln{tt}",
                    sq_key="y_t")
            S.op("sync", lambda e, tt=tt: e.dma_start(
                out=out_t[tt * 128:(tt + 1) * 128, :], in_=z_t[:]),
                reads=["z_t"], queue="dso")

        S.barrier()
        S.flush()
        S.pe_auto_nop = True


def build_l1moe():
    nc = bass.Bass("TRN2", target_bir_lowering=False, debug=False)
    io = decl_l1moe(nc, {}, "ExternalInput")
    with ExitStack() as stack:
        S = Sched(nc, stack)
        C = emit_consts(nc, S, stack)
        emit_l1moe(nc, S, C, io)
        S.finalize()
    return nc


def prep_l1moe(ins, core, r3):
    return {
        "r3": r3,
        "wgate": ins["L1_ffn_Wg"],
        "bgate": ins["L1_ffn_bg"].reshape(1, NE),
        "w1": ins["L1_ffn_W1"],
        "b1": np.ascontiguousarray(
            ins["L1_ffn_b1"].reshape(NE, 16, 128).transpose(0, 2, 1)),
        "w2": ins["L1_ffn_W2"],
        "b2": ins["L1_ffn_b2"],
        "gam": ins["L1_ffn_gamma"].reshape(1, D),
        "bet": ins["L1_ffn_beta"].reshape(1, D),
    }


class _Prefix:

    def __init__(self, nc, p):
        self._nc = nc
        self._p = p

    def sbuf_tensor(self, name, *a, **k):
        return self._nc.sbuf_tensor(self._p + name, *a, **k)

    def psum_tensor(self, name, *a, **k):
        return self._nc.psum_tensor(self._p + name, *a, **k)

    def dram_tensor(self, name, *a, **k):
        return self._nc.dram_tensor(self._p + name, *a, **k)

    def __getattr__(self, k):
        return getattr(self._nc, k)


def build_fused():
    nc = bass.Bass("TRN2", target_bir_lowering=False, debug=False)
    io = decl_l0attn(nc, "Internal")
    decl_l0ffn(nc, io, None, "Internal")
    decl_l1attn(nc, io, None, "Internal")
    decl_l1moe(nc, io, None)
    with ExitStack() as stack:
        S = Sched(nc, stack)
        C = emit_consts(nc, S, stack)
        emit_l0attn(_Prefix(nc, "a_"), S, C, io)
        with nc.named_scope("l0ffn"):
            emit_l0ffn(_Prefix(nc, "b_"), S, C, io)
        with nc.named_scope("l1attn"):
            emit_l1attn(_Prefix(nc, "c_"), S, C, io)
        with nc.named_scope("l1moe"):
            emit_l1moe(_Prefix(nc, "d_"), S, C, io)
        S.finalize()
    return nc


def prep_fused(ins, core):
    m = prep_l0attn(ins, core)
    m.update({k: v for k, v in prep_l0ffn(ins, core, None).items() if k != "r1"})
    m.update({k: v for k, v in prep_l1attn(ins, core, None).items() if k != "r2"})
    m.update({k: v for k, v in prep_l1moe(ins, core, None).items() if k != "r3"})
    return m


def kernel(**ins):
    cores = list(range(NCORE))
    nc = build_fused()
    res = run_bass_kernel_spmd(nc, [prep_fused(ins, c) for c in cores], core_ids=cores)
    out = [np.asarray(res.results[c]["out"]) for c in cores]
    return np.stack(out, axis=0)


def kernel_unfused(**ins):
    cores = list(range(NCORE))

    nc1 = build_l0attn()
    res1 = run_bass_kernel_spmd(nc1, [prep_l0attn(ins, c) for c in cores], core_ids=cores)
    r1 = [np.asarray(res1.results[c]["r1"]) for c in cores]

    nc2 = build_l0ffn()
    res2 = run_bass_kernel_spmd(nc2, [prep_l0ffn(ins, c, r1[c]) for c in cores], core_ids=cores)
    r2 = [np.asarray(res2.results[c]["r2"]) for c in cores]

    nc3 = build_l1attn()
    res3 = run_bass_kernel_spmd(nc3, [prep_l1attn(ins, c, r2[c]) for c in cores], core_ids=cores)
    r3 = [np.asarray(res3.results[c]["r3"]) for c in cores]

    nc4 = build_l1moe()
    res4 = run_bass_kernel_spmd(nc4, [prep_l1moe(ins, c, r3[c]) for c in cores], core_ids=cores)
    out = [np.asarray(res4.results[c]["out"]) for c in cores]

    return np.stack(out, axis=0)


# version 18
# speedup vs baseline: 1.2217x; 1.0445x over previous
import math
import numpy as np
from contextlib import ExitStack
import concourse.bass as bass
import concourse.mybir as mybir
from concourse.bass_utils import run_bass_kernel_spmd

F32 = mybir.dt.float32
BF16 = mybir.dt.bfloat16
I32 = mybir.dt.int32
AF = mybir.ActivationFunctionType
ALU = mybir.AluOpType
AX = mybir.AxisListType

NCORE = 8
T = 2048
D = 1024
DI = 2048
NS = 16
DTR = 64
HF = 3072
NE = 8
FE = 2048
H = 16
HD = 64
LN_EPS = 1e-5
QSCALE = float(HD) ** -0.5
DEQ = float(np.float32(1.0) / np.float32(127.0))


class Sched:
    ENGS = ("tensor", "vector", "scalar", "gpsimd", "sync")

    def __init__(self, nc, stack):
        self.nc = nc
        self.stack = stack
        self.plan = {e: [] for e in self.ENGS}
        self.sems = {}
        self.counts = {}
        self.wm = {e: {} for e in self.ENGS}
        self.bufs = {}
        for e in self.ENGS:
            self._sem(e)

    def _sem(self, name):
        if name not in self.sems:
            self.sems[name] = self.stack.enter_context(self.nc.semaphore("s_" + name))
            self.counts[name] = 0
        return self.sems[name]

    def op(self, eng, fn, reads=(), writes=(), queue=None):
        semkey = queue or eng
        self._sem(semkey)
        inc = 16 if queue else 1
        waits = {}
        pe_auto_nop = getattr(self, "pe_auto_nop", True)

        def add(tok):
            if tok is None:
                return
            s, v = tok
            if self.wm[eng].get(s, 0) < v and waits.get(s, 0) < v:
                waits[s] = v

        for k in reads:
            st = self.bufs.get(k)
            if st:
                add(st["w"])
        for k in writes:
            st = self.bufs.get(k)
            if st:
                add(st["w"])
                for r in st["r"]:
                    add(r)

        for s in list(waits):
            if s not in self.ENGS:
                if self.counts[s] > waits[s]:
                    waits[s] = self.counts[s]
                if self.wm[eng].get(s, 0) >= waits[s]:
                    del waits[s]

        self.counts[semkey] += inc
        tok = (semkey, self.counts[semkey])
        wlist = sorted(waits.items())
        sems = self.sems

        pe_nop = (eng == "tensor" and queue is None and pe_auto_nop)

        def thunk(engine, wlist=wlist, fn=fn, semkey=semkey, inc=inc, pe_nop=pe_nop):
            for s, v in wlist:
                engine.wait_ge(sems[s], v)
            if pe_nop:
                engine.nop()
            fn(engine).then_inc(sems[semkey], inc)

        self.plan[eng].append(thunk)
        for s, v in waits.items():
            self.wm[eng][s] = max(self.wm[eng].get(s, 0), v)
        for k in writes:
            self.bufs[k] = {"w": tok, "r": []}
        for k in reads:
            self.bufs.setdefault(k, {"w": None, "r": []})["r"].append(tok)
        return tok

    def barrier(self):
        tgt = dict(self.counts)
        sems = self.sems
        for e in self.ENGS:
            wl = [(s, v) for s, v in sorted(tgt.items())
                  if v > 0 and self.wm[e].get(s, 0) < v]

            def thunk(engine, wl=wl):
                for s, v in wl:
                    engine.wait_ge(sems[s], v)
            self.plan[e].append(thunk)
            for s, v in tgt.items():
                self.wm[e][s] = max(self.wm[e].get(s, 0), v)
        self.bufs = {}

    def flush(self):
        def mk(plan):
            def closure(engine):
                for th in plan:
                    th(engine)
            return closure
        if any(self.plan[e] for e in self.ENGS):
            with self.nc.Block() as block:
                for e in self.ENGS:
                    if self.plan[e]:
                        getattr(block, e)(mk(self.plan[e]))
        self.plan = {e: [] for e in self.ENGS}

    def finalize(self):
        tgt = dict(self.counts)
        sems = self.sems

        def final_thunk(engine):
            for s, v in sorted(tgt.items()):
                if v > 0:
                    engine.wait_ge(sems[s], v)
        self.plan["sync"].append(final_thunk)
        self.flush()


def emit_consts(nc, S, stack):
    E = stack.enter_context
    ident = E(nc.sbuf_tensor("c_ident", [128, 128], BF16))
    ones128 = E(nc.sbuf_tensor("c_ones128", [1, 128], F32))

    S.op("gpsimd", lambda e: e.memset(ident[:], 1.0), writes=["ident"])
    S.op("gpsimd", lambda e: e.affine_select(
        out=ident[:], in_=ident[:], pattern=[[-1, 128]],
        compare_op=ALU.is_equal, fill=0.0, base=0, channel_multiplier=1),
        reads=["ident"], writes=["ident"])
    S.op("gpsimd", lambda e: e.memset(ones128[0:1, :], 1.0), writes=["ones128"])
    ceps = E(nc.sbuf_tensor("c_eps", [128, 1], F32))
    S.op("gpsimd", lambda e: e.memset(ceps[:], LN_EPS), writes=["ceps"])
    nc.const_aps.register(F32, LN_EPS, ceps[:, 0:1])
    S.barrier()
    return dict(ident=ident, ones128=ones128)


def emit_sel(nc, S, E):
    sel = E(nc.sbuf_tensor("c_sel", [NS, NS * 128], F32))
    S.op("gpsimd", lambda e: e.memset(sel[:], 1.0), writes=["sel"])
    S.op("gpsimd", lambda e: e.affine_select(
        out=sel[:], in_=sel[:], pattern=[[1, NS * 128]],
        compare_op=ALU.is_ge, fill=0.0, base=0, channel_multiplier=-128),
        reads=["sel"], writes=["sel"])
    S.op("gpsimd", lambda e: e.affine_select(
        out=sel[:], in_=sel[:], pattern=[[-1, NS * 128]],
        compare_op=ALU.is_ge, fill=0.0, base=127, channel_multiplier=128),
        reads=["sel"], writes=["sel"])
    return sel


def emit_ln(nc, S, key, tile, scr, tag, sq_key=None):
    sq, cols = scr["sq"], scr["cols"]
    ksq, kc = (sq_key or tag + "sq"), tag + "c"
    S.op("vector", lambda e: e.tensor_scalar(
        out=sq[:], in0=tile, scalar1=-1.0 / D, scalar2=None,
        op0=ALU.mult, op1=ALU.add, accum_out=cols[:, 1:2]),
        reads=[key], writes=[ksq, kc + "1"])
    S.op("scalar", lambda e: e.activation(
        out=sq[:], in_=tile, func=AF.Square, bias=cols[:, 1:2]),
        reads=[key, kc + "1"], writes=[ksq])
    S.op("vector", lambda e: e.tensor_scalar(
        out=sq[:], in0=sq[:], scalar1=1.0 / D, scalar2=None,
        op0=ALU.mult, op1=ALU.add, accum_out=cols[:, 2:3]),
        reads=[ksq], writes=[ksq, kc + "2"])
    S.op("scalar", lambda e: e.activation(
        out=cols[:, 2:3], in_=cols[:, 2:3], func=AF.Ln, bias=LN_EPS),
        reads=[kc + "2"], writes=[kc + "2"])
    S.op("scalar", lambda e: e.activation(
        out=cols[:, 2:3], in_=cols[:, 2:3], func=AF.Exp, scale=-0.5),
        reads=[kc + "2"], writes=[kc + "2"])
    S.op("vector", lambda e: e.tensor_tensor(
        out=cols[:, 3:4], in0=cols[:, 1:2], in1=cols[:, 2:3], op=ALU.mult),
        reads=[kc + "1", kc + "2"], writes=[kc + "3"])
    S.op("vector", lambda e: e.tensor_scalar(
        out=tile, in0=tile, scalar1=cols[:, 2:3], scalar2=cols[:, 3:4],
        op0=ALU.mult, op1=ALU.add),
        reads=[key, kc + "2", kc + "3"], writes=[key])


def decl_l0attn(nc, r1_kind):
    io = {}
    io["x"] = nc.dram_tensor("x", [T, D], F32, kind="ExternalInput").ap()
    io["w_in"] = nc.dram_tensor("w_in", [D, 2 * DI], F32, kind="ExternalInput").ap()
    io["w_xp"] = nc.dram_tensor("w_xp", [DI, 128], F32, kind="ExternalInput").ap()
    io["w_dt"] = nc.dram_tensor("w_dt", [DTR, DI], F32, kind="ExternalInput").ap()
    io["dtb_col"] = nc.dram_tensor("dtb_col", [128, 16], F32, kind="ExternalInput").ap()
    io["convw_col"] = nc.dram_tensor("convw_col", [128, 64], F32, kind="ExternalInput").ap()
    io["convb_col"] = nc.dram_tensor("convb_col", [128, 16], F32, kind="ExternalInput").ap()
    io["alog_col"] = nc.dram_tensor("alog_col", [128, 256], F32, kind="ExternalInput").ap()
    io["d_col"] = nc.dram_tensor("d_col", [128, 16], F32, kind="ExternalInput").ap()
    io["w_po"] = nc.dram_tensor("w_po", [DI, D], F32, kind="ExternalInput").ap()
    io["r1"] = nc.dram_tensor("r1", [T, D], F32, kind=r1_kind).ap()
    return io


def emit_l0attn(nc, S, C, io):
    x_in = io["x"]; w_in = io["w_in"]; w_xp = io["w_xp"]; w_dt = io["w_dt"]
    dtb_col = io["dtb_col"]; convw_col = io["convw_col"]; convb_col = io["convb_col"]
    alog_col = io["alog_col"]; d_col = io["d_col"]; w_po = io["w_po"]
    r1_out = io["r1"]

    s_xi = nc.dram_tensor("s_xi", [DI, T], F32, kind="Internal").ap()
    s_gres = nc.dram_tensor("s_gres", [DI, T], F32, kind="Internal").ap()
    s_dt = nc.dram_tensor("s_dt", [DI, T], F32, kind="Internal").ap()
    s_prod = nc.dram_tensor("s_prod", [DI, T], BF16, kind="Internal").ap()
    s_bt = nc.dram_tensor("s_bt", [NS, T], F32, kind="Internal").ap()
    s_ct = nc.dram_tensor("s_ct", [NS, T], F32, kind="Internal").ap()

    TC = 512
    NCH = T // TC

    with ExitStack() as stack:
        E = stack.enter_context
        C = dict(C, sel=emit_sel(nc, S, E))

        a_sb = E(nc.sbuf_tensor("a_sb", [128, 256], F32))
        dcol_sb = E(nc.sbuf_tensor("dcol_sb", [128, 16], F32))
        dtb_sb = E(nc.sbuf_tensor("dtb_sb", [128, 16], F32))
        cwcol_sb = E(nc.sbuf_tensor("cwcol_sb", [128, 64], F32))
        cbcol_sb = E(nc.sbuf_tensor("cbcol_sb", [128, 16], F32))
        hist = E(nc.sbuf_tensor("hist", [128, 3 * 16], F32))

        S.op("sync", lambda e: e.dma_start(out=a_sb[:], in_=alog_col),
             writes=["A"], queue="qA")
        S.op("scalar", lambda e: e.activation(out=a_sb[:], in_=a_sb[:], func=AF.Exp),
             reads=["A"], writes=["A"])
        S.op("vector", lambda e: e.tensor_scalar(
            out=a_sb[:], in0=a_sb[:], scalar1=-1.0, scalar2=None, op0=ALU.mult),
            reads=["A"], writes=["A"])
        S.op("sync", lambda e: e.dma_start(out=dcol_sb[:], in_=d_col),
             writes=["Dcol"], queue="qD")
        S.op("sync", lambda e: e.dma_start(out=dtb_sb[:], in_=dtb_col),
             writes=["dtb"], queue="qdtb")
        S.op("sync", lambda e: e.dma_start(out=cwcol_sb[:], in_=convw_col),
             writes=["cw"], queue="qcw")
        S.op("sync", lambda e: e.dma_start(out=cbcol_sb[:], in_=convb_col),
             writes=["cb"], queue="qcb")
        S.op("vector", lambda e: e.memset(hist[:], 0.0), writes=["hist"])

        with ExitStack() as ph:
            ph.enter_context(nc.named_scope("l0a_inproj"))
            Ep = ph.enter_context
            w_in_sb = [Ep(nc.sbuf_tensor(f"w_in_sb{d}", [128, 2 * DI], BF16))
                       for d in range(8)]
            w_xp_sb = Ep(nc.sbuf_tensor("w_xp_sb", [128, 16 * 128], BF16))
            w_dt_sb = Ep(nc.sbuf_tensor("w_dt_sb", [128, DI], BF16))
            x_tok = [Ep(nc.sbuf_tensor(f"x_tok{j}", [128, D], F32)) for j in range(4)]
            xbf = [Ep(nc.sbuf_tensor(f"xbf{j}", [128, D], BF16)) for j in range(4)]
            xT = [Ep(nc.sbuf_tensor(f"xT{d}", [128, TC], BF16)) for d in range(8)]
            NB = 4
            convbuf = [Ep(nc.sbuf_tensor(f"convbuf{i}", [128, TC + 3], F32)) for i in range(NB)]
            convacc = [Ep(nc.sbuf_tensor(f"convacc{i}", [128, TC], F32)) for i in range(NB)]
            xi_t = [Ep(nc.sbuf_tensor(f"xi_t{i}", [128, TC], F32)) for i in range(NB)]
            xibk_t = [Ep(nc.sbuf_tensor(f"xibk_t{k}", [128, TC], BF16)) for k in range(16)]
            gres_t = [Ep(nc.sbuf_tensor(f"gres_t{i}", [128, TC], F32)) for i in range(NB)]
            xdbl = Ep(nc.sbuf_tensor("xdbl", [128, TC], F32))
            deltab = Ep(nc.sbuf_tensor("deltab", [128, TC], BF16))
            dt_t = [Ep(nc.sbuf_tensor(f"dt_t{i}", [128, TC], F32)) for i in range(NB)]
            ps_tr = Ep(nc.psum_tensor("ps_tr", [128, TC], BF16))
            ps_m = [Ep(nc.psum_tensor(f"ps_m{i}", [128, 512], F32)) for i in range(4)]
            ps_xp = Ep(nc.psum_tensor("ps_xp", [128, 512], F32))
            ps_dt = [Ep(nc.psum_tensor(f"ps_dt{i}", [128, 512], F32)) for i in range(2)]

            for d in range(8):
                S.op("gpsimd", lambda e, d=d: e.dma_start(
                    out=w_in_sb[d][:], in_=w_in[d * 128:(d + 1) * 128, :]),
                    writes=[f"w_in{d}"], queue="qwin")
            S.op("gpsimd", lambda e: e.dma_start(
                out=w_xp_sb.ap().rearrange("p (b m) -> p b m", m=128),
                in_=w_xp.rearrange("(b p) m -> p b m", p=128)),
                writes=["w_xp"], queue="qwxp")
            S.op("gpsimd", lambda e: e.dma_start(out=w_dt_sb[64:128, :], in_=w_dt),
                 writes=["w_dt"], queue="qwdt")

            for c in range(NCH):
                t0 = c * TC
                for j in range(4):
                    S.op("sync", lambda e, j=j, t0=t0: e.dma_start(
                        out=x_tok[j][:], in_=x_in[t0 + j * 128: t0 + (j + 1) * 128, :]),
                        writes=[f"x_tok{j}"], queue=f"qx{j}")
                    S.op("scalar", lambda e, j=j: e.copy(out=xbf[j][:], in_=x_tok[j][:]),
                         reads=[f"x_tok{j}"], writes=[f"xbf{j}"])
                for d in range(8):
                    def tr(e, d=d):
                        for j in range(4):
                            last = e.transpose(
                                ps_tr[:, j * 128:(j + 1) * 128],
                                xbf[j][:, d * 128:(d + 1) * 128], C["ident"][:])
                        return last
                    S.op("tensor", tr,
                         reads=[f"xbf{j}" for j in range(4)] + ["ident"],
                         writes=["ps_tr"])
                    S.op("scalar", lambda e, d=d: e.copy(out=xT[d][:], in_=ps_tr[:]),
                         reads=["ps_tr"], writes=[f"xT{d}"])

                for fo in range(32):
                    pm = ps_m[fo % 4]
                    pk = f"ps_m{fo % 4}"

                    def mm(e, fo=fo, pm=pm):
                        for d in range(8):
                            last = e.matmul(
                                pm[:], w_in_sb[d][:, fo * 128:(fo + 1) * 128],
                                xT[d][:], start=(d == 0), stop=(d == 7))
                        return last
                    S.op("tensor", mm,
                         reads=[f"w_in{d}" for d in range(8)] + [f"xT{d}" for d in range(8)],
                         writes=[pk])
                    if fo < 16:
                        dblk = fo
                        i = dblk % 4
                        S.op("scalar", lambda e, pm=pm, i=i: e.copy(
                            out=convbuf[i][:, 3:TC + 3], in_=pm[:]),
                            reads=[pk], writes=[f"convbuf{i}"])
                        S.op("vector", lambda e, i=i, dblk=dblk: e.tensor_copy(
                            convbuf[i][:, 0:3], hist[:, dblk * 3:(dblk + 1) * 3]),
                            reads=["hist"], writes=[f"convbuf{i}"])
                        S.op("vector", lambda e, i=i, dblk=dblk: e.tensor_scalar(
                            out=convacc[i][:], in0=convbuf[i][:, 0:TC],
                            scalar1=cwcol_sb[:, dblk * 4:dblk * 4 + 1], scalar2=None,
                            op0=ALU.mult),
                            reads=[f"convbuf{i}", "cw"], writes=[f"convacc{i}"])
                        for k in (1, 2, 3):
                            S.op("vector", lambda e, i=i, dblk=dblk, k=k: e.scalar_tensor_tensor(
                                out=convacc[i][:], in0=convbuf[i][:, k:TC + k],
                                scalar=cwcol_sb[:, dblk * 4 + k:dblk * 4 + k + 1],
                                in1=convacc[i][:], op0=ALU.mult, op1=ALU.add),
                                reads=[f"convbuf{i}", "cw", f"convacc{i}"],
                                writes=[f"convacc{i}"])
                        S.op("vector", lambda e, i=i, dblk=dblk: e.tensor_copy(
                            hist[:, dblk * 3:(dblk + 1) * 3], convbuf[i][:, TC:TC + 3]),
                            reads=[f"convbuf{i}"], writes=["hist"])
                        S.op("scalar", lambda e, i=i, dblk=dblk: e.activation(
                            out=xi_t[i][:], in_=convacc[i][:], func=AF.Silu,
                            bias=cbcol_sb[:, dblk:dblk + 1]),
                            reads=[f"convacc{i}", "cb"], writes=[f"xi_t{i}"])
                        S.op("gpsimd", lambda e, i=i, dblk=dblk, t0=t0: e.dma_start(
                            out=s_xi[dblk * 128:(dblk + 1) * 128, t0:t0 + TC],
                            in_=xi_t[i][:]),
                            reads=[f"xi_t{i}"], queue="dst")
                        S.op("scalar", lambda e, i=i, dblk=dblk: e.copy(
                            out=xibk_t[dblk][:], in_=xi_t[i][:]),
                            reads=[f"xi_t{i}"], writes=[f"xib{dblk}"])
                    else:
                        gblk = fo - 16
                        i = gblk % 4
                        S.op("scalar", lambda e, pm=pm, i=i: e.activation(
                            out=gres_t[i][:], in_=pm[:], func=AF.Silu),
                            reads=[pk], writes=[f"gres_t{i}"])
                        S.op("gpsimd", lambda e, i=i, gblk=gblk, t0=t0: e.dma_start(
                            out=s_gres[gblk * 128:(gblk + 1) * 128, t0:t0 + TC],
                            in_=gres_t[i][:]),
                            reads=[f"gres_t{i}"], queue="dst")

                def mmxp(e):
                    for dblk in range(16):
                        last = e.matmul(
                            ps_xp[:], w_xp_sb[:, dblk * 128:(dblk + 1) * 128],
                            xibk_t[dblk][:], start=(dblk == 0), stop=(dblk == 15))
                    return last
                S.op("tensor", mmxp,
                     reads=[f"xib{k}" for k in range(16)] + ["w_xp"],
                     writes=["ps_xp"])

                S.op("scalar", lambda e: e.copy(out=xdbl[:], in_=ps_xp[:]),
                     reads=["ps_xp"], writes=["xdbl"])
                S.op("sync", lambda e, t0=t0: e.dma_start(
                    out=s_bt[:, t0:t0 + TC], in_=xdbl[0:16, :]),
                    reads=["xdbl"], writes=["BT"], queue="qbt")
                S.op("sync", lambda e, t0=t0: e.dma_start(
                    out=s_ct[:, t0:t0 + TC], in_=xdbl[32:48, :]),
                    reads=["xdbl"], writes=["CT"], queue="qct")
                S.op("scalar", lambda e: e.copy(
                    out=deltab[64:128, :], in_=xdbl[64:128, :]),
                    reads=["xdbl"], writes=["deltab"])
                for dblk in range(16):
                    i = dblk % 4
                    pd = dblk % 2
                    S.op("tensor", lambda e, dblk=dblk, pd=pd: e.matmul(
                        ps_dt[pd][:], w_dt_sb[64:128, dblk * 128:(dblk + 1) * 128],
                        deltab[64:128, :], start=True, stop=True),
                        reads=["deltab", "w_dt"], writes=[f"ps_dt{pd}"])
                    S.op("scalar", lambda e, dblk=dblk, i=i, pd=pd: e.activation(
                        out=dt_t[i][:], in_=ps_dt[pd][:], func=AF.Exp,
                        bias=dtb_sb[:, dblk:dblk + 1]),
                        reads=[f"ps_dt{pd}", "dtb"], writes=[f"dt_t{i}"])
                    S.op("scalar", lambda e, i=i: e.activation(
                        out=dt_t[i][:], in_=dt_t[i][:], func=AF.Ln, bias=1.0),
                        reads=[f"dt_t{i}"], writes=[f"dt_t{i}"])
                    S.op("gpsimd", lambda e, i=i, dblk=dblk, t0=t0: e.dma_start(
                        out=s_dt[dblk * 128:(dblk + 1) * 128, t0:t0 + TC],
                        in_=dt_t[i][:]),
                        reads=[f"dt_t{i}"], queue="dst")

            S.barrier()
            S.flush()

        with ExitStack() as ph:
            ph.enter_context(nc.named_scope("l0a_scan"))
            Ep = ph.enter_context
            dtd = [Ep(nc.sbuf_tensor(f"dtd{i}", [128, T], F32)) for i in range(2)]
            xid = [Ep(nc.sbuf_tensor(f"xid{i}", [128, T], F32)) for i in range(2)]
            grd = [Ep(nc.sbuf_tensor(f"grd{i}", [128, T], F32)) for i in range(2)]
            dud = [Ep(nc.sbuf_tensor(f"dud{i}", [128, T], F32)) for i in range(2)]
            bb = [Ep(nc.sbuf_tensor(f"bb{i}", [128, T], F32)) for i in range(2)]
            cbx = [Ep(nc.sbuf_tensor(f"cbx{i}", [128, T], BF16)) for i in range(2)]
            at = [Ep(nc.sbuf_tensor(f"at{i}", [128, T], F32)) for i in range(2)]
            ut = Ep(nc.sbuf_tensor("ut", [128, T], F32))
            st_ = Ep(nc.sbuf_tensor("st_", [128, T], BF16))
            zt = [Ep(nc.sbuf_tensor(f"zt{i}", [128, T], BF16)) for i in range(2)]
            yt = [Ep(nc.sbuf_tensor(f"yt{i}", [128, T], F32)) for i in range(2)]
            prodb = [Ep(nc.sbuf_tensor(f"prodb{i}", [128, T], BF16)) for i in range(2)]
            ps_y = [Ep(nc.psum_tensor(f"ps_y{i}", [128, T], F32)) for i in range(2)]

            def bcast(n, src_dram, src_key, dst, dst_key, qn):
                S.op("gpsimd", lambda e, n=n: e.dma_start(
                    out=dst[:], in_=bass.AP(src_dram.tensor, n * T + src_dram.offset,
                                            [[0, 128], [1, T]])),
                    reads=[src_key], writes=[dst_key], queue=qn)

            for dpair in range(8):
                for p in range(2):
                    dblk = 2 * dpair + p
                    S.op("sync", lambda e, dblk=dblk, p=p: e.dma_start(
                        out=dtd[p][:], in_=s_dt[dblk * 128:(dblk + 1) * 128, :]),
                        writes=[f"dtd{p}"], queue=f"qdtd{p}")
                    S.op("sync", lambda e, dblk=dblk, p=p: e.dma_start(
                        out=xid[p][:], in_=s_xi[dblk * 128:(dblk + 1) * 128, :]),
                        writes=[f"xid{p}"], queue=f"qxid{p}")
                    S.op("sync", lambda e, dblk=dblk, p=p: e.dma_start(
                        out=grd[p][:], in_=s_gres[dblk * 128:(dblk + 1) * 128, :]),
                        writes=[f"grd{p}"], queue=f"qgrd{p}")
                    S.op("vector", lambda e, p=p: e.tensor_tensor(
                        out=dud[p][:], in0=dtd[p][:], in1=xid[p][:], op=ALU.mult),
                        reads=[f"dtd{p}", f"xid{p}"], writes=[f"dud{p}"])
                for n in range(NS):
                    i = n % 2
                    bcast(n, s_bt, "BT", bb[i], f"bb{i}", f"qbb{i}")
                    bcast(n, s_ct, "CT", cbx[i], f"cbx{i}", f"qcb{i}")
                    for p in range(2):
                        dblk = 2 * dpair + p
                        S.op("scalar", lambda e, p=p, dblk=dblk, n=n: e.activation(
                            out=at[p][:], in_=dtd[p][:], func=AF.Exp,
                            scale=a_sb[:, dblk * 16 + n:dblk * 16 + n + 1]),
                            reads=[f"dtd{p}", "A"], writes=[f"at{p}"])
                        S.op("vector", lambda e, i=i, p=p: e.tensor_tensor(
                            out=ut[:], in0=dud[p][:], in1=bb[i][:], op=ALU.mult),
                            reads=[f"dud{p}", f"bb{i}"], writes=["ut"])
                        S.op("vector", lambda e, p=p: e.tensor_tensor_scan(
                            out=st_[:], data0=at[p][:], data1=ut[:], initial=0.0,
                            op0=ALU.mult, op1=ALU.add),
                            reads=[f"at{p}", "ut"], writes=["st"])
                        S.op("vector", lambda e, i=i, p=p: e.tensor_tensor(
                            out=zt[p][:], in0=st_[:], in1=cbx[i][:], op=ALU.mult),
                            reads=["st", f"cbx{i}"], writes=[f"zt{p}"])

                        def mmy(e, p=p, n=n):
                            for q in range(4):
                                last = e.matmul(
                                    ps_y[p][:, q * 512:(q + 1) * 512],
                                    C["ident"][:], zt[p][:, q * 512:(q + 1) * 512],
                                    start=(n == 0), stop=(n == NS - 1))
                            return last
                        S.op("tensor", mmy, reads=[f"zt{p}", "ident"],
                             writes=[f"ps_y{p}"])
                for p in range(2):
                    dblk = 2 * dpair + p
                    S.op("vector", lambda e, p=p, dblk=dblk: e.scalar_tensor_tensor(
                        out=yt[p][:], in0=xid[p][:], scalar=dcol_sb[:, dblk:dblk + 1],
                        in1=ps_y[p][:], op0=ALU.mult, op1=ALU.add),
                        reads=[f"xid{p}", "Dcol", f"ps_y{p}"], writes=[f"yt{p}"])
                    S.op("vector", lambda e, p=p: e.tensor_tensor(
                        out=prodb[p][:], in0=yt[p][:], in1=grd[p][:], op=ALU.mult),
                        reads=[f"yt{p}", f"grd{p}"], writes=[f"prodb{p}"])
                    S.op("gpsimd", lambda e, dblk=dblk, p=p: e.dma_start(
                        out=s_prod[dblk * 128:(dblk + 1) * 128, :], in_=prodb[p][:]),
                        reads=[f"prodb{p}"], queue="dst")

            S.barrier()
            S.flush()

        with ExitStack() as ph:
            ph.enter_context(nc.named_scope("l0a_outproj"))
            Ep = ph.enter_context
            w_po_sb = [Ep(nc.sbuf_tensor(f"w_po_sb{b}", [128, D], BF16)) for b in range(16)]
            prod_sb = [Ep(nc.sbuf_tensor(f"prod_sb{i}", [128, 16 * 128], BF16)) for i in range(2)]
            x_res = [Ep(nc.sbuf_tensor(f"x_res{i}", [128, D], F32)) for i in range(2)]
            r1_t = [Ep(nc.sbuf_tensor(f"r1_t{i}", [128, D], F32)) for i in range(2)]
            ln_sq = Ep(nc.sbuf_tensor("ln_sq", [128, D], F32))
            ln_cols = Ep(nc.sbuf_tensor("ln_cols", [128, 4], F32))
            ps_o = [Ep(nc.psum_tensor(f"ps_o{i}", [128, 512], F32)) for i in range(2)]

            for b in range(16):
                S.op("gpsimd", lambda e, b=b: e.dma_start(
                    out=w_po_sb[b][:], in_=w_po[b * 128:(b + 1) * 128, :]),
                    writes=[f"w_po{b}"], queue="qwpo")

            for tt in range(16):
                i = tt % 2
                S.op("sync", lambda e, tt=tt, i=i: e.dma_start(
                    out=prod_sb[i][:],
                    in_=s_prod.rearrange("(b p) t -> p b t", p=128)[:, :, tt * 128:(tt + 1) * 128]),
                    writes=[f"prod_sb{i}"], queue=f"qprodl{i}")
                S.op("sync", lambda e, tt=tt, i=i: e.dma_start(
                    out=x_res[i][:], in_=x_in[tt * 128:(tt + 1) * 128, :]),
                    writes=[f"x_res{i}"], queue=f"qxr{i}")
                for ch in range(2):
                    def mm(e, i=i, ch=ch):
                        for b in range(16):
                            last = e.matmul(
                                ps_o[ch][:], prod_sb[i][:, b * 128:(b + 1) * 128],
                                w_po_sb[b][:, ch * 512:(ch + 1) * 512],
                                start=(b == 0), stop=(b == 15))
                        return last
                    S.op("tensor", mm,
                         reads=[f"prod_sb{i}"] + [f"w_po{b}" for b in range(16)],
                         writes=[f"ps_o{ch}"])
                    S.op("vector", lambda e, i=i, ch=ch: e.tensor_add(
                        r1_t[i][:, ch * 512:(ch + 1) * 512], ps_o[ch][:],
                        x_res[i][:, ch * 512:(ch + 1) * 512]),
                        reads=[f"ps_o{ch}", f"x_res{i}"], writes=[f"r1_t{i}"])
                emit_ln(nc, S, f"r1_t{i}", r1_t[i][:],
                        dict(sq=ln_sq, cols=ln_cols), f"ln{tt}")
                S.op("sync", lambda e, tt=tt, i=i: e.dma_start(
                    out=r1_out[tt * 128:(tt + 1) * 128, :], in_=r1_t[i][:]),
                    reads=[f"r1_t{i}"], queue="dso")

            S.barrier()
            S.flush()


def build_l0attn():
    nc = bass.Bass("TRN2", target_bir_lowering=False, debug=False)
    io = decl_l0attn(nc, "ExternalOutput")
    with ExitStack() as stack:
        S = Sched(nc, stack)
        C = emit_consts(nc, S, stack)
        emit_l0attn(nc, S, C, io)
        S.finalize()
    return nc


def prep_l0attn(ins, core):
    conv_w = ins["L0_attn_conv_w"].reshape(16, 128, 4).transpose(1, 0, 2).reshape(128, 64)
    conv_b = ins["L0_attn_conv_b"].reshape(16, 128).T
    dtb = ins["L0_attn_dt_proj_b"].reshape(16, 128).T
    alog = ins["L0_attn_A_log"].reshape(16, 128, 16).transpose(1, 0, 2).reshape(128, 256)
    dcol = ins["L0_attn_Dparam"].reshape(16, 128).T
    xp = ins["L0_attn_x_proj_w"]
    xp_pad = np.zeros((DI, 128), np.float32)
    xp_pad[:, 0:16] = xp[:, 64:80]
    xp_pad[:, 32:48] = xp[:, 80:96]
    xp_pad[:, 64:128] = xp[:, 0:64]
    return {
        "x": np.ascontiguousarray(ins["L0_attn_x"][core]),
        "w_in": ins["L0_attn_in_proj_w"],
        "w_xp": np.ascontiguousarray(xp_pad),
        "w_dt": ins["L0_attn_dt_proj_w"],
        "dtb_col": np.ascontiguousarray(dtb),
        "convw_col": np.ascontiguousarray(conv_w),
        "convb_col": np.ascontiguousarray(conv_b),
        "alog_col": np.ascontiguousarray(alog),
        "d_col": np.ascontiguousarray(dcol),
        "w_po": ins["L0_attn_out_proj_w"],
    }


def decl_l0ffn(nc, io, r1_kind, r2_kind):
    if "r1" not in io:
        io["r1"] = nc.dram_tensor("r1", [T, D], F32, kind=r1_kind).ap()
    io["wgT"] = nc.dram_tensor("wgT", [D, HF], I32, kind="ExternalInput").ap()
    io["wuT"] = nc.dram_tensor("wuT", [D, HF], I32, kind="ExternalInput").ap()
    io["wdT"] = nc.dram_tensor("wdT", [HF, D], I32, kind="ExternalInput").ap()
    io["r2"] = nc.dram_tensor("r2", [T, D], F32, kind=r2_kind).ap()
    return io


def emit_l0ffn(nc, S, C, io):
    r1_in = io["r1"]; wgT = io["wgT"]; wuT = io["wuT"]; wdT = io["wdT"]
    r2_out = io["r2"]

    TG = 256
    NG = T // TG

    S.pe_auto_nop = False
    with ExitStack() as stack:
        E = stack.enter_context
        wg_sb = [E(nc.sbuf_tensor(f"wg_sb{d}", [128, HF], BF16)) for d in range(8)]
        wu_sb = [E(nc.sbuf_tensor(f"wu_sb{d}", [128, HF], BF16)) for d in range(8)]
        wd_sb = [E(nc.sbuf_tensor(f"wd_sb{h}", [128, D], BF16)) for h in range(24)]
        r1_tok = [E(nc.sbuf_tensor(f"r1_tok{j}", [128, D], F32)) for j in range(2)]
        r1bf = [E(nc.sbuf_tensor(f"r1bf{j}", [128, D], BF16)) for j in range(2)]
        r1T = [E(nc.sbuf_tensor(f"r1T{d}", [128, TG], BF16)) for d in range(8)]
        g_s = E(nc.sbuf_tensor("g_s", [128, TG], F32))
        u_s = E(nc.sbuf_tensor("u_s", [128, TG], F32))
        h_sb = [E(nc.sbuf_tensor(f"h_sb{f}", [128, TG], BF16)) for f in range(24)]
        r2_t = E(nc.sbuf_tensor("r2_t", [128, D], F32))
        ln_sq = E(nc.sbuf_tensor("ln_sq", [128, D], F32))
        ln_cols = E(nc.sbuf_tensor("ln_cols", [128, 4], F32))
        ps_tr = E(nc.psum_tensor("ps_tr", [128, TG], BF16))
        ps_g = E(nc.psum_tensor("ps_g", [128, TG], F32))
        ps_u = E(nc.psum_tensor("ps_u", [128, TG], F32))
        ps_d = [E(nc.psum_tensor(f"ps_d{i}", [128, 512], F32)) for i in range(2)]

        for d in range(8):
            S.op("gpsimd", lambda e, d=d: e.dma_start(
                out=wg_sb[d][:], in_=wgT[d * 128:(d + 1) * 128, :]),
                writes=[f"wg{d}"], queue="qwg")
            S.op("gpsimd", lambda e, d=d: e.dma_start(
                out=wu_sb[d][:], in_=wuT[d * 128:(d + 1) * 128, :]),
                writes=[f"wu{d}"], queue="qwu")
        for hh in range(24):
            S.op("gpsimd", lambda e, hh=hh: e.dma_start(
                out=wd_sb[hh][:], in_=wdT[hh * 128:(hh + 1) * 128, :]),
                writes=[f"wd{hh}"], queue="qwd")

        for g in range(NG):
            t0 = g * TG
            for j in range(2):
                S.op("sync", lambda e, j=j, t0=t0: e.dma_start(
                    out=r1_tok[j][:], in_=r1_in[t0 + j * 128: t0 + (j + 1) * 128, :]),
                    writes=[f"r1_tok{j}"], queue=f"qr1{j}")
                S.op("scalar", lambda e, j=j: e.copy(out=r1bf[j][:], in_=r1_tok[j][:]),
                     reads=[f"r1_tok{j}"], writes=[f"r1bf{j}"])
            for d in range(8):
                def tr(e, d=d):
                    for j in range(2):
                        last = e.transpose(
                            ps_tr[:, j * 128:(j + 1) * 128],
                            r1bf[j][:, d * 128:(d + 1) * 128], C["ident"][:])
                    return last
                S.op("tensor", tr,
                     reads=[f"r1bf{j}" for j in range(2)] + ["ident"],
                     writes=["ps_tr"])
                S.op("scalar", lambda e, d=d: e.copy(out=r1T[d][:], in_=ps_tr[:]),
                     reads=["ps_tr"], writes=[f"r1T{d}"])

            for f in range(24):
                def mmg(e, f=f):
                    for d in range(8):
                        last = e.matmul(ps_g[:], wg_sb[d][:, f * 128:(f + 1) * 128],
                                        r1T[d][:], start=(d == 0), stop=(d == 7))
                    return last
                S.op("tensor", mmg,
                     reads=[f"wg{d}" for d in range(8)] + [f"r1T{d}" for d in range(8)],
                     writes=["ps_g"])

                def mmu(e, f=f):
                    for d in range(8):
                        last = e.matmul(ps_u[:], wu_sb[d][:, f * 128:(f + 1) * 128],
                                        r1T[d][:], start=(d == 0), stop=(d == 7))
                    return last
                S.op("tensor", mmu,
                     reads=[f"wu{d}" for d in range(8)] + [f"r1T{d}" for d in range(8)],
                     writes=["ps_u"])
                S.op("scalar", lambda e: e.activation(
                    out=g_s[:], in_=ps_g[:], func=AF.Silu, scale=DEQ),
                    reads=["ps_g"], writes=["g_s"])
                S.op("scalar", lambda e: e.activation(
                    out=u_s[:], in_=ps_u[:], func=AF.Copy, scale=DEQ),
                    reads=["ps_u"], writes=["u_s"])
                S.op("vector", lambda e, f=f: e.tensor_tensor(
                    out=h_sb[f][:], in0=g_s[:], in1=u_s[:], op=ALU.mult),
                    reads=["g_s", "u_s"], writes=[f"h{f}"])

            for j in range(2):
                for ch in range(2):
                    def mmd(e, j=j, ch=ch):
                        for hh in range(24):
                            last = e.matmul(
                                ps_d[ch][:], h_sb[hh][:, j * 128:(j + 1) * 128],
                                wd_sb[hh][:, ch * 512:(ch + 1) * 512],
                                start=(hh == 0), stop=(hh == 23))
                        return last
                    S.op("tensor", mmd,
                         reads=[f"h{hh}" for hh in range(24)] + [f"wd{hh}" for hh in range(24)],
                         writes=[f"ps_d{ch}"])
                    S.op("vector", lambda e, j=j, ch=ch: e.scalar_tensor_tensor(
                        out=r2_t[:, ch * 512:(ch + 1) * 512], in0=ps_d[ch][:],
                        scalar=DEQ, in1=r1_tok[j][:, ch * 512:(ch + 1) * 512],
                        op0=ALU.mult, op1=ALU.add),
                        reads=[f"ps_d{ch}", f"r1_tok{j}"], writes=["r2_t"])
                emit_ln(nc, S, "r2_t", r2_t[:],
                        dict(sq=ln_sq, cols=ln_cols), f"ln{g}_{j}")
                S.op("sync", lambda e, j=j, t0=t0: e.dma_start(
                    out=r2_out[t0 + j * 128: t0 + (j + 1) * 128, :], in_=r2_t[:]),
                    reads=["r2_t"], queue="dso")

        S.barrier()
        S.flush()
        S.pe_auto_nop = True


def build_l0ffn():
    nc = bass.Bass("TRN2", target_bir_lowering=False, debug=False)
    io = decl_l0ffn(nc, {}, "ExternalInput", "ExternalOutput")
    with ExitStack() as stack:
        S = Sched(nc, stack)
        C = emit_consts(nc, S, stack)
        emit_l0ffn(nc, S, C, io)
        S.finalize()
    return nc


def prep_l0ffn(ins, core, r1):
    return {
        "r1": r1,
        "wgT": np.ascontiguousarray(ins["L0_ffn_w_gate"].T),
        "wuT": np.ascontiguousarray(ins["L0_ffn_w_up"].T),
        "wdT": np.ascontiguousarray(ins["L0_ffn_w_down"].T),
    }


def perm_runs():
    g = int(math.ceil(math.sqrt(T)))
    grid = np.arange(g * g).reshape(g, g)
    grid[1::2] = grid[1::2, ::-1]
    flat = grid.reshape(-1)
    flat = flat[flat < T][:T]
    runs = []
    i = 0
    while i < len(flat):
        j = i + 1
        step = 0
        while j < len(flat) and flat[j] - flat[j - 1] in (1, -1) and \
                (step == 0 or flat[j] - flat[j - 1] == step):
            step = flat[j] - flat[j - 1]
            j += 1
        runs.append((int(flat[i]), int(step) if step else 1, j - i))
        i = j
    return flat, runs


def decl_l1attn(nc, io, r2_kind, r3_kind):
    if "r2" not in io:
        io["r2"] = nc.dram_tensor("r2", [T, D], F32, kind=r2_kind).ap()
    io["wqkvT"] = nc.dram_tensor("wqkvT", [D, 3 * D], F32, kind="ExternalInput").ap()
    io["woT"] = nc.dram_tensor("woT", [D, D], F32, kind="ExternalInput").ap()
    io["r3"] = nc.dram_tensor("r3", [T, D], F32, kind=r3_kind).ap()
    return io


def emit_l1attn(nc, S, C, io, stop=99):
    r2_in = io["r2"]; wqkvT = io["wqkvT"]; woT = io["woT"]; r3_out = io["r3"]
    s_xp = nc.dram_tensor("s_xp", [T, D], F32, kind="Internal").ap()

    TG = 512
    NG = T // TG

    with ExitStack() as stack:
        E = stack.enter_context
        wq_sb = [E(nc.sbuf_tensor(f"wq_sb{d}", [128, 3 * D], BF16)) for d in range(8)]
        wo_sb = [E(nc.sbuf_tensor(f"wo_sb{d}", [128, D], BF16)) for d in range(8)]
        xp_tok = [E(nc.sbuf_tensor(f"xp_tok{j}", [128, D], F32)) for j in range(4)]
        xpbf = [E(nc.sbuf_tensor(f"xpbf{j}", [128, D], BF16)) for j in range(4)]
        xpT = [E(nc.sbuf_tensor(f"xpT{d}", [128, TG], BF16)) for d in range(8)]
        qkvT = [E(nc.sbuf_tensor(f"qkvT{f}", [128, TG], BF16)) for f in range(24)]
        sc_sb = E(nc.sbuf_tensor("sc_sb", [128, H * HD], F32))
        nmax = E(nc.sbuf_tensor("nmax", [128, H], F32))
        esum = E(nc.sbuf_tensor("esum", [128, H], F32))
        attn_bf = E(nc.sbuf_tensor("attn_bf", [128, H * HD], BF16))
        attnT = E(nc.sbuf_tensor("attnT", [64, 16 * 128], BF16))
        vd_sb = [E(nc.sbuf_tensor(f"vd_sb{s}", [64, 16 * 64], BF16)) for s in range(4)]
        oT = E(nc.sbuf_tensor("oT", [128, 8 * 128], BF16))
        x_res = E(nc.sbuf_tensor("x_res", [128, D], F32))
        r3_t = E(nc.sbuf_tensor("r3_t", [128, D], F32))
        ln_sq = E(nc.sbuf_tensor("ln_sq", [128, D], F32))
        ln_cols = E(nc.sbuf_tensor("ln_cols", [128, 4], F32))
        ps_tr = E(nc.psum_tensor("ps_tr", [128, TG], BF16))
        ps_q = E(nc.psum_tensor("ps_q", [128, TG], F32))
        ps_sc = E(nc.psum_tensor("ps_sc", [128, H * HD], F32))
        ps_ot = E(nc.psum_tensor("ps_ot", [128, 8 * 128], F32))
        ps_o = [E(nc.psum_tensor(f"ps_o{i}", [128, 512], F32)) for i in range(2)]

        for d in range(8):
            S.op("gpsimd", lambda e, d=d: e.dma_start(
                out=wq_sb[d][:], in_=wqkvT[d * 128:(d + 1) * 128, :]),
                writes=[f"wq{d}"], queue="qwq")
            S.op("gpsimd", lambda e, d=d: e.dma_start(
                out=wo_sb[d][:], in_=woT[d * 128:(d + 1) * 128, :]),
                writes=[f"wo{d}"], queue="qwo")

        G46 = 46
        nfull = 22
        S.op("sync", lambda e: e.dma_start(
            out=bass.AP(s_xp.tensor, 0, [[2 * G46 * D, nfull], [1, G46 * D]]),
            in_=bass.AP(r2_in.tensor, 0, [[2 * G46 * D, nfull], [1, G46 * D]])),
            writes=["s_xp"], queue="qgather")
        S.op("sync", lambda e: e.dma_start(
            out=bass.AP(s_xp.tensor, G46 * D, [[2 * G46 * D, nfull], [D, G46], [1, D]]),
            in_=bass.AP(r2_in.tensor, (2 * G46 - 1) * D,
                        [[2 * G46 * D, nfull], [-D, G46], [1, D]])),
            writes=["s_xp"], queue="qgather")
        S.op("sync", lambda e: e.dma_start(
            out=bass.AP(s_xp.tensor, 44 * G46 * D, [[1, (T - 44 * G46) * D]]),
            in_=bass.AP(r2_in.tensor, 44 * G46 * D, [[1, (T - 44 * G46) * D]])),
            writes=["s_xp"], queue="qgather")

        for g in range(NG):
            t0 = g * TG
            for j in range(4):
                S.op("sync", lambda e, j=j, t0=t0: e.dma_start(
                    out=xp_tok[j][:], in_=s_xp[t0 + j * 128: t0 + (j + 1) * 128, :]),
                    reads=["s_xp"], writes=[f"xp_tok{j}"], queue=f"qxp{j}")
                S.op("scalar", lambda e, j=j: e.copy(out=xpbf[j][:], in_=xp_tok[j][:]),
                     reads=[f"xp_tok{j}"], writes=[f"xpbf{j}"])
            for d in range(8 if stop >= 2 else 0):
                def tr(e, d=d):
                    for j in range(4):
                        last = e.transpose(
                            ps_tr[:, j * 128:(j + 1) * 128],
                            xpbf[j][:, d * 128:(d + 1) * 128], C["ident"][:])
                    return last
                S.op("tensor", tr,
                     reads=[f"xpbf{j}" for j in range(4)] + ["ident"],
                     writes=["ps_tr"])
                S.op("scalar", lambda e, d=d: e.copy(out=xpT[d][:], in_=ps_tr[:]),
                     reads=["ps_tr"], writes=[f"xpT{d}"])

            for f in range(24 if stop >= 3 else 0):
                def mmq(e, f=f):
                    for d in range(8):
                        last = e.matmul(ps_q[:], wq_sb[d][:, f * 128:(f + 1) * 128],
                                        xpT[d][:], start=(d == 0), stop=(d == 7))
                    return last
                S.op("tensor", mmq,
                     reads=[f"wq{d}" for d in range(8)] + [f"xpT{d}" for d in range(8)],
                     writes=["ps_q"])
                S.op("scalar", lambda e, f=f: e.copy(out=qkvT[f][:], in_=ps_q[:]),
                     reads=["ps_q"], writes=[f"qkvT{f}"])

            qk = [f"qkvT{f}" for f in range(24)]
            for sg in range(4 if stop >= 4 else 0):
                s0 = sg * 128
                for par in (0, 1):
                    b = 64 * par

                    def trv(e, par=par, b=b, s0=s0):
                        for i in range(8):
                            h = 2 * i + par
                            vt = qkvT[16 + h // 2][b:b + 64, s0:s0 + 128:2]
                            last = e.transpose(
                                ps_tr[0:64, i * 64:i * 64 + 64],
                                vt, C["ident"][b:b + 64, b:b + 64])
                        return last
                    S.op("tensor", trv, reads=qk + ["ident"], writes=["ps_tr"])
                    S.op("scalar", lambda e, par=par, sg=sg: e.copy(
                        out=vd_sb[sg][0:64, :].rearrange(
                            "p (h k) -> p h k", k=64)[:, par::2, :],
                        in_=ps_tr[0:64, 0:512].rearrange(
                            "p (h k) -> p h k", k=64)),
                        reads=["ps_tr"], writes=[f"vd{sg}"])
            for sg in range(4):
                if stop < 4:
                    break
                s0 = sg * 128

                def mmsc(e, s0=s0):
                    for par in (0, 1):
                        if par:
                            e.nop()
                        for h in range(par, H, 2):
                            b = 64 * par
                            qt = qkvT[h // 2][b:b + 64, s0:s0 + 128]
                            kt = qkvT[8 + h // 2][b:b + 64, s0:s0 + 128:2]
                            last = e.matmul(ps_sc[:, h * 64:(h + 1) * 64], qt, kt,
                                            start=True, stop=True)
                    return last
                S.op("tensor", mmsc, reads=qk, writes=["ps_sc"])
                if stop < 5:
                    continue
                S.op("vector", lambda e: e.tensor_reduce(
                    out=nmax[:], in_=ps_sc.ap().rearrange("p (h k) -> p h k", k=HD),
                    axis=AX.X, op=ALU.max, negate=True),
                    reads=["ps_sc"], writes=["nmax"])
                S.op("vector", lambda e: e.tensor_tensor(
                    out=sc_sb.ap().rearrange("p (h k) -> p h k", k=HD),
                    in0=ps_sc.ap().rearrange("p (h k) -> p h k", k=HD),
                    in1=nmax[:, :, None].broadcast_to([128, H, HD]),
                    op=ALU.add),
                    reads=["ps_sc", "nmax"], writes=["sc"])
                S.op("scalar", lambda e: e.activation(
                    out=sc_sb[:], in_=sc_sb[:], func=AF.Exp, scale=QSCALE),
                    reads=["sc"], writes=["sc"])
                S.op("vector", lambda e: e.tensor_reduce(
                    out=esum[:], in_=sc_sb.ap().rearrange("p (h k) -> p h k", k=HD),
                    axis=AX.X, op=ALU.add),
                    reads=["sc"], writes=["esum"])
                S.op("vector", lambda e: e.reciprocal(out=esum[:], in_=esum[:]),
                     reads=["esum"], writes=["esum"])
                S.op("vector", lambda e: e.tensor_tensor(
                    out=attn_bf.ap().rearrange("p (h k) -> p h k", k=HD),
                    in0=sc_sb.ap().rearrange("p (h k) -> p h k", k=HD),
                    in1=esum[:, :, None].broadcast_to([128, H, HD]),
                    op=ALU.mult),
                    reads=["sc", "esum"], writes=["attn"])
                if stop < 6:
                    continue
                for hb in range(4):
                    def trb(e, hb=hb):
                        for hh in range(4):
                            h = hb * 4 + hh
                            last = e.transpose(
                                ps_tr[0:64, hh * 128:hh * 128 + 128],
                                attn_bf[:, h * 64:(h + 1) * 64], C["ident"][:])
                        return last
                    S.op("tensor", trb, reads=["attn", "ident"], writes=["ps_tr"])
                    S.op("scalar", lambda e, hb=hb: e.copy(
                        out=attnT[0:64, hb * 512:(hb + 1) * 512], in_=ps_tr[0:64, 0:512]),
                        reads=["ps_tr"], writes=["attnT"])
                if stop < 7:
                    continue

                def mmo(e, sg=sg):
                    for h in range(H):
                        last = e.matmul(
                            ps_ot[64 * (h % 2):64 * (h % 2) + 64,
                                  (h // 2) * 128:(h // 2) * 128 + 128],
                            vd_sb[sg][0:64, h * 64:(h + 1) * 64],
                            attnT[0:64, h * 128:(h + 1) * 128],
                            start=True, stop=True)
                    return last
                S.op("tensor", mmo, reads=[f"vd{sg}", "attnT"], writes=["ps_ot"])
                S.op("scalar", lambda e: e.copy(out=oT[:], in_=ps_ot[:]),
                     reads=["ps_ot"], writes=["oT"])

                if stop < 8:
                    continue
                tseg = t0 + s0
                S.op("sync", lambda e, tseg=tseg: e.dma_start(
                    out=x_res[:], in_=r2_in[tseg:tseg + 128, :]),
                    writes=["x_res"], queue="qxr")
                for ch in range(2):
                    def mmop(e, ch=ch):
                        for d in range(8):
                            last = e.matmul(
                                ps_o[ch][:], oT[:, d * 128:(d + 1) * 128],
                                wo_sb[d][:, ch * 512:(ch + 1) * 512],
                                start=(d == 0), stop=(d == 7))
                        return last
                    S.op("tensor", mmop,
                         reads=["oT"] + [f"wo{d}" for d in range(8)],
                         writes=[f"ps_o{ch}"])
                    S.op("vector", lambda e, ch=ch: e.tensor_add(
                        r3_t[:, ch * 512:(ch + 1) * 512], ps_o[ch][:],
                        x_res[:, ch * 512:(ch + 1) * 512]),
                        reads=[f"ps_o{ch}", "x_res"], writes=["r3_t"])
                emit_ln(nc, S, "r3_t", r3_t[:],
                        dict(sq=ln_sq, cols=ln_cols), f"ln{g}_{sg}")
                S.op("sync", lambda e, tseg=tseg: e.dma_start(
                    out=r3_out[tseg:tseg + 128, :], in_=r3_t[:]),
                    reads=["r3_t"], queue="dso")

        S.barrier()
        S.flush()


def build_l1attn(stop=99):
    nc = bass.Bass("TRN2", target_bir_lowering=False, debug=False)
    io = decl_l1attn(nc, {}, "ExternalInput", "ExternalOutput")
    with ExitStack() as stack:
        S = Sched(nc, stack)
        C = emit_consts(nc, S, stack)
        emit_l1attn(nc, S, C, io, stop=stop)
        S.finalize()
    return nc


def prep_l1attn(ins, core, r2):
    return {
        "r2": r2,
        "wqkvT": np.ascontiguousarray(ins["L1_attn_w_qkv"].T),
        "woT": np.ascontiguousarray(ins["L1_attn_w_out"].T),
    }


def decl_l1moe(nc, io, r3_kind):
    if "r3" not in io:
        io["r3"] = nc.dram_tensor("r3", [T, D], F32, kind=r3_kind).ap()
    io["wgate"] = nc.dram_tensor("wgate", [D, NE], F32, kind="ExternalInput").ap()
    io["bgate"] = nc.dram_tensor("bgate", [1, NE], F32, kind="ExternalInput").ap()
    io["w1"] = nc.dram_tensor("w1", [NE, D, FE], F32, kind="ExternalInput").ap()
    io["b1"] = nc.dram_tensor("b1", [NE, 128, 16], F32, kind="ExternalInput").ap()
    io["w2"] = nc.dram_tensor("w2", [NE, FE, D], F32, kind="ExternalInput").ap()
    io["b2"] = nc.dram_tensor("b2", [NE, D], F32, kind="ExternalInput").ap()
    io["gam"] = nc.dram_tensor("gam", [1, D], F32, kind="ExternalInput").ap()
    io["bet"] = nc.dram_tensor("bet", [1, D], F32, kind="ExternalInput").ap()
    io["out"] = nc.dram_tensor("out", [T, D], F32, kind="ExternalOutput").ap()
    return io


def emit_l1moe(nc, S, C, io):
    r3_in = io["r3"]; wg_in = io["wgate"]; bg_in = io["bgate"]
    w1_in = io["w1"]; b1_in = io["b1"]; w2_in = io["w2"]; b2_in = io["b2"]
    gam_in = io["gam"]; bet_in = io["bet"]; out_t = io["out"]

    s_ym = nc.dram_tensor("s_ym", [T, D], F32, kind="Internal").ap()

    TG = 512
    NG = T // TG
    BIG = 1e30

    S.pe_auto_nop = False
    with ExitStack() as stack:
        E = stack.enter_context
        r3T = [E(nc.sbuf_tensor(f"r3T{d}", [128, T], BF16)) for d in range(8)]
        wgate_sb = E(nc.sbuf_tensor("wgate_sb", [128, 8 * NE], BF16))
        bgate_sb = E(nc.sbuf_tensor("bgate_sb", [1, NE], F32))
        wfull = [E(nc.sbuf_tensor(f"wfull{tt}", [128, NE], F32)) for tt in range(16)]
        wfT = E(nc.sbuf_tensor("wfT", [NE, 128], F32))
        w1_sb = [[E(nc.sbuf_tensor(f"w1_sb{sl}_{d}", [128, FE], BF16)) for d in range(8)]
                 for sl in range(2)]
        w2_one = [E(nc.sbuf_tensor(f"w2_sb{h}", [128, D], BF16)) for h in range(16)]
        w2_sb = [w2_one, w2_one]
        b1_sb = [E(nc.sbuf_tensor(f"b1_sb{sl}", [128, 16], F32)) for sl in range(2)]
        b2_sb = E(nc.sbuf_tensor("b2_sb", [NE, D], F32))
        h_sb = [E(nc.sbuf_tensor(f"h_sb{f}", [128, TG], BF16)) for f in range(16)]
        gam_b = E(nc.sbuf_tensor("gam_b", [128, D], F32))
        bet_b = E(nc.sbuf_tensor("bet_b", [128, D], F32))
        identf = E(nc.sbuf_tensor("identf", [128, 128], F32))
        lg = E(nc.sbuf_tensor("lg", [128, NE], F32))
        lcols = E(nc.sbuf_tensor("lcols", [128, 4], F32))
        mask = E(nc.sbuf_tensor("mask", [128, NE], F32))
        r3_tok = [E(nc.sbuf_tensor(f"r3_tok{j}", [128, D], F32)) for j in range(2)]
        r3bf = E(nc.sbuf_tensor("r3bf", [128, D], BF16))
        ytmp0 = E(nc.sbuf_tensor("ytmp0", [128, D], F32))
        y_t = E(nc.sbuf_tensor("y_t", [128, D], F32))
        z_t = E(nc.sbuf_tensor("z_t", [128, D], F32))
        ytbuf = [ytmp0, z_t]
        kbuf = ["ytmp0", "z_t"]
        ln_cols = E(nc.sbuf_tensor("ln_cols", [128, 4], F32))
        ps_tr = E(nc.psum_tensor("ps_tr", [128, 512], BF16))
        ps_trf = E(nc.psum_tensor("ps_trf", [128, 128], F32))
        ps_lg = E(nc.psum_tensor("ps_lg", [128, NE], F32))
        ps_h = [E(nc.psum_tensor(f"ps_h{i}", [128, TG], F32)) for i in range(3)]
        ps_eo = [E(nc.psum_tensor(f"ps_eo{i}", [128, 512], F32)) for i in range(2)]

        S.op("gpsimd", lambda e: e.memset(identf[:], 1.0), writes=["identf"])
        S.op("gpsimd", lambda e: e.affine_select(
            out=identf[:], in_=identf[:], pattern=[[-1, 128]],
            compare_op=ALU.is_equal, fill=0.0, base=0, channel_multiplier=1),
            reads=["identf"], writes=["identf"])

        S.op("gpsimd", lambda e: e.dma_start(
            out=wgate_sb.ap().rearrange("p (b m) -> p b m", m=NE),
            in_=wg_in.rearrange("(b p) m -> p b m", p=128)),
            writes=["wgate"], queue="qwgt")
        S.op("sync", lambda e: e.dma_start(out=bgate_sb[0:1, :], in_=bg_in),
             writes=["bgate"], queue="qbg")
        for src_in, dst, dkey in ((gam_in, gam_b, "gam_b"), (bet_in, bet_b, "bet_b")):
            S.op("sync", lambda e, src_in=src_in: e.dma_start(
                out=b2_sb[0:1, :], in_=src_in),
                reads=["ps_eo0", "ps_eo1"], writes=["brow"], queue="qbrow")
            for half in range(2):
                S.op("tensor", lambda e, half=half: e.matmul(
                    ps_eo[half][:], C["ones128"][0:1, :],
                    b2_sb[0:1, half * 512:(half + 1) * 512], start=True, stop=True),
                    reads=["brow", "ones128"], writes=[f"ps_eo{half}"])
                S.op("scalar", lambda e, half=half, dst=dst: e.copy(
                    out=dst[:, half * 512:(half + 1) * 512], in_=ps_eo[half][:]),
                    reads=[f"ps_eo{half}"], writes=[dkey])
        S.op("sync", lambda e: e.dma_start(out=b2_sb[:], in_=b2_in),
             reads=["brow", "ps_eo0", "ps_eo1"], writes=["b2", "brow"], queue="qb2")

        for tt in range(16):
            j = tt % 2
            S.op("sync", lambda e, tt=tt, j=j: e.dma_start(
                out=r3_tok[j][:], in_=r3_in[tt * 128:(tt + 1) * 128, :]),
                writes=[f"r3_tok{j}"], queue=f"qr3{j}")
            S.op("scalar", lambda e, j=j: e.copy(out=r3bf[:], in_=r3_tok[j][:]),
                 reads=[f"r3_tok{j}"], writes=["r3bf"])
            for d in range(8):
                S.op("tensor", lambda e, d=d: e.transpose(
                    ps_tr[:, 0:128], r3bf[:, d * 128:(d + 1) * 128], C["ident"][:]),
                    reads=["r3bf", "ident"], writes=["ps_tr"])
                S.op("scalar", lambda e, d=d, tt=tt: e.copy(
                    out=r3T[d][:, tt * 128:(tt + 1) * 128], in_=ps_tr[:, 0:128]),
                    reads=["ps_tr"], writes=[f"r3T{d}"])

            def mmlg(e, tt=tt):
                for d in range(8):
                    e.matmul(ps_lg[:], r3T[d][:, tt * 128:(tt + 1) * 128],
                             wgate_sb[:, d * NE:(d + 1) * NE],
                             start=(d == 0), stop=False)
                return e.matmul(ps_lg[:], C["ones128"][0:1, 0:128],
                                bgate_sb[0:1, :], start=False, stop=True)
            S.op("tensor", mmlg,
                 reads=[f"r3T{d}" for d in range(8)] + ["wgate", "bgate", "ones128"],
                 writes=["ps_lg"])
            S.op("scalar", lambda e: e.copy(out=lg[:], in_=ps_lg[:]),
                 reads=["ps_lg"], writes=["lg"])
            S.op("vector", lambda e: e.reduce_max(
                out=lcols[:, 0:1], in_=lg[:], axis=AX.X),
                reads=["lg"], writes=["m1"])
            S.op("vector", lambda e: e.tensor_scalar(
                out=mask[:], in0=lg[:], scalar1=lcols[:, 0:1], scalar2=None,
                op0=ALU.is_ge),
                reads=["lg", "m1"], writes=["mask"])
            S.op("vector", lambda e: e.scalar_tensor_tensor(
                out=mask[:], in0=mask[:], scalar=-BIG, in1=lg[:],
                op0=ALU.mult, op1=ALU.add),
                reads=["mask", "lg"], writes=["mask"])
            S.op("vector", lambda e: e.reduce_max(
                out=lcols[:, 1:2], in_=mask[:], axis=AX.X),
                reads=["mask"], writes=["m2"])
            S.op("vector", lambda e: e.tensor_scalar(
                out=lcols[:, 2:3], in0=lcols[:, 0:1], scalar1=-1.0, scalar2=None,
                op0=ALU.mult),
                reads=["m1"], writes=["nm1"])
            S.op("scalar", lambda e: e.activation(
                out=lg[:], in_=lg[:], func=AF.Exp, bias=lcols[:, 2:3]),
                reads=["lg", "nm1"], writes=["lg"])
            S.op("scalar", lambda e: e.activation(
                out=lcols[:, 3:4], in_=lcols[:, 1:2], func=AF.Exp, bias=lcols[:, 2:3]),
                reads=["m2", "nm1"], writes=["em2"])
            S.op("vector", lambda e: e.tensor_scalar(
                out=mask[:], in0=lg[:], scalar1=lcols[:, 3:4], scalar2=None,
                op0=ALU.is_ge),
                reads=["lg", "em2"], writes=["mask"])
            S.op("vector", lambda e: e.tensor_tensor(
                out=lg[:], in0=lg[:], in1=mask[:], op=ALU.mult),
                reads=["lg", "mask"], writes=["lg"])
            S.op("vector", lambda e: e.tensor_scalar(
                out=lcols[:, 3:4], in0=lcols[:, 3:4], scalar1=1.0, scalar2=None,
                op0=ALU.add),
                reads=["em2"], writes=["em2"])
            S.op("vector", lambda e: e.reciprocal(
                out=lcols[:, 3:4], in_=lcols[:, 3:4]),
                reads=["em2"], writes=["em2"])
            S.op("vector", lambda e, tt=tt: e.tensor_scalar(
                out=wfull[tt][:], in0=lg[:], scalar1=lcols[:, 3:4], scalar2=None,
                op0=ALU.mult),
                reads=["lg", "em2"], writes=[f"wfull{tt}"])
            S.op("tensor", lambda e, tt=tt: e.transpose(
                ps_trf[0:NE, 0:128], wfull[tt][:], identf[:]),
                reads=[f"wfull{tt}", "identf"], writes=["ps_trf"])
            S.op("scalar", lambda e: e.copy(out=wfT[:], in_=ps_trf[0:NE, 0:128]),
                 reads=["ps_trf"], writes=["wfT"])
            i = tt % 2
            for ch in range(2):
                S.op("tensor", lambda e, ch=ch: e.matmul(
                    ps_eo[ch][:], wfT[:], b2_sb[:, ch * 512:(ch + 1) * 512],
                    start=True, stop=True),
                    reads=["wfT", "b2"], writes=[f"ps_eo{ch}"])
                S.op("scalar", lambda e, ch=ch, i=i: e.copy(
                    out=ytbuf[i][:, ch * 512:(ch + 1) * 512], in_=ps_eo[ch][:]),
                    reads=[f"ps_eo{ch}"], writes=[kbuf[i]])
            S.op("gpsimd", lambda e, tt=tt, i=i: e.dma_start(
                out=s_ym[tt * 128:(tt + 1) * 128, :], in_=ytbuf[i][:]),
                reads=[kbuf[i]], writes=[f"sym{tt}"], queue=f"qym{tt % 4}")

        for ex in range(NE):
            sl = ex % 2
            for d in range(8):
                S.op("gpsimd", lambda e, ex=ex, d=d, sl=sl: e.dma_start(
                    out=w1_sb[sl][d][:], in_=w1_in[ex, d * 128:(d + 1) * 128, :]),
                    writes=[f"w1_{sl}_{d}"], queue=f"qw1{sl}")
            for hh in range(16):
                S.op("gpsimd", lambda e, ex=ex, hh=hh, sl=sl: e.dma_start(
                    out=w2_sb[sl][hh][:], in_=w2_in[ex, hh * 128:(hh + 1) * 128, :]),
                    writes=[f"w2_{hh}"], queue="qw2")
            S.op("sync", lambda e, ex=ex, sl=sl: e.dma_start(
                out=b1_sb[sl][:], in_=b1_in[ex]),
                writes=[f"b1{sl}"], queue=f"qb1{sl}")
            for g in range(NG):
                t0 = g * TG
                for f in range(16):
                    pi = f % 3

                    def mmh(e, f=f, t0=t0, pi=pi, sl=sl):
                        for d in range(8):
                            last = e.matmul(ps_h[pi][:],
                                            w1_sb[sl][d][:, f * 128:(f + 1) * 128],
                                            r3T[d][:, t0:t0 + TG], start=(d == 0),
                                            stop=(d == 7))
                        return last
                    S.op("tensor", mmh,
                         reads=[f"w1_{sl}_{d}" for d in range(8)] +
                               [f"r3T{d}" for d in range(8)],
                         writes=[f"ps_h{pi}"])
                    S.op("scalar", lambda e, f=f, pi=pi, sl=sl: e.activation(
                        out=h_sb[f][:], in_=ps_h[pi][:], func=AF.Relu,
                        bias=b1_sb[sl][:, f:f + 1]),
                        reads=[f"ps_h{pi}", f"b1{sl}"], writes=[f"h{f}"])
                for j in range(4):
                    tt = g * 4 + j
                    i = j % 2
                    for ch in range(2):
                        pi = ch

                        def mme(e, j=j, ch=ch, pi=pi, sl=sl):
                            for hh in range(16):
                                last = e.matmul(ps_eo[pi][:],
                                                h_sb[hh][:, j * 128:(j + 1) * 128],
                                                w2_sb[sl][hh][:, ch * 512:(ch + 1) * 512],
                                                start=(hh == 0), stop=(hh == 15))
                            return last
                        S.op("tensor", mme,
                             reads=[f"h{hh}" for hh in range(16)] +
                                   [f"w2_{hh}" for hh in range(16)],
                             writes=[f"ps_eo{pi}"])
                        S.op("vector", lambda e, tt=tt, ch=ch, pi=pi, i=i, ex=ex: e.tensor_scalar(
                            out=ytbuf[i][:, ch * 512:(ch + 1) * 512],
                            in0=ps_eo[pi][:], scalar1=wfull[tt][:, ex:ex + 1],
                            scalar2=None, op0=ALU.mult),
                            reads=[f"ps_eo{pi}", f"wfull{tt}"],
                            writes=[kbuf[i]])
                    S.op("gpsimd", lambda e, tt=tt, i=i: e.dma_start(
                        out=s_ym[tt * 128:(tt + 1) * 128, :], in_=ytbuf[i][:],
                        accum_op=ALU.add),
                        reads=[kbuf[i]], writes=[f"sym{tt}"], queue=f"qym{tt % 4}")

        for tt in range(16):
            j = tt % 2
            S.op("sync", lambda e, tt=tt, j=j: e.dma_start(
                out=r3_tok[j][:], in_=r3_in[tt * 128:(tt + 1) * 128, :]),
                writes=[f"r3_tok{j}"], queue=f"qr3{j}")
            S.op("sync", lambda e, tt=tt, j=j: e.dma_start(
                out=y_t[:], in_=s_ym[tt * 128:(tt + 1) * 128, :]),
                reads=[f"sym{tt}"], writes=["y_t"], queue="qyl")
            S.op("vector", lambda e, tt=tt, j=j: e.tensor_add(
                y_t[:], y_t[:], r3_tok[j][:]),
                reads=["y_t", f"r3_tok{j}"], writes=["y_t"])
            emit_ln(nc, S, "y_t", y_t[:], dict(sq=z_t, cols=ln_cols), f"iln{tt}",
                    sq_key="z_t")
            S.op("vector", lambda e: e.tensor_tensor(
                out=y_t[:], in0=y_t[:], in1=gam_b[:], op=ALU.mult),
                reads=["y_t", "gam_b"], writes=["y_t"])
            S.op("vector", lambda e: e.tensor_add(y_t[:], y_t[:], bet_b[:]),
                 reads=["y_t", "bet_b"], writes=["y_t"])
            S.op("vector", lambda e, j=j: e.tensor_add(z_t[:], y_t[:], r3_tok[j][:]),
                 reads=["y_t", f"r3_tok{j}"], writes=["z_t"])
            emit_ln(nc, S, "z_t", z_t[:], dict(sq=y_t, cols=ln_cols), f"oln{tt}",
                    sq_key="y_t")
            S.op("sync", lambda e, tt=tt: e.dma_start(
                out=out_t[tt * 128:(tt + 1) * 128, :], in_=z_t[:]),
                reads=["z_t"], queue="dso")

        S.barrier()
        S.flush()
        S.pe_auto_nop = True


def build_l1moe():
    nc = bass.Bass("TRN2", target_bir_lowering=False, debug=False)
    io = decl_l1moe(nc, {}, "ExternalInput")
    with ExitStack() as stack:
        S = Sched(nc, stack)
        C = emit_consts(nc, S, stack)
        emit_l1moe(nc, S, C, io)
        S.finalize()
    return nc


def prep_l1moe(ins, core, r3):
    return {
        "r3": r3,
        "wgate": ins["L1_ffn_Wg"],
        "bgate": ins["L1_ffn_bg"].reshape(1, NE),
        "w1": ins["L1_ffn_W1"],
        "b1": np.ascontiguousarray(
            ins["L1_ffn_b1"].reshape(NE, 16, 128).transpose(0, 2, 1)),
        "w2": ins["L1_ffn_W2"],
        "b2": ins["L1_ffn_b2"],
        "gam": ins["L1_ffn_gamma"].reshape(1, D),
        "bet": ins["L1_ffn_beta"].reshape(1, D),
    }


class _Prefix:

    def __init__(self, nc, p):
        self._nc = nc
        self._p = p

    def sbuf_tensor(self, name, *a, **k):
        return self._nc.sbuf_tensor(self._p + name, *a, **k)

    def psum_tensor(self, name, *a, **k):
        return self._nc.psum_tensor(self._p + name, *a, **k)

    def dram_tensor(self, name, *a, **k):
        return self._nc.dram_tensor(self._p + name, *a, **k)

    def __getattr__(self, k):
        return getattr(self._nc, k)


def build_fused():
    nc = bass.Bass("TRN2", target_bir_lowering=False, debug=False)
    io = decl_l0attn(nc, "Internal")
    decl_l0ffn(nc, io, None, "Internal")
    decl_l1attn(nc, io, None, "Internal")
    decl_l1moe(nc, io, None)
    with ExitStack() as stack:
        S = Sched(nc, stack)
        C = emit_consts(nc, S, stack)
        emit_l0attn(_Prefix(nc, "a_"), S, C, io)
        with nc.named_scope("l0ffn"):
            emit_l0ffn(_Prefix(nc, "b_"), S, C, io)
        with nc.named_scope("l1attn"):
            emit_l1attn(_Prefix(nc, "c_"), S, C, io)
        with nc.named_scope("l1moe"):
            emit_l1moe(_Prefix(nc, "d_"), S, C, io)
        S.finalize()
    return nc


def prep_fused(ins, core):
    m = prep_l0attn(ins, core)
    m.update({k: v for k, v in prep_l0ffn(ins, core, None).items() if k != "r1"})
    m.update({k: v for k, v in prep_l1attn(ins, core, None).items() if k != "r2"})
    m.update({k: v for k, v in prep_l1moe(ins, core, None).items() if k != "r3"})
    return m


def kernel(**ins):
    cores = list(range(NCORE))
    nc = build_fused()
    res = run_bass_kernel_spmd(nc, [prep_fused(ins, c) for c in cores], core_ids=cores)
    out = [np.asarray(res.results[c]["out"]) for c in cores]
    return np.stack(out, axis=0)


def kernel_unfused(**ins):
    cores = list(range(NCORE))

    nc1 = build_l0attn()
    res1 = run_bass_kernel_spmd(nc1, [prep_l0attn(ins, c) for c in cores], core_ids=cores)
    r1 = [np.asarray(res1.results[c]["r1"]) for c in cores]

    nc2 = build_l0ffn()
    res2 = run_bass_kernel_spmd(nc2, [prep_l0ffn(ins, c, r1[c]) for c in cores], core_ids=cores)
    r2 = [np.asarray(res2.results[c]["r2"]) for c in cores]

    nc3 = build_l1attn()
    res3 = run_bass_kernel_spmd(nc3, [prep_l1attn(ins, c, r2[c]) for c in cores], core_ids=cores)
    r3 = [np.asarray(res3.results[c]["r3"]) for c in cores]

    nc4 = build_l1moe()
    res4 = run_bass_kernel_spmd(nc4, [prep_l1moe(ins, c, r3[c]) for c in cores], core_ids=cores)
    out = [np.asarray(res4.results[c]["out"]) for c in cores]

    return np.stack(out, axis=0)
